# version 12
# baseline (speedup 1.0000x reference)
import contextlib
import numpy as np
import concourse.bass as bass
import concourse.mybir as mybir
from concourse.bass_utils import run_bass_kernel_spmd

F32 = mybir.dt.float32
BF16 = mybir.dt.bfloat16
AF = mybir.ActivationFunctionType
ALU = mybir.AluOpType
AX = mybir.AxisListType.X

C = 1024
T = 2048
NCT = 8
TCA = 256
NCH = T // TCA
TCF = 1024
NE = 8
HD = 2048
NHS = 16
EPS = 1e-5
NEG = -1e30

ENGINES = ("sync", "tensor", "vector", "scalar", "gpsimd")


class PB:
    def __init__(self, nc):
        self.nc = nc
        self.steps = {e: [] for e in ENGINES}
        self.sems = {}
        self.cnt = {}
        self._es = contextlib.ExitStack()

    def sem(self, name):
        self.sems[name] = self._es.enter_context(self.nc.semaphore(name))
        self.cnt[name] = 0

    def plan(self, eng, emit, waits=(), inc=None, n=1, fast=False):
        incs = ()
        if inc is not None:
            self.cnt[inc] += n
            incs = ((inc, n),)
        for s, v in waits:
            assert v <= self.cnt[s], f"future wait {s}:{v}>{self.cnt[s]}"
        ws = tuple((s, v) for (s, v) in waits if v > 0)
        self.steps[eng].append((ws, emit, incs, fast))

    def tok(self, s):
        return (s, self.cnt[s])

    def emit_all(self):
        with self.nc.Block() as block:
            for e in ENGINES:
                steps = self.steps[e]
                sems = self.sems
                drained = e in ("vector", "scalar", "gpsimd")

                def body(eng, steps=steps, drained=drained):
                    last = {}
                    for waits, emit, incs, fast in steps:
                        for s, v in waits:
                            if last.get(s, 0) < v:
                                eng.wait_ge(sems[s], v)
                                last[s] = v
                        instr = emit(eng)
                        dma_incs = [(s, nn) for s, nn in incs if nn >= 16]
                        eng_incs = [(s, nn) for s, nn in incs if nn < 16]
                        for s, nn in dma_incs:
                            instr.then_inc(sems[s], nn)
                        if drained and not dma_incs and not fast:
                            instr = eng.drain()
                        for s, nn in eng_incs:
                            instr.then_inc(sems[s], nn)
                getattr(block, e)(body)
        self._es.close()


class SbufMap:
    def __init__(self, nc):
        self.nc = nc
        base = nc.SBUF_PARTITION_SIZE_BYTES - nc.sbuf_bytes_remaining
        self.base = (base + 127) // 128 * 128
        self.cap = nc.SBUF_PARTITION_SIZE_BYTES - self.base - 512
        self.cursor = 0
        self.marks = {}
        self.offs = {}

    def alloc(self, name, shape, dtype, align=64):
        itm = 2 if dtype == BF16 else 4
        self.cursor = (self.cursor + align - 1) // align * align
        off = self.cursor
        self.cursor += shape[-1] * itm
        assert self.cursor <= self.cap, (name, self.cursor, self.cap)
        self.offs[name] = off
        return self.nc.alloc_sbuf_tensor_at(name, shape, dtype,
                                            offset=self.base + off)

    def at(self, name, shape, dtype, off):
        itm = 2 if dtype == BF16 else 4
        assert off + shape[-1] * itm <= self.cap, name
        self.offs[name] = off
        return self.nc.alloc_sbuf_tensor_at(name, shape, dtype,
                                            offset=self.base + off)

    def mark(self, name):
        self.marks[name] = self.cursor

    def reset_to(self, name):
        self.cursor = self.marks[name]


def build_program(debug=False):
    nc = bass.Bass("TRN2", target_bir_lowering=False)

    dx = nc.dram_tensor("x", [C, T], F32, kind="ExternalInput").ap()
    dW = {}
    for nm in ("WkT", "WvT", "WrT", "WoT", "rw1"):
        dW[nm] = nc.dram_tensor(nm, [C, C], F32, kind="ExternalInput").ap()
    drw2 = nc.dram_tensor("rw2", [C, NE], F32, kind="ExternalInput").ap()
    dvec = nc.dram_tensor("vec6", [48, 128], F32, kind="ExternalInput").ap()
    deb1 = nc.dram_tensor("eb1", [NE, HD], F32, kind="ExternalInput").ap()
    deb2 = nc.dram_tensor("eb2", [NE, C], F32, kind="ExternalInput").ap()
    drb2 = nc.dram_tensor("rb2", [NE, 1], F32, kind="ExternalInput").ap()
    dew1 = nc.dram_tensor("ew1", [NE * C, HD], F32, kind="ExternalInput").ap()
    dew2 = nc.dram_tensor("ew2", [NE * HD, C], F32, kind="ExternalInput").ap()
    deye = nc.dram_tensor("eye", [128, 128], F32, kind="ExternalInput").ap()

    dout = nc.dram_tensor("out", [C, T], F32, kind="ExternalOutput").ap()
    dx1 = nc.dram_tensor("x1d", [C, T], F32, kind="Internal").ap()
    dyd = nc.dram_tensor("yd", [C, TCF], F32, kind="Internal").ap()
    dwfm = nc.dram_tensor("wfm_d", [NE, T], BF16, kind="Internal")
    dew1b = nc.dram_tensor("ew1b_d", [NE * C, HD], BF16, kind="Internal")
    dew2b = nc.dram_tensor("ew2b_d", [NE * HD, C], BF16, kind="Internal")
    drB = nc.dram_tensor("rows_d", [2, T], F32, kind="Internal")
    dbg = {}
    if debug:
        for nm, shp in (("x1", [C, T]), ("logits", [NE, T]), ("w", [NE, T]),
                        ("k", [C, T]), ("wkv", [C, T]), ("carry", [128, 256]),
                        ("stats", [2, T])):
            dbg[nm] = nc.dram_tensor("dbg_" + nm, shp, F32,
                                     kind="ExternalOutput").ap()
        dbg["t"] = nc.dram_tensor("dbg_t", [C, T], BF16, kind="ExternalOutput").ap()
        dbg["y"] = nc.dram_tensor("dbg_y", [C, T], F32, kind="ExternalOutput").ap()
        dbg["ehw"] = nc.dram_tensor("dbg_ehw", [HD, TCF], BF16, kind="ExternalOutput").ap()
        dbg["rh"] = nc.dram_tensor("dbg_rh", [128, 4096], BF16, kind="ExternalOutput").ap()
        dbg["dcb"] = nc.dram_tensor("dbg_dcb", [128, 2080], F32, kind="ExternalOutput").ap()

    sm = SbufMap(nc)
    A = sm.alloc
    eyeS = A("eyeS", [128, 128], F32)
    onesK = A("onesK", [1, 128], F32)
    ones_s = A("ones_s", [128, 1], F32)
    vecT = A("vecT", [128, 48], F32)
    dcoefT = A("dcoefT", [128, 8], F32)
    euT = A("euT", [128, 8], F32)
    ebT = A("ebT", [128, 128], F32)
    rb2c = A("rb2c", [NE, 1], F32)
    Pc8 = A("Pc8", [128, 8], F32)
    Qc8 = A("Qc8", [128, 8], F32)
    M8 = A("M8", [128, 8], F32)
    kmax8 = A("kmax8", [128, 8], F32)
    dMs = A("dMs", [128, 8], F32)
    sc8 = A("sc8", [128, 8], F32)
    negM8 = A("negM8", [128, 8], F32)
    rowsum = A("rowsum", [1, T], F32)
    rowsq = A("rowsq", [1, T], F32)
    rstdR = A("rstdR", [1, T], F32)
    mrsR = A("mrsR", [1, T], F32)
    stgA0 = A("stgA0", [128, 2048], F32)
    stgA1 = A("stgA1", [128, 2048], F32)
    junk = A("junk", [128, 16], F32)
    sm.mark("overlay")

    Wbk = A("Wbk", [128, 8192], BF16)
    Wbv = A("Wbv", [128, 8192], BF16)
    Wbr = A("Wbr", [128, 8192], BF16)
    Wbo = A("Wbo", [128, 8192], BF16)
    xsb = [A("xs0", [128, 8 * 257], F32), A("xs1", [128, 8 * 257], F32)]
    xn = A("xn", [128, 8 * 257], F32)
    xkb = A("xkb", [128, 2048], BF16)
    xvb = A("xvb", [128, 2048], BF16)
    xrb = A("xrb", [128, 2048], BF16)
    kf = A("kf", [128, 2048], F32)
    vf = A("vf", [128, 2048], F32)
    erf = A("erf", [128, 2048], F32)
    dd = A("dd", [128, 256], F32)
    Escr = A("Escr", [128, 256], F32)
    Gscr = A("Gscr", [128, 256], F32)
    Nscr = A("Nscr", [128, 256], F32)
    Dscr = A("Dscr", [128, 256], F32)
    Pbuf = A("Pbuf", [128, 256], F32)
    Qbuf = A("Qbuf", [128, 256], F32)
    sqf = A("sqf", [128, 512], F32)
    mean2R = A("mean2R", [1, TCA], F32)
    scr2 = A("scr2", [1, TCA], F32)
    rstd2R = A("rstd2R", [1, TCA], F32)
    mrs2R = A("mrs2R", [1, TCA], F32)
    aofR = A("aofR", [128, 2048], F32)
    xbc0 = A("xbc0", [128, 264], F32)
    xbc1 = A("xbc1", [128, 264], F32)
    rwkvbR = A("rwkvbR", [128, 2048], BF16)
    attn_end = sm.cursor
    sm.reset_to("overlay")

    yacc = A("yacc", [128, 8 * TCF], F32)
    t_bf = A("t_bf", [128, 8 * T], BF16)
    e1b = A("e1b", [128, 8 * 2048], BF16)
    ehw = A("ehw", [128, NHS * 1024], BF16)
    e2b = [A("e2b0", [128, 1024], BF16), A("e2b1", [128, 1024], BF16)]
    rw2b = A("rw2b", [128, 64], BF16)
    wbuf = A("wbuf", [128, TCF], BF16)
    scrX = A("scrX", [128, 512], F32)
    eb2sb = A("eb2sb", [NE, C], F32)
    w_fm = A("w_fm", [NE, T], F32)
    lgT = A("lgT", [128, 8], F32)
    wTj = A("wTj", [128, 8], F32)
    rw2stg = A("rw2stg", [128, 64], F32)
    bcF = [A("bcF0", [128, 512], F32), A("bcF1", [128, 512], F32),
           A("bcF2", [128, 512], F32), A("bcF3", [128, 512], F32)]
    ffn_end = sm.cursor
    assert max(attn_end, ffn_end) <= sm.cap, (attn_end, ffn_end, sm.cap)

    rhb = sm.at("rhb", [128, 4096], BF16, sm.offs["ehw"])
    wfmb = sm.at("wfmb", [NE, T], BF16, sm.offs["ehw"])
    stgCa = sm.at("stgCa", [128, 2048], BF16, sm.offs["stgA0"])
    logits_sb = sm.at("logits_sb", [NE, T], F32, sm.offs["ehw"] + 8192)
    rw1b = sm.at("rw1b", [128, 8192], BF16, sm.offs["ehw"] + 16384)
    aof = sm.at("aof", [128, 2048], F32, sm.offs["aofR"])
    rwkvb = sm.at("rwkvb", [128, 2048], BF16, sm.offs["rwkvbR"])
    vstg = sm.at("vstg", [8, 768], F32, sm.offs["vf"])
    outbuf = sm.at("outbuf", [128, 16384], F32, sm.offs["e1b"])
    ebstg = sm.at("ebstg", [8, 2048], F32, sm.offs["kf"])

    PS = nc.alloc_psum_tensor("PS", [128, 4096], F32)

    pb = PB(nc)
    for s in ("dS", "dW", "dO", "dG", "dF", "dB", "dR", "dV0", "dV1", "dX",
              "tE", "vE", "sE", "pE", "x0", "x1s", "sl0", "sl1", "sl2", "sl3",
              "c0s", "c1s", "eA", "e20", "e21"):
        pb.sem(s)
    P = pb.plan
    tok = pb.tok

    def TT(e, out, a, b, op):
        return e.tensor_tensor(out=out, in0=a, in1=b, op=op)

    def TS(e, out, a, s1, op):
        return e.tensor_scalar(out=out, in0=a, scalar1=s1, scalar2=None, op0=op)

    def vsp(n=2, inc=None):
        for q in range(n):
            P("vector", lambda e: e.tensor_scalar(
                out=junk[:], in0=junk[:], scalar1=0.0, scalar2=None, op0=ALU.mult),
              inc=inc if q == n - 1 else None)

    P("sync", lambda e: e.dma_start(out=eyeS[:], in_=deye), inc="dW", n=16)
    for g in range(6):
        P("sync", lambda e, g=g: e.dma_start(
            out=vstg[:, g * 128:(g + 1) * 128], in_=dvec[g * 8:(g + 1) * 8, :]),
          inc="dW", n=16)
    P("sync", lambda e: e.dma_start(out=ebstg[:], in_=deb1), inc="dW", n=16)
    P("sync", lambda e: e.dma_start(out=rb2c[:], in_=drb2), inc="dW", n=16)
    w_const = tok("dW")

    P("vector", lambda e: e.memset(junk[:], 0.0))
    P("vector", lambda e: e.memset(onesK[:], 1.0))
    P("vector", lambda e: e.memset(ones_s[:], 1.0))
    P("vector", lambda e: e.memset(M8[:], NEG))
    P("vector", lambda e: e.memset(Pc8[:], 0.0))
    P("vector", lambda e: e.memset(Qc8[:], 0.0), inc="vE")
    init_tok = tok("vE")

    for j in range(16):
        P("tensor", lambda e, j=j: e.transpose(
            PS[:, 8 * j:8 * j + 8], ebstg[0:8, j * 128:(j + 1) * 128],
            eyeS[0:8, 0:8]), waits=[w_const] if j == 0 else ())
    for g in range(6):
        P("tensor", lambda e, g=g: e.transpose(
            PS[:, 128 + 8 * g:136 + 8 * g], vstg[0:8, g * 128:(g + 1) * 128],
            eyeS[0:8, 0:8]), inc="tE" if g == 5 else None)
    tr_tok = tok("tE")
    P("scalar", lambda e: e.copy(ebT[:], PS[:, 0:128]), waits=[tr_tok])
    P("scalar", lambda e: e.copy(vecT[:], PS[:, 128:176]), inc="sE")
    vecT_tok = tok("sE")
    P("scalar", lambda e: e.activation(out=dcoefT[:], in_=vecT[:, 32:40], func=AF.Exp))
    P("scalar", lambda e: e.activation(out=dcoefT[:], in_=dcoefT[:], func=AF.Exp, scale=-1.0))
    P("scalar", lambda e: e.activation(out=euT[:], in_=vecT[:, 40:48], func=AF.Exp))
    P("scalar", lambda e: e.copy(dMs[:], dcoefT[:]))
    P("scalar", lambda e: e.copy(sc8[:], euT[:]), inc="sE")
    dco_tok = tok("sE")

    half = [(stgA0, 0), (stgA0, 1024), (stgA1, 0), (stgA1, 1024)]
    hfree = [init_tok] * 4
    hfill = [("dW", 0)] * 4
    hrr = [0]
    prr = [0]

    def load1024(src, free_extra=(), sem=None):
        i = hrr[0] % 4
        hrr[0] += 1
        buf, off = half[i]
        ssem = f"sl{i}"
        h = hfree[i] if isinstance(hfree[i], list) else [hfree[i]]
        P("sync", lambda e, buf=buf, off=off, src=src: e.dma_start(
            out=buf[:, off:off + 1024], in_=src),
          waits=h + list(free_extra), inc=ssem, n=16)
        hfill[i] = tok(ssem)
        return i, buf, off

    def load2048(src, free_extra=(), sem=None):
        i = 0 if (prr[0] % 2 == 0) else 2
        prr[0] += 1
        buf = half[i][0]
        ssem = f"sl{i}"
        h0 = hfree[i] if isinstance(hfree[i], list) else [hfree[i]]
        h1 = hfree[i + 1] if isinstance(hfree[i + 1], list) else [hfree[i + 1]]
        P("sync", lambda e, buf=buf, src=src: e.dma_start(
            out=buf[:, 0:2048], in_=src),
          waits=h0 + h1 + list(free_extra), inc=ssem, n=16)
        hfill[i] = hfill[i + 1] = tok(ssem)
        return i, buf

    def free_half(i, t):
        hfree[i] = t

    def free_pair(i, t):
        hfree[i] = t
        hfree[i + 1] = t

    wbt = {}
    for nm, dst in (("WkT", Wbk), ("WvT", Wbv), ("WrT", Wbr), ("WoT", Wbo)):
        for ct in range(NCT):
            i, buf, off = load1024(dW[nm][ct * 128:(ct + 1) * 128, :])
            P("scalar", lambda e, buf=buf, off=off, dst=dst, ct=ct: e.copy(
                dst[:, ct * 1024:(ct + 1) * 1024], buf[:, off:off + 1024]),
              waits=[hfill[i]], inc="sE", fast=(ct != 7))
            free_half(i, tok("sE"))
        wbt[nm] = tok("sE")

    stgC_free = [wbt["WoT"], wbt["WoT"]]
    stgC_sem = ["c0s", "c1s"]
    pc_in_free = [wbt["WoT"], wbt["WoT"]]
    ci = [0]

    def precast(srcap, dstap, width):
        assert width == 1024
        q = ci[0] % 2
        ci[0] += 1
        isem = "sl2" if q == 0 else "sl3"
        P("sync", lambda e, srcap=srcap, q=q: e.dma_start(
            out=stgA1[:, q * 1024:(q + 1) * 1024], in_=srcap),
          waits=[pc_in_free[q]], inc=isem, n=16)
        ld = tok(isem)
        P("scalar", lambda e, q=q: e.copy(
            stgCa[:, q * 1024:(q + 1) * 1024], stgA1[:, q * 1024:(q + 1) * 1024]),
          waits=[ld, stgC_free[q]], inc="sE", fast=True)
        cst = tok("sE")
        pc_in_free[q] = cst
        P("sync", lambda e, dstap=dstap, q=q: e.dma_start(
            out=dstap, in_=stgCa[:, q * 1024:(q + 1) * 1024]),
          waits=[cst], inc=stgC_sem[q], n=16)
        stgC_free[q] = tok(stgC_sem[q])

    precast_jobs = []
    for eidx in range(NE):
        for ct in range(NCT):
            r0 = eidx * C + ct * 128
            for hh in range(2):
                precast_jobs.append(
                    (dew1[r0:r0 + 128, hh * 1024:(hh + 1) * 1024],
                     dew1b.ap()[r0:r0 + 128, hh * 1024:(hh + 1) * 1024], 1024))
    for eidx in range(NE):
        for hs in range(NHS):
            r0 = eidx * HD + hs * 128
            precast_jobs.append(
                (dew2[r0:r0 + 128, :], dew2b.ap()[r0:r0 + 128, :], 1024))
    pc_pos = [0]

    def pump_precast(n):
        while n > 0 and pc_pos[0] < len(precast_jobs):
            precast(*precast_jobs[pc_pos[0]])
            pc_pos[0] += 1
            n -= 1

    ph0_mm_tok = {}
    for ct in range(NCT):
        xs = xsb[ct % 2]
        xw = [ph0_mm_tok[ct - 2]] if ct - 2 in ph0_mm_tok else [init_tok]
        P("sync", lambda e, xs=xs, ct=ct: e.dma_start(
            out=xs[:, 0:2048], in_=dx[ct * 128:(ct + 1) * 128, :]),
          waits=xw, inc="x0" if ct % 2 == 0 else "x1s", n=16)
        ld = tok("x0" if ct % 2 == 0 else "x1s")
        P("gpsimd", lambda e, xs=xs: e.tensor_mul(
            xn[:, 0:2048], xs[:, 0:2048], xs[:, 0:2048]),
          waits=[ld] + ([ph0_mm_tok[ct - 1]] if ct - 1 in ph0_mm_tok else []),
          inc="pE")
        sq = tok("pE")
        for s in range(4):
            P("tensor", lambda e, xs=xs, s=s, ct=ct: e.matmul(
                PS[0:1, s * 512:(s + 1) * 512], ones_s[:],
                xs[:, s * 512:(s + 1) * 512], start=(ct == 0), stop=(ct == 7)),
              waits=[ld, vecT_tok] if s == 0 else ())
        for s in range(4):
            P("tensor", lambda e, s=s, ct=ct: e.matmul(
                PS[0:1, 2048 + s * 512:2048 + (s + 1) * 512], ones_s[:],
                xn[:, s * 512:(s + 1) * 512], start=(ct == 0), stop=(ct == 7)),
              waits=[sq] if s == 0 else (), inc="tE" if s == 3 else None)
        ph0_mm_tok[ct] = tok("tE")
    ph0_mm = tok("tE")

    def ln_rows(sum_ap, sq_ap, rstd_ap, mrs_ap, mean_ap, scr_ap, waits_v):
        vsp(2)
        P("vector", lambda e: TS(e, mean_ap, sum_ap, 1.0 / C, ALU.mult), waits=waits_v)
        P("vector", lambda e: TS(e, scr_ap, sq_ap, 1.0 / C, ALU.mult))
        vsp(1)
        P("vector", lambda e: TT(e, rstd_ap, mean_ap, mean_ap, ALU.mult))
        P("vector", lambda e: TS(e, scr_ap, scr_ap, EPS, ALU.add))
        vsp(1)
        P("vector", lambda e: TT(e, scr_ap, scr_ap, rstd_ap, ALU.subtract))
        vsp(2, inc="vE")
        tv = tok("vE")
        P("scalar", lambda e: e.activation(out=rstd_ap, in_=scr_ap, func=AF.Ln),
          waits=[tv])
        P("scalar", lambda e: e.activation(out=rstd_ap, in_=rstd_ap, func=AF.Exp, scale=-0.5))
        P("scalar", lambda e: e.copy(junk[:], junk[:]))
        P("scalar", lambda e: e.copy(junk[:], junk[:]), inc="sE")
        ts_ = tok("sE")
        P("vector", lambda e: TT(e, mrs_ap, mean_ap, rstd_ap, ALU.mult), waits=[ts_])
        vsp(2, inc="vE")
        return tok("vE"), tv

    for s in range(4):
        P("vector", lambda e, s=s: e.tensor_copy(
            rowsum[0:1, s * 512:(s + 1) * 512], PS[0:1, s * 512:(s + 1) * 512]),
          waits=[ph0_mm] if s == 0 else ())
        P("vector", lambda e, s=s: e.tensor_copy(
            rowsq[0:1, s * 512:(s + 1) * 512],
            PS[0:1, 2048 + s * 512:2048 + (s + 1) * 512]),
          inc="vE" if s == 3 else None)
    ph0_ev = tok("vE")
    ln1_tok, _ = ln_rows(rowsum[0:1, 0:T], rowsq[0:1, 0:T], rstdR[0:1, 0:T],
                         mrsR[0:1, 0:T], rowsum[0:1, 0:T], rowsq[0:1, 0:T],
                         [ph0_ev])
    P("sync", lambda e: e.dma_start(out=drB.ap()[0:1, :], in_=rstdR[0:1, 0:T]),
      waits=[ln1_tok], inc="dR", n=16)
    P("sync", lambda e: e.dma_start(out=drB.ap()[1:2, :], in_=mrsR[0:1, 0:T]),
      inc="dR", n=16)
    rows1_tok = tok("dR")

    pp = [0, 512, 1024, 1536]
    pp_free = [ph0_ev] * 4
    pp_rr = [0]
    bc = [3072, 3584]
    bc_free = [ph0_ev, ph0_ev]
    stA_free = ph0_ev
    stB_free = ph0_ev
    sq_free = [ph0_ev, ph0_ev]
    tok_x1ops = {}
    tok_x1dma = {}
    tok_statB = {}
    tok_xn = {}
    st = {}

    def a_loads(tau):
        s = st[tau] = {}
        s["xs"] = xsb[tau % 2]
        s["c0"] = tau * TCA
        xs, c0 = s["xs"], s["c0"]
        xw = [tok_x1ops[tau - 2]] if tau - 2 in tok_x1ops else [ph0_ev]
        xsem = "x0" if tau % 2 == 0 else "x1s"
        for ct in range(NCT):
            if tau == 0:
                P("sync", lambda e, ct=ct, xs=xs: e.dma_start(
                    out=xs[:, ct * 257 + 1:ct * 257 + 257],
                    in_=dx[ct * 128:(ct + 1) * 128, 0:256]),
                  waits=xw, inc=xsem, n=16)
            else:
                P("sync", lambda e, ct=ct, xs=xs, c0=c0: e.dma_start(
                    out=xs[:, ct * 257:ct * 257 + 257],
                    in_=dx[ct * 128:(ct + 1) * 128, c0 - 1:c0 + 256]),
                  waits=xw, inc=xsem, n=16)
        s["xs_tok"] = tok(xsem)
        lo = max(c0 - 1, 0)
        o0h = 1 if tau == 0 else 0
        xbw = [rows1_tok]
        if tau - 1 in tok_xn:
            xbw.append(tok_xn[tau - 1])
        P("sync", lambda e, lo=lo, o0h=o0h, c0=c0: e.dma_start(
            out=xbc0[:, o0h:257],
            in_=drB.ap()[0:1, lo:c0 + 256].broadcast_to((128, 257 - o0h))),
          waits=xbw, inc="dX", n=16)
        P("sync", lambda e, lo=lo, o0h=o0h, c0=c0: e.dma_start(
            out=xbc1[:, o0h:257],
            in_=drB.ap()[1:2, lo:c0 + 256].broadcast_to((128, 257 - o0h))),
          inc="dX", n=16)
        s["bcast_tok"] = tok("dX")
        pump_precast(32)

    def a_xnmix(tau):
        s = st[tau]
        xs = s["xs"]
        o0 = 1 if tau == 0 else 0
        for ct in range(NCT):
            P("vector", lambda e, ct=ct, xs=xs, o0=o0: TT(
                e, xn[:, ct * 257 + o0:(ct + 1) * 257],
                xs[:, ct * 257 + o0:(ct + 1) * 257],
                xbc0[:, o0:257], ALU.mult),
              waits=[s["xs_tok"], s["bcast_tok"]] if ct == 0 else ())
            last = (ct == 7) and (tau != 0)
            P("vector", lambda e, ct=ct, o0=o0: TT(
                e, xn[:, ct * 257 + o0:(ct + 1) * 257],
                xn[:, ct * 257 + o0:(ct + 1) * 257],
                xbc1[:, o0:257], ALU.subtract),
              inc="vE" if last else None)
        if tau == 0:
            for ct in range(NCT):
                P("vector", lambda e, ct=ct: e.memset(xn[:, ct * 257:ct * 257 + 1], 0.0),
                  inc="vE" if ct == 7 else None)
        tok_xn[tau] = tok("vE")
        mix_w = [st[tau - 1]["projr_tok"]] if tau - 1 in st else []
        for ct in range(NCT):
            cur = xn[:, ct * 257 + 1:ct * 257 + 257]
            prv = xn[:, ct * 257:ct * 257 + 256]
            P("vector", lambda e, cur=cur, prv=prv: TT(e, dd[:], cur, prv, ALU.subtract),
              waits=mix_w if ct == 0 else ())
            for dst, col in ((xkb, 0), (xvb, 8), (xrb, 16)):
                lastop = (ct == 7 and col == 16)
                P("vector", lambda e, dst=dst, col=col, ct=ct, prv=prv:
                  e.scalar_tensor_tensor(
                      out=dst[:, ct * 256:(ct + 1) * 256], in0=dd[:],
                      scalar=vecT[:, col + ct:col + ct + 1], in1=prv,
                      op0=ALU.mult, op1=ALU.add),
                  inc="vE" if lastop else None)
        s["mix_tok"] = tok("vE")

    def a_proj(tau):
        s = st[tau]
        rw_prev = st[tau - 1]["rw_toks"] if tau - 1 in st else {}
        ev_toks = {}
        for pi, nm in enumerate(("k", "v", "r")):
            Wb = (Wbk, Wbv, Wbr)[pi]
            srcb = (xkb, xvb, xrb)[pi]
            wkey = ("WkT", "WvT", "WrT")[pi]
            dstb = (kf, vf, None)[pi]
            for dt in range(NCT):
                sli = pp_rr[0] % 4
                sl = pp[sli]
                fw = pp_free[sli]
                pp_rr[0] += 1
                for ctk in range(NCT):
                    P("tensor", lambda e, sl=sl, Wb=Wb, ctk=ctk, dt=dt, srcb=srcb: e.matmul(
                        PS[:, sl:sl + 256],
                        Wb[:, ctk * 1024 + dt * 128:ctk * 1024 + dt * 128 + 128],
                        srcb[:, ctk * 256:(ctk + 1) * 256],
                        start=(ctk == 0), stop=(ctk == 7)),
                      waits=[s["mix_tok"], fw, wbt[wkey]] if ctk == 0 else (),
                      inc="tE" if ctk == 7 else None)
                tk = tok("tE")
                guard = [rw_prev[dt]] if dt in rw_prev else []
                if nm == "r":
                    P("scalar", lambda e, sl=sl, dt=dt: e.activation(
                        out=erf[:, dt * 256:(dt + 1) * 256], in_=PS[:, sl:sl + 256],
                        func=AF.Exp, scale=-1.0), waits=[tk] + guard, inc="sE")
                else:
                    P("scalar", lambda e, sl=sl, dstb=dstb, dt=dt: e.copy(
                        dstb[:, dt * 256:(dt + 1) * 256], PS[:, sl:sl + 256]),
                      waits=[tk] + guard, inc="sE")
                ev_toks[(nm, dt)] = tok("sE")
                pp_free[sli] = tok("sE")
        s["ev_toks"] = ev_toks
        s["projr_tok"] = tok("tE")
        if debug:
            for dt in range(NCT):
                P("sync", lambda e, dt=dt, c0=s["c0"]: e.dma_start(
                    out=dbg["k"][dt * 128:(dt + 1) * 128, c0:c0 + 256],
                    in_=kf[:, dt * 256:(dt + 1) * 256]),
                  waits=[ev_toks[("k", dt)]], inc="dS", n=16)
            s["kdump_tok"] = tok("dS")

    def a_scanprep(tau):
        s = st[tau]
        ev_toks = s["ev_toks"]
        for dt in range(NCT):
            P("vector", lambda e, dt=dt: e.reduce_max(
                out=kmax8[:, dt:dt + 1], in_=kf[:, dt * 256:(dt + 1) * 256], axis=AX),
              waits=[ev_toks[("k", dt)], dco_tok])
        vsp(2)
        fprev = [st[tau - 1]["F7"]] if tau - 1 in st and "F7" in st[tau - 1] else []
        P("vector", lambda e: TT(e, kmax8[:], kmax8[:], M8[:], ALU.max), waits=fprev)
        vsp(2)
        P("vector", lambda e: TT(e, dMs[:], M8[:], kmax8[:], ALU.subtract))
        P("vector", lambda e: e.tensor_copy(M8[:], kmax8[:]))
        P("vector", lambda e: TS(e, negM8[:], kmax8[:], -1.0, ALU.mult))
        vsp(2, inc="vE")
        mb_tok = tok("vE")
        P("scalar", lambda e: e.activation(out=sc8[:], in_=dMs[:], func=AF.Exp),
          waits=[mb_tok], inc="sE")
        sc_tok = tok("sE")
        P("vector", lambda e: TT(e, Pc8[:], Pc8[:], sc8[:], ALU.mult), waits=[sc_tok])
        P("vector", lambda e: TT(e, Qc8[:], Qc8[:], sc8[:], ALU.mult))
        vsp(2, inc="vE")
        s["resc_tok"] = tok("vE")
        if debug:
            P("sync", lambda e, tau=tau: e.dma_start(
                out=dbg["carry"][:, tau * 32:tau * 32 + 8], in_=Pc8[:]),
              waits=[s["resc_tok"]], inc="dS", n=16)
            P("sync", lambda e, tau=tau: e.dma_start(
                out=dbg["carry"][:, tau * 32 + 8:tau * 32 + 16], in_=Qc8[:]),
              inc="dS", n=16)
            carry_dump = tok("dS")
            P("vector", lambda e: e.nop(), waits=[carry_dump])

    def a_wkv(tau):
        s = st[tau]
        ev_toks = s["ev_toks"]
        F_toks = {}
        for dt in range(NCT):
            fwaits = [s["resc_tok"]] if dt == 0 else []
            if debug and dt == 0 and "kdump_tok" in s:
                fwaits.append(s["kdump_tok"])
            P("scalar", lambda e, dt=dt: e.activation(
                out=kf[:, dt * 256:(dt + 1) * 256], in_=kf[:, dt * 256:(dt + 1) * 256],
                func=AF.Exp, bias=negM8[:, dt:dt + 1]), waits=fwaits, inc="sE")
            F_toks[dt] = tok("sE")
        s["F7"] = F_toks[7]
        wo_prev = [st[tau - 1]["wo_last"]] if tau - 1 in st and "wo_last" in st[tau - 1] else []
        rw_toks = {}
        for dt in range(NCT):
            kfs = kf[:, dt * 256:(dt + 1) * 256]
            vfs = vf[:, dt * 256:(dt + 1) * 256]
            dcbv = dcoefT[:, dt:dt + 1].broadcast_to((128, 256))
            w0 = [F_toks[dt]] + (wo_prev if dt == 0 else [])
            P("vector", lambda e, kfs=kfs, vfs=vfs: TT(e, Escr[:], kfs, vfs, ALU.mult),
              waits=w0)
            P("vector", lambda e, dt=dt, dcbv=dcbv: e.tensor_tensor_scan(
                out=Pbuf[:], data0=dcbv, data1=Escr[:],
                initial=Pc8[:, dt:dt + 1], op0=ALU.mult, op1=ALU.add))
            P("vector", lambda e, dt=dt, kfs=kfs, dcbv=dcbv: e.tensor_tensor_scan(
                out=Qbuf[:], data0=dcbv, data1=kfs,
                initial=Qc8[:, dt:dt + 1], op0=ALU.mult, op1=ALU.add))
            P("vector", lambda e, dt=dt, kfs=kfs, vfs=vfs: e.scalar_tensor_tensor(
                out=Gscr[:], in0=kfs, scalar=euT[:, dt:dt + 1], in1=vfs,
                op0=ALU.mult, op1=ALU.mult))
            P("vector", lambda e: TT(e, Nscr[:, 1:256], Gscr[:, 1:256],
                                     Pbuf[:, 0:255], ALU.add))
            P("vector", lambda e, dt=dt: TT(e, Nscr[:, 0:1], Gscr[:, 0:1],
                                            Pc8[:, dt:dt + 1], ALU.add))
            P("vector", lambda e, dt=dt, kfs=kfs: e.scalar_tensor_tensor(
                out=Dscr[:, 1:256], in0=kfs[:, 1:256], scalar=euT[:, dt:dt + 1],
                in1=Qbuf[:, 0:255], op0=ALU.mult, op1=ALU.add))
            P("vector", lambda e, dt=dt, kfs=kfs: e.scalar_tensor_tensor(
                out=Dscr[:, 0:1], in0=kfs[:, 0:1], scalar=euT[:, dt:dt + 1],
                in1=Qc8[:, dt:dt + 1], op0=ALU.mult, op1=ALU.add))
            P("vector", lambda e, dt=dt: e.tensor_copy(Pc8[:, dt:dt + 1], Pbuf[:, 255:256]),
              fast=True)
            P("vector", lambda e, dt=dt: e.tensor_copy(Qc8[:, dt:dt + 1], Qbuf[:, 255:256]),
              fast=True)
            P("vector", lambda e, dt=dt: e.scalar_tensor_tensor(
                out=Gscr[:], in0=erf[:, dt * 256:(dt + 1) * 256], scalar=1.0,
                in1=Dscr[:], op0=ALU.add, op1=ALU.mult),
              waits=[ev_toks[("r", dt)]])
            P("vector", lambda e: e.reciprocal(out=Escr[:], in_=Gscr[:]))
            vsp(1)
            P("vector", lambda e, dt=dt: TT(
                e, rwkvb[:, dt * 256:(dt + 1) * 256], Nscr[:], Escr[:], ALU.mult),
              inc="vE")
            rw_toks[dt] = tok("vE")
            if debug:
                P("vector", lambda e: TT(e, Nscr[:], Nscr[:], Escr[:], ALU.mult), inc="vE")
                dtk = tok("vE")
                P("sync", lambda e, dt=dt, c0=s["c0"]: e.dma_start(
                    out=dbg["wkv"][dt * 128:(dt + 1) * 128, c0:c0 + 256], in_=Nscr[:]),
                  waits=[dtk], inc="dS", n=16)
        s["rw_toks"] = rw_toks

    def a_tail(tau):
        nonlocal stA_free, stB_free
        s = st[tau]
        xs, c0 = s["xs"], s["c0"]
        rw_toks = s["rw_toks"]
        ao_toks = {}
        sq_toks = {}
        aof_guard = []
        if tau - 1 in tok_x1dma:
            aof_guard = [("dO", tok_x1dma[tau - 1]), tok_x1ops[tau - 1]]
        for dt in range(NCT):
            sli = pp_rr[0] % 4
            sl = pp[sli]
            fw = pp_free[sli]
            pp_rr[0] += 1
            for ctk in range(NCT):
                P("tensor", lambda e, sl=sl, ctk=ctk, dt=dt: e.matmul(
                    PS[:, sl:sl + 256],
                    Wbo[:, ctk * 1024 + dt * 128:ctk * 1024 + dt * 128 + 128],
                    rwkvb[:, ctk * 256:(ctk + 1) * 256],
                    start=(ctk == 0), stop=(ctk == 7)),
                  waits=[rw_toks[7], fw, wbt["WoT"]] if ctk == 0 else (),
                  inc="tE" if ctk == 7 else None)
            tk = tok("tE")
            P("scalar", lambda e, sl=sl, dt=dt: e.copy(
                aof[:, dt * 256:(dt + 1) * 256], PS[:, sl:sl + 256]),
              waits=[tk] + (aof_guard if dt == 0 else []), inc="sE")
            ao_toks[dt] = tok("sE")
            pp_free[sli] = tok("sE")
            if dt == NCT - 1:
                s["wo_last"] = tk
            P("scalar", lambda e, dt=dt: e.activation(
                out=sqf[:, (dt % 2) * 256:(dt % 2) * 256 + 256],
                in_=aof[:, dt * 256:(dt + 1) * 256], func=AF.Square),
              waits=[sq_free[dt % 2]], inc="sE")
            sq_toks[dt] = tok("sE")
            P("tensor", lambda e, dt=dt: e.matmul(
                PS[0:1, 2048:2304], ones_s[:], aof[:, dt * 256:(dt + 1) * 256],
                start=(dt == 0), stop=(dt == 7)),
              waits=[ao_toks[dt], stA_free] if dt == 0 else [ao_toks[dt]])
            P("tensor", lambda e, dt=dt: e.matmul(
                PS[0:1, 2560:2816], ones_s[:],
                sqf[:, (dt % 2) * 256:(dt % 2) * 256 + 256],
                start=(dt == 0), stop=(dt == 7)),
              waits=[sq_toks[dt]], inc="tE")
            sq_free[dt % 2] = tok("tE")
        statA_tok = tok("tE")

        mrs2_tok, var2_tok = ln_rows(
            PS[0:1, 2048:2304], PS[0:1, 2560:2816], rstd2R[0:1, 0:TCA],
            mrs2R[0:1, 0:TCA], mean2R[0:1, 0:TCA], scr2[0:1, 0:TCA],
            [statA_tok])
        stA_free = var2_tok

        for i, srcr in ((0, rstd2R), (1, mrs2R)):
            P("tensor", lambda e, i=i, srcr=srcr: e.matmul(
                PS[:, bc[i]:bc[i] + 256], onesK[:], srcr[0:1, 0:256],
                start=True, stop=True),
              waits=[bc_free[i], mrs2_tok], inc="tE" if i == 1 else None)
        bcast2_tok = tok("tE")

        x1_toks = {}
        sqB_toks = {}
        for dt in range(NCT):
            P("vector", lambda e, dt=dt: TT(
                e, dd[:], aof[:, dt * 256:(dt + 1) * 256],
                PS[:, bc[0]:bc[0] + 256], ALU.mult),
              waits=[bcast2_tok] if dt == 0 else ())
            P("vector", lambda e: TT(e, dd[:], dd[:], PS[:, bc[1]:bc[1] + 256],
                                     ALU.subtract))
            P("vector", lambda e, dt=dt, xs=xs: TT(
                e, aof[:, dt * 256:(dt + 1) * 256], dd[:],
                xs[:, dt * 257 + 1:dt * 257 + 257], ALU.add), inc="vE")
            x1_toks[dt] = tok("vE")
            P("scalar", lambda e, dt=dt: e.activation(
                out=sqf[:, (dt % 2) * 256:(dt % 2) * 256 + 256],
                in_=aof[:, dt * 256:(dt + 1) * 256], func=AF.Square),
              waits=[x1_toks[dt], sq_free[dt % 2]], inc="sE")
            sqB_toks[dt] = tok("sE")
            P("sync", lambda e, dt=dt, c0=c0: e.dma_start(
                out=dx1[dt * 128:(dt + 1) * 128, c0:c0 + 256],
                in_=aof[:, dt * 256:(dt + 1) * 256]),
              waits=[x1_toks[dt]], inc="dO", n=16)
            if debug:
                P("sync", lambda e, dt=dt, c0=c0: e.dma_start(
                    out=dbg["x1"][dt * 128:(dt + 1) * 128, c0:c0 + 256],
                    in_=aof[:, dt * 256:(dt + 1) * 256]), inc="dO", n=16)
            P("tensor", lambda e, dt=dt: e.matmul(
                PS[0:1, 2304:2560], ones_s[:], aof[:, dt * 256:(dt + 1) * 256],
                start=(dt == 0), stop=(dt == 7)),
              waits=[x1_toks[dt], stB_free] if dt == 0 else [x1_toks[dt]])
            P("tensor", lambda e, dt=dt: e.matmul(
                PS[0:1, 2816:3072], ones_s[:],
                sqf[:, (dt % 2) * 256:(dt % 2) * 256 + 256],
                start=(dt == 0), stop=(dt == 7)),
              waits=[sqB_toks[dt]], inc="tE")
            sq_free[dt % 2] = tok("tE")
        tok_x1dma[tau] = pb.cnt["dO"]
        tok_statB[tau] = tok("tE")
        P("vector", lambda e, c0=c0: e.tensor_copy(
            rowsum[0:1, c0:c0 + 256], PS[0:1, 2304:2560]),
          waits=[tok_statB[tau]])
        P("vector", lambda e, c0=c0: e.tensor_copy(
            rowsq[0:1, c0:c0 + 256], PS[0:1, 2816:3072]), inc="vE")
        stB_free = tok("vE")
        tok_x1ops[tau] = tok("vE")
        bc_free[0] = tok_x1ops[tau]
        bc_free[1] = tok_x1ops[tau]

    for tau in range(NCH + 1):
        if tau < NCH:
            a_loads(tau)
            a_xnmix(tau)
        if tau >= 1:
            a_wkv(tau - 1)
        if tau < NCH:
            a_proj(tau)
            a_scanprep(tau)
        if tau >= 1:
            a_tail(tau - 1)

    pump_precast(len(precast_jobs))
    wcast_tok = (tok("c0s"), tok("c1s"))
    attn_done = [tok("tE"), tok("vE"), tok("sE"), tok("pE"),
                 ("dO", pb.cnt["dO"]), tok("dS")]

    ln2_tok, _ = ln_rows(rowsum[0:1, 0:T], rowsq[0:1, 0:T], rstdR[0:1, 0:T],
                         mrsR[0:1, 0:T], rowsum[0:1, 0:T], rowsq[0:1, 0:T],
                         [tok_x1ops[NCH - 1]])
    P("sync", lambda e: e.dma_start(out=drB.ap()[0:1, :], in_=rstdR[0:1, 0:T]),
      waits=[ln2_tok, ("dX", pb.cnt["dX"])], inc="dR", n=16)
    P("sync", lambda e: e.dma_start(out=drB.ap()[1:2, :], in_=mrsR[0:1, 0:T]),
      inc="dR", n=16)
    rows2_tok = tok("dR")

    hfree[0] = [wcast_tok[0], wcast_tok[1]]
    hfree[1] = [wcast_tok[0], wcast_tok[1]]
    hfree[2] = [wcast_tok[0], wcast_tok[1]]
    hfree[3] = [wcast_tok[0], wcast_tok[1]]

    for ct in range(NCT):
        i, buf, off = load1024(dW["rw1"][ct * 128:(ct + 1) * 128, :])
        P("scalar", lambda e, buf=buf, off=off, ct=ct: e.copy(
            rw1b[:, ct * 1024:(ct + 1) * 1024], buf[:, off:off + 1024]),
          waits=[hfill[i]] + (attn_done if ct == 0 else []), inc="sE", fast=(ct != 7))
        free_half(i, tok("sE"))
    rw1_tok = tok("sE")
    for ct in range(NCT):
        P("sync", lambda e, ct=ct: e.dma_start(
            out=rw2stg[:, ct * 8:(ct + 1) * 8], in_=drw2[ct * 128:(ct + 1) * 128, :]),
          waits=attn_done if ct == 0 else (), inc="dW", n=16)
    rw2d = tok("dW")
    P("scalar", lambda e: e.copy(rw2b[:], rw2stg[:]),
      waits=[rw2d], inc="sE")
    rw2_tok = tok("sE")


    bc_free2 = [attn_done[1], attn_done[1]]
    bc_idx = [0]

    def bc_load(c0, width, rows_tok, gates=()):
        p = bc_idx[0] % 2
        bc_idx[0] += 1
        sname = "dV%d" % p
        P("sync", lambda e, c0=c0, width=width, p=p: e.dma_start(
            out=bcF[2 * p][:, 0:width],
            in_=drB.ap()[0:1, c0:c0 + width].broadcast_to((128, width))),
          waits=[rows_tok, bc_free2[p]] + list(gates), inc=sname, n=16)
        P("sync", lambda e, c0=c0, width=width, p=p: e.dma_start(
            out=bcF[2 * p + 1][:, 0:width],
            in_=drB.ap()[1:2, c0:c0 + width].broadcast_to((128, width))),
          inc=sname, n=16)
        return p, tok(sname)

    for ct in range(NCT):
        fx = [attn_done[4]] if ct == 0 else []
        i, buf = load2048(dx1[ct * 128:(ct + 1) * 128, :], free_extra=fx, sem="dS")
        ld = hfill[i]
        for s in range(4):
            p, bct = bc_load(s * 512, 512, rows2_tok)
            P("vector", lambda e, buf=buf, s=s, p=p: TT(
                e, scrX[:], buf[:, s * 512:(s + 1) * 512], bcF[2 * p][:, 0:512],
                ALU.mult), waits=[ld, bct])
            P("vector", lambda e, ct=ct, s=s, p=p: TT(
                e, t_bf[:, ct * 2048 + s * 512:ct * 2048 + (s + 1) * 512],
                scrX[:], bcF[2 * p + 1][:, 0:512], ALU.subtract), inc="vE")
            bc_free2[p] = tok("vE")
        free_pair(i, tok("vE"))
    tbf_tok = tok("vE")
    if debug:
        for ct in range(NCT):
            P("sync", lambda e, ct=ct: e.dma_start(
                out=dbg["t"][ct * 128:(ct + 1) * 128, :],
                in_=t_bf[:, ct * 2048:(ct + 1) * 2048]),
              waits=[tbf_tok], inc="dS", n=16)

    rh_sl = [1024, 1536]
    rh_free = [attn_done[1], attn_done[1]]
    lg_free = attn_done[1]
    lg_toks = {}
    for tr in range(4):
        rhb_toks = {}
        for ht in range(NCT):
            sl = rh_sl[ht % 2]
            fw = rh_free[ht % 2]
            for ctk in range(NCT):
                P("tensor", lambda e, sl=sl, ht=ht, ctk=ctk, tr=tr: e.matmul(
                    PS[:, sl:sl + 512],
                    rw1b[:, ctk * 1024 + ht * 128:ctk * 1024 + ht * 128 + 128],
                    t_bf[:, ctk * 2048 + tr * 512:ctk * 2048 + (tr + 1) * 512],
                    start=(ctk == 0), stop=(ctk == 7)),
                  waits=[tbf_tok, fw, rw1_tok] if ctk == 0 else (),
                  inc="tE" if ctk == 7 else None)
            mmt = tok("tE")
            P("scalar", lambda e, sl=sl, ht=ht: e.activation(
                out=rhb[:, ht * 512:(ht + 1) * 512], in_=PS[:, sl:sl + 512],
                func=AF.Silu, bias=vecT[:, 24 + ht:25 + ht]),
              waits=[mmt, wcast_tok[0]] if (tr == 0 and ht == 0) else [mmt],
              inc="sE")
            rhb_toks[ht] = tok("sE")
            rh_free[ht % 2] = tok("sE")
            if debug and tr == 0:
                P("sync", lambda e, ht=ht: e.dma_start(
                    out=dbg["rh"][:, ht * 512:(ht + 1) * 512],
                    in_=rhb[:, ht * 512:(ht + 1) * 512]),
                  waits=[rhb_toks[ht]], inc="dS", n=16)
        for hc in range(NCT):
            w0 = [rhb_toks[hc]]
            if hc == 0:
                w0 += [rw2_tok, lg_free]
            P("tensor", lambda e, hc=hc: e.matmul(
                PS[0:8, 2048:2560], rw2b[:, hc * 8:(hc + 1) * 8],
                rhb[:, hc * 512:(hc + 1) * 512], start=(hc == 0), stop=(hc == 7)),
              waits=w0, inc="tE" if hc == 7 else None)
        lgmm = tok("tE")
        P("vector", lambda e, tr=tr: e.tensor_scalar(
            out=logits_sb[0:8, tr * 512:(tr + 1) * 512], in0=PS[0:8, 2048:2560],
            scalar1=rb2c[0:8, 0:1], scalar2=None, op0=ALU.add),
          waits=[lgmm], inc="vE")
        lg_toks[tr] = tok("vE")
        lg_free = tok("vE")
    logits_tok = tok("vE")
    if debug:
        P("sync", lambda e: e.dma_start(out=dbg["logits"], in_=logits_sb[:]),
          waits=[logits_tok], inc="dS", n=16)

    tp_sl = [2560, 2568]
    tp_free = [attn_done[1], attn_done[1]]
    wt_sl = 2576
    wt_free = attn_done[1]
    for j in range(16):
        P("tensor", lambda e, j=j: e.transpose(
            PS[:, tp_sl[j % 2]:tp_sl[j % 2] + 8],
            logits_sb[0:8, j * 128:(j + 1) * 128], eyeS[0:8, 0:8]),
          waits=[logits_tok, tp_free[j % 2]], inc="tE")
        tpt = tok("tE")
        P("vector", lambda e, j=j: e.tensor_copy(
            lgT[:], PS[:, tp_sl[j % 2]:tp_sl[j % 2] + 8]), waits=[tpt])
        vsp(2)
        P("vector", lambda e: e.reduce_max(out=kmax8[:, 0:1], in_=lgT[:], axis=AX))
        vsp(2)
        P("vector", lambda e: TS(e, wTj[:], lgT[:], kmax8[:, 0:1], ALU.is_ge))
        vsp(2)
        P("vector", lambda e: e.scalar_tensor_tensor(
            out=sc8[:], in0=wTj[:], scalar=-1e30, in1=lgT[:],
            op0=ALU.mult, op1=ALU.add))
        vsp(2)
        P("vector", lambda e: e.reduce_max(out=kmax8[:, 1:2], in_=sc8[:], axis=AX))
        vsp(2)
        P("vector", lambda e: TS(e, wTj[:], lgT[:], kmax8[:, 1:2], ALU.is_ge))
        P("vector", lambda e: TS(e, kmax8[:, 2:3], kmax8[:, 0:1], -1.0, ALU.mult))
        vsp(2, inc="vE")
        gv = tok("vE")
        P("scalar", lambda e: e.activation(out=sc8[:], in_=lgT[:], func=AF.Exp,
                                           bias=kmax8[:, 2:3]), waits=[gv], inc="sE")
        gs = tok("sE")
        P("vector", lambda e: TT(e, wTj[:], wTj[:], sc8[:], ALU.mult), waits=[gs])
        vsp(2)
        P("vector", lambda e: e.tensor_reduce(out=kmax8[:, 3:4], in_=wTj[:],
                                              axis=AX, op=ALU.add))
        vsp(2)
        P("vector", lambda e: e.reciprocal(out=kmax8[:, 4:5], in_=kmax8[:, 3:4]))
        vsp(2)
        P("vector", lambda e: TS(e, wTj[:], wTj[:], kmax8[:, 4:5], ALU.mult))
        vsp(2, inc="vE")
        gw = tok("vE")
        tp_free[j % 2] = gw
        P("tensor", lambda e, j=j: e.transpose(
            PS[0:8, wt_sl:wt_sl + 128], wTj[:], eyeS[:]),
          waits=[gw, wt_free], inc="tE")
        wtt = tok("tE")
        P("vector", lambda e, j=j: e.tensor_copy(
            w_fm[0:8, j * 128:(j + 1) * 128], PS[0:8, wt_sl:wt_sl + 128]),
          waits=[wtt], inc="vE")
        wt_free = tok("vE")
    gates_tok = tok("vE")
    P("scalar", lambda e: e.copy(wfmb[:], w_fm[:]),
      waits=[gates_tok, tok("tE"), logits_tok], inc="sE")
    wfmb_tok = tok("sE")
    P("sync", lambda e: e.dma_start(out=dwfm.ap(), in_=wfmb[:]),
      waits=[wfmb_tok], inc="dF", n=16)
    wfmd_tok = tok("dF")
    P("sync", lambda e: e.dma_start(out=eb2sb[:], in_=deb2), inc="dB", n=16)
    eb2_tok = tok("dB")
    if debug:
        P("sync", lambda e: e.dma_start(out=dbg["w"], in_=w_fm[:]),
          waits=[gates_tok], inc="dS", n=16)

    def plan_e1b_load(eidx, guard_toks):
        for ct in range(NCT):
            P("sync", lambda e, eidx=eidx, ct=ct: e.dma_start(
                out=e1b[:, ct * 2048:(ct + 1) * 2048],
                in_=dew1b.ap()[eidx * C + ct * 128:eidx * C + (ct + 1) * 128, :]),
              waits=(guard_toks + [wcast_tok[0], wcast_tok[1]]) if ct == 0 else (),
              inc="eA", n=16)
        return tok("eA")

    e1b_tok = plan_e1b_load(0, attn_done)
    ehw_guard = [gates_tok, logits_tok, wfmd_tok]
    spill_tok = None
    yev_last = {}
    wbuf_tok = None
    e2slot_free = [attn_done[0], attn_done[0]]

    for tf in range(2):
        tc0 = tf * TCF
        for ct in range(NCT):
            fx = []
            if tf == 1 and ct == 0:
                fx = [spill_tok]
            i, buf = load2048(dx1[ct * 128:(ct + 1) * 128, :], free_extra=fx, sem="dS")
            ld = hfill[i]
            for s2 in range(2):
                sidx = tf * 2 + s2
                gates = [spill_tok] if (tf == 1 and ct == 0 and s2 == 0) else []
                p, bct = bc_load(sidx * 512, 512, rows2_tok, gates)
                wv = [ld, bct] + gates
                P("vector", lambda e, buf=buf, s2=s2, tf=tf, p=p: TT(
                    e, scrX[:], buf[:, tf * 1024 + s2 * 512:tf * 1024 + (s2 + 1) * 512],
                    bcF[2 * p][:, 0:512], ALU.mult), waits=wv)
                P("vector", lambda e, ct=ct, s2=s2, p=p: TT(
                    e, yacc[:, ct * 1024 + s2 * 512:ct * 1024 + (s2 + 1) * 512],
                    scrX[:], bcF[2 * p + 1][:, 0:512], ALU.subtract), inc="vE")
                bc_free2[p] = tok("vE")
            free_pair(i, tok("vE"))
        yinit_tok = tok("vE")

        for eidx in range(NE):
            wg = [wfmd_tok, yinit_tok, tok("pE")]
            if wbuf_tok is not None:
                wg.append(wbuf_tok)
            P("sync", lambda e, eidx=eidx, tc0=tc0: e.dma_start(
                out=wbuf[:], in_=dwfm.ap()[eidx:eidx + 1, tc0:tc0 + TCF]
                .broadcast_to((128, TCF))),
              waits=wg, inc="dG", n=16)
            wbuf_tok = tok("dG")

            eh_sl = [0, 1024]
            ehw_toks = {}
            for hs in range(NHS):
                sl = eh_sl[hs % 2]
                if hs - 2 in ehw_toks:
                    slot_w = [ehw_toks[hs - 2][1]]
                else:
                    slot_w = [wbuf_tok]
                    for dsg in (2 * hs, 2 * hs + 1):
                        if (1, dsg) in yev_last:
                            slot_w.append(yev_last[(1, dsg)])
                        if (0, dsg) in yev_last:
                            slot_w.append(yev_last[(0, dsg)])
                    if not yev_last and eidx == 0 and tf == 0:
                        slot_w.append(tok("vE"))
                for nn in range(2):
                    for ctk in range(NCT):
                        w0 = []
                        if nn == 0 and ctk == 0:
                            w0 = slot_w + [e1b_tok, tbf_tok, wbuf_tok]
                            if hs == 0 and eidx == 0 and tf == 0:
                                w0 += ehw_guard
                        P("tensor", lambda e, sl=sl, nn=nn, ctk=ctk, hs=hs, tf=tf: e.matmul(
                            PS[:, sl + nn * 512:sl + (nn + 1) * 512],
                            e1b[:, ctk * 2048 + hs * 128:ctk * 2048 + hs * 128 + 128],
                            t_bf[:, ctk * 2048 + tf * 1024 + nn * 512:
                                 ctk * 2048 + tf * 1024 + (nn + 1) * 512],
                            start=(ctk == 0), stop=(ctk == 7)),
                          waits=w0, inc="tE" if (nn == 1 and ctk == 7) else None)
                mmt = tok("tE")
                P("scalar", lambda e, sl=sl, hs=hs, eidx=eidx: e.activation(
                    out=ehw[:, hs * 1024:(hs + 1) * 1024], in_=PS[:, sl:sl + 1024],
                    func=AF.Silu, bias=ebT[:, hs * 8 + eidx:hs * 8 + eidx + 1]),
                  waits=[mmt], inc="sE")
                silu_t = tok("sE")
                P("gpsimd", lambda e, hs=hs: e.tensor_mul(
                    ehw[:, hs * 1024:(hs + 1) * 1024],
                    ehw[:, hs * 1024:(hs + 1) * 1024], wbuf[:]),
                  waits=[silu_t], inc="pE", fast=True)
                ehw_toks[hs] = (tok("pE"), silu_t)
                if debug and tf == 0 and eidx == 1:
                    P("sync", lambda e, hs=hs: e.dma_start(
                        out=dbg["ehw"][hs * 128:(hs + 1) * 128, :],
                        in_=ehw[:, hs * 1024:(hs + 1) * 1024]),
                      waits=[ehw_toks[hs][0]], inc="dS", n=16)
            c1_tok = tok("tE")

            nxt = eidx + 1
            if nxt < NE:
                e1b_tok = plan_e1b_load(nxt, [c1_tok])
            elif tf == 0:
                e1b_tok = plan_e1b_load(0, [c1_tok])

            for nn in range(2):
                for hs in range(NHS):
                    ssem = "e2%d" % (hs % 2)
                    P("sync", lambda e, eidx=eidx, hs=hs: e.dma_start(
                        out=e2b[hs % 2][:],
                        in_=dew2b.ap()[eidx * HD + hs * 128:eidx * HD + (hs + 1) * 128, :]),
                      waits=[e2slot_free[hs % 2], wcast_tok[0], wcast_tok[1]],
                      inc=ssem, n=16)
                    ct_tok = tok(ssem)
                    mm_w = [ct_tok, ehw_toks[hs][0]]
                    if hs == 0:
                        mm_w += [tok("vE"), ehw_toks[NHS - 1][1], ehw_toks[NHS - 2][1],
                                 wbuf_tok]
                    for ds in range(8):
                        P("tensor", lambda e, ds=ds, hs=hs, nn=nn, eidx=eidx: e.matmul(
                            PS[:, ds * 512:(ds + 1) * 512],
                            e2b[hs % 2][:, ds * 128:(ds + 1) * 128],
                            ehw[:, hs * 1024 + nn * 512:hs * 1024 + (nn + 1) * 512],
                            start=(hs == 0),
                            stop=(hs == NHS - 1 and eidx != NE - 1)),
                          waits=mm_w if ds == 0 else (),
                          inc="tE" if ds == 7 else None)
                    e2slot_free[hs % 2] = tok("tE")
                if eidx == NE - 1:
                    for ds in range(8):
                        P("tensor", lambda e, ds=ds, nn=nn, tc0=tc0: e.matmul(
                            PS[:, ds * 512:(ds + 1) * 512],
                            eb2sb[0:8, ds * 128:(ds + 1) * 128],
                            w_fm[0:8, tc0 + nn * 512:tc0 + (nn + 1) * 512],
                            start=False, stop=True),
                          waits=[eb2_tok, gates_tok] if ds == 0 else (),
                          inc="tE" if ds == 7 else None)
                c2_tok = tok("tE")
                for ds in range(8):
                    P("vector", lambda e, ds=ds, nn=nn: TT(
                        e, yacc[:, ds * 1024 + nn * 512:ds * 1024 + (nn + 1) * 512],
                        PS[:, ds * 512:(ds + 1) * 512],
                        yacc[:, ds * 1024 + nn * 512:ds * 1024 + (nn + 1) * 512],
                        ALU.add),
                      waits=[c2_tok] if ds == 0 else (), inc="vE")
                    yev_last[(nn, ds)] = tok("vE")
        if tf == 0:
            for ds in range(8):
                P("sync", lambda e, ds=ds: e.dma_start(
                    out=dyd[ds * 128:(ds + 1) * 128, :],
                    in_=yacc[:, ds * 1024:(ds + 1) * 1024]),
                  waits=[yev_last[(0, ds)], yev_last[(1, ds)]], inc="dO", n=16)
            spill_tok = ("dO", pb.cnt["dO"])

    yfin = dict(yev_last)
    if debug:
        for ds in range(8):
            P("sync", lambda e, ds=ds: e.dma_start(
                out=dbg["y"][ds * 128:(ds + 1) * 128, TCF:2 * TCF],
                in_=yacc[:, ds * 1024:(ds + 1) * 1024]),
              waits=[yfin[(0, ds)], yfin[(1, ds)], spill_tok], inc="dS", n=16)
            P("sync", lambda e, ds=ds: e.dma_start(
                out=dbg["y"][ds * 128:(ds + 1) * 128, 0:TCF],
                in_=dyd[ds * 128:(ds + 1) * 128, :]), inc="dS", n=16)
        ydump = tok("dS")
        P("vector", lambda e: e.nop(), waits=[ydump])

    d_mm_last = None
    for hf in range(2):
        for ct in range(NCT):
            if hf == 0:
                fx = [spill_tok] if ct == 0 else []
                i, buf, off = load1024(dyd[ct * 128:(ct + 1) * 128, :],
                                       free_extra=fx, sem="dS")
                srcf = lambda s2, buf=buf, off=off: buf[:, off + s2 * 512:off + (s2 + 1) * 512]
                ld = [hfill[i]]
            else:
                srcf = lambda s2, ct=ct: yacc[:, ct * 1024 + s2 * 512:ct * 1024 + (s2 + 1) * 512]
                ld = [yfin[(0, ct)], yfin[(1, ct)]]
            for s2 in range(2):
                pw = [tok("vE")] if d_mm_last is None else [d_mm_last]
                P("gpsimd", lambda e, srcf=srcf, s2=s2: e.tensor_mul(
                    scrX[:], srcf(s2), srcf(s2)), waits=ld + pw, inc="pE")
                sqt = tok("pE")
                bidx = hf * 2 + s2
                P("tensor", lambda e, srcf=srcf, s2=s2, bidx=bidx, ct=ct: e.matmul(
                    PS[0:1, bidx * 512:(bidx + 1) * 512], ones_s[:], srcf(s2),
                    start=(ct == 0), stop=(ct == 7)),
                  waits=ld + ([tok("vE"), e2slot_free[0], e2slot_free[1]]
                              if hf == 0 and ct == 0 and s2 == 0 else []))
                P("tensor", lambda e, bidx=bidx, ct=ct: e.matmul(
                    PS[0:1, 2048 + bidx * 512:2048 + (bidx + 1) * 512], ones_s[:],
                    scrX[:], start=(ct == 0), stop=(ct == 7)),
                  waits=[sqt], inc="tE")
                d_mm_last = tok("tE")
            if hf == 0:
                free_half(i, tok("tE"))
    d_mm = tok("tE")
    for s in range(4):
        P("vector", lambda e, s=s: e.tensor_copy(
            rowsum[0:1, s * 512:(s + 1) * 512], PS[0:1, s * 512:(s + 1) * 512]),
          waits=[d_mm] if s == 0 else ())
        P("vector", lambda e, s=s: e.tensor_copy(
            rowsq[0:1, s * 512:(s + 1) * 512],
            PS[0:1, 2048 + s * 512:2048 + (s + 1) * 512]),
          inc="vE" if s == 3 else None)
    d_ev = tok("vE")
    ln3_tok, _ = ln_rows(rowsum[0:1, 0:T], rowsq[0:1, 0:T], rstdR[0:1, 0:T],
                         mrsR[0:1, 0:T], rowsum[0:1, 0:T], rowsq[0:1, 0:T],
                         [d_ev])
    P("sync", lambda e: e.dma_start(out=drB.ap()[0:1, :], in_=rstdR[0:1, 0:T]),
      waits=[ln3_tok, ("dV0", pb.cnt["dV0"]), ("dV1", pb.cnt["dV1"])], inc="dR", n=16)
    P("sync", lambda e: e.dma_start(out=drB.ap()[1:2, :], in_=mrsR[0:1, 0:T]),
      inc="dR", n=16)
    rows3_tok = tok("dR")


    for ct in range(NCT):
        for hf in range(2):
            i1, b1, o1 = load1024(dx1[ct * 128:(ct + 1) * 128,
                                      hf * 1024:(hf + 1) * 1024], sem="dS")
            ldx = hfill[i1]
            if hf == 0:
                i2, b2, o2 = load1024(dyd[ct * 128:(ct + 1) * 128, :], sem="dS")
                ysrc = lambda s2, b2=b2, o2=o2: b2[:, o2 + s2 * 512:o2 + (s2 + 1) * 512]
                ldy = [hfill[i2]]
            else:
                ysrc = lambda s2, ct=ct: yacc[:, ct * 1024 + s2 * 512:ct * 1024 + (s2 + 1) * 512]
                ldy = []
            last_ot = None
            for s2 in range(2):
                sidx = hf * 2 + s2
                p, bct = bc_load(sidx * 512, 512, rows3_tok)
                ob = outbuf[:, ct * 2048 + hf * 1024 + s2 * 512:
                            ct * 2048 + hf * 1024 + (s2 + 1) * 512]
                P("vector", lambda e, ysrc=ysrc, s2=s2, p=p: TT(
                    e, scrX[:], ysrc(s2), bcF[2 * p][:, 0:512], ALU.mult),
                  waits=[bct, ldx] + ldy)
                P("vector", lambda e, p=p: TT(e, scrX[:], scrX[:],
                                              bcF[2 * p + 1][:, 0:512],
                                              ALU.subtract))
                P("vector", lambda e, ob=ob, s2=s2, b1=b1, o1=o1: TT(
                    e, ob, scrX[:], b1[:, o1 + s2 * 512:o1 + (s2 + 1) * 512],
                    ALU.add), inc="vE")
                ot = tok("vE")
                bc_free2[p] = ot
                last_ot = ot
                P("sync", lambda e, ob=ob, s2=s2, ct=ct, hf=hf: e.dma_start(
                    out=dout[ct * 128:(ct + 1) * 128,
                             hf * 1024 + s2 * 512:hf * 1024 + (s2 + 1) * 512],
                    in_=ob), waits=[ot], inc="dO", n=16)
            free_half(i1, last_ot)
            if hf == 0:
                free_half(i2, last_ot)

    final_dO = tok("dO")
    P("sync", lambda e: e.nop(), waits=[final_dO])

    pb.emit_all()
    return nc


def kernel(**inputs):
    x = np.asarray(inputs["L0_attn_x"], np.float32)
    Wk = np.asarray(inputs["L0_attn_Wk"], np.float32)
    Wv = np.asarray(inputs["L0_attn_Wv"], np.float32)
    Wr = np.asarray(inputs["L0_attn_Wr"], np.float32)
    Wo = np.asarray(inputs["L0_attn_Wo"], np.float32)
    vec6 = np.concatenate([
        np.asarray(inputs["L0_attn_mix_k"], np.float32).reshape(8, 128),
        np.asarray(inputs["L0_attn_mix_v"], np.float32).reshape(8, 128),
        np.asarray(inputs["L0_attn_mix_r"], np.float32).reshape(8, 128),
        np.asarray(inputs["L0_ffn_rb1"], np.float32).reshape(8, 128),
        np.asarray(inputs["L0_attn_w"], np.float32).reshape(8, 128),
        np.asarray(inputs["L0_attn_u"], np.float32).reshape(8, 128),
    ], axis=0)
    common = dict(
        WkT=np.ascontiguousarray(Wk.T),
        WvT=np.ascontiguousarray(Wv.T),
        WrT=np.ascontiguousarray(Wr.T),
        WoT=np.ascontiguousarray(Wo.T),
        rw1=np.ascontiguousarray(np.asarray(inputs["L0_ffn_rw1"], np.float32)),
        rw2=np.ascontiguousarray(np.asarray(inputs["L0_ffn_rw2"], np.float32)),
        vec6=np.ascontiguousarray(vec6),
        eb1=np.ascontiguousarray(np.asarray(inputs["L0_ffn_eb1"], np.float32)),
        eb2=np.ascontiguousarray(np.asarray(inputs["L0_ffn_eb2"], np.float32)),
        rb2=np.ascontiguousarray(np.asarray(inputs["L0_ffn_rb2"], np.float32).reshape(8, 1)),
        ew1=np.ascontiguousarray(np.asarray(inputs["L0_ffn_ew1"], np.float32).reshape(NE * C, HD)),
        ew2=np.ascontiguousarray(np.asarray(inputs["L0_ffn_ew2"], np.float32).reshape(NE * HD, C)),
        eye=np.eye(128, dtype=np.float32),
    )
    nc = build_program(debug=False)
    in_maps = []
    for b in range(8):
        m = dict(common)
        m["x"] = np.ascontiguousarray(x[b].T)
        in_maps.append(m)
    res = run_bass_kernel_spmd(nc, in_maps, core_ids=list(range(8)))
    out = np.stack([np.asarray(r["out"], np.float32).T for r in res.results], axis=0)
    return np.ascontiguousarray(out)


# version 13
# speedup vs baseline: 1.0393x; 1.0393x over previous
import contextlib
import numpy as np
import concourse.bass as bass
import concourse.mybir as mybir
from concourse.bass_utils import run_bass_kernel_spmd

F32 = mybir.dt.float32
BF16 = mybir.dt.bfloat16
AF = mybir.ActivationFunctionType
ALU = mybir.AluOpType
AX = mybir.AxisListType.X

C = 1024
T = 2048
NCT = 8
TCA = 256
NCH = T // TCA
TCF = 1024
NE = 8
HD = 2048
NHS = 16
EPS = 1e-5
NEG = -1e30

ENGINES = ("sync", "tensor", "vector", "scalar", "gpsimd")


class PB:
    def __init__(self, nc):
        self.nc = nc
        self.steps = {e: [] for e in ENGINES}
        self.sems = {}
        self.cnt = {}
        self._es = contextlib.ExitStack()

    def sem(self, name):
        self.sems[name] = self._es.enter_context(self.nc.semaphore(name))
        self.cnt[name] = 0

    def plan(self, eng, emit, waits=(), inc=None, n=1, fast=False):
        incs = ()
        if inc is not None:
            self.cnt[inc] += n
            incs = ((inc, n),)
        for s, v in waits:
            assert v <= self.cnt[s], f"future wait {s}:{v}>{self.cnt[s]}"
        ws = tuple((s, v) for (s, v) in waits if v > 0)
        self.steps[eng].append((ws, emit, incs, fast))

    def tok(self, s):
        return (s, self.cnt[s])

    def emit_all(self):
        with self.nc.Block() as block:
            for e in ENGINES:
                steps = self.steps[e]
                sems = self.sems
                drained = e in ("vector", "scalar", "gpsimd")

                def body(eng, steps=steps, drained=drained):
                    last = {}
                    for waits, emit, incs, fast in steps:
                        for s, v in waits:
                            if last.get(s, 0) < v:
                                eng.wait_ge(sems[s], v)
                                last[s] = v
                        instr = emit(eng)
                        dma_incs = [(s, nn) for s, nn in incs if nn >= 16]
                        eng_incs = [(s, nn) for s, nn in incs if nn < 16]
                        for s, nn in dma_incs:
                            instr.then_inc(sems[s], nn)
                        if drained and not dma_incs and not fast:
                            instr = eng.drain()
                        for s, nn in eng_incs:
                            instr.then_inc(sems[s], nn)
                getattr(block, e)(body)
        self._es.close()


class SbufMap:
    def __init__(self, nc):
        self.nc = nc
        base = nc.SBUF_PARTITION_SIZE_BYTES - nc.sbuf_bytes_remaining
        self.base = (base + 127) // 128 * 128
        self.cap = nc.SBUF_PARTITION_SIZE_BYTES - self.base - 512
        self.cursor = 0
        self.marks = {}
        self.offs = {}

    def alloc(self, name, shape, dtype, align=64):
        itm = 2 if dtype == BF16 else 4
        self.cursor = (self.cursor + align - 1) // align * align
        off = self.cursor
        self.cursor += shape[-1] * itm
        assert self.cursor <= self.cap, (name, self.cursor, self.cap)
        self.offs[name] = off
        return self.nc.alloc_sbuf_tensor_at(name, shape, dtype,
                                            offset=self.base + off)

    def at(self, name, shape, dtype, off):
        itm = 2 if dtype == BF16 else 4
        assert off + shape[-1] * itm <= self.cap, name
        self.offs[name] = off
        return self.nc.alloc_sbuf_tensor_at(name, shape, dtype,
                                            offset=self.base + off)

    def mark(self, name):
        self.marks[name] = self.cursor

    def reset_to(self, name):
        self.cursor = self.marks[name]


def build_program(debug=False):
    nc = bass.Bass("TRN2", target_bir_lowering=False)

    dx = nc.dram_tensor("x", [C, T], F32, kind="ExternalInput").ap()
    dW = {}
    for nm in ("WkT", "WvT", "WrT", "WoT", "rw1"):
        dW[nm] = nc.dram_tensor(nm, [C, C], F32, kind="ExternalInput").ap()
    drw2 = nc.dram_tensor("rw2", [C, NE], F32, kind="ExternalInput").ap()
    dvec = nc.dram_tensor("vec6", [48, 128], F32, kind="ExternalInput").ap()
    deb1 = nc.dram_tensor("eb1", [NE, HD], F32, kind="ExternalInput").ap()
    deb2 = nc.dram_tensor("eb2", [NE, C], F32, kind="ExternalInput").ap()
    drb2 = nc.dram_tensor("rb2", [NE, 1], F32, kind="ExternalInput").ap()
    dew1 = nc.dram_tensor("ew1", [NE * C, HD], F32, kind="ExternalInput").ap()
    dew2 = nc.dram_tensor("ew2", [NE * HD, C], F32, kind="ExternalInput").ap()
    deye = nc.dram_tensor("eye", [128, 128], F32, kind="ExternalInput").ap()

    dout = nc.dram_tensor("out", [C, T], F32, kind="ExternalOutput").ap()
    dx1 = nc.dram_tensor("x1d", [C, T], F32, kind="Internal").ap()
    dyd = nc.dram_tensor("yd", [C, TCF], F32, kind="Internal").ap()
    dwfm = nc.dram_tensor("wfm_d", [NE, T], BF16, kind="Internal")
    dew1b = nc.dram_tensor("ew1b_d", [NE * C, HD], BF16, kind="Internal")
    dew2b = nc.dram_tensor("ew2b_d", [NE * HD, C], BF16, kind="Internal")
    drB = nc.dram_tensor("rows_d", [2, T], F32, kind="Internal")
    dbg = {}
    if debug:
        for nm, shp in (("x1", [C, T]), ("logits", [NE, T]), ("w", [NE, T]),
                        ("k", [C, T]), ("wkv", [C, T]), ("carry", [128, 256]),
                        ("stats", [2, T])):
            dbg[nm] = nc.dram_tensor("dbg_" + nm, shp, F32,
                                     kind="ExternalOutput").ap()
        dbg["t"] = nc.dram_tensor("dbg_t", [C, T], BF16, kind="ExternalOutput").ap()
        dbg["y"] = nc.dram_tensor("dbg_y", [C, T], F32, kind="ExternalOutput").ap()
        dbg["ehw"] = nc.dram_tensor("dbg_ehw", [HD, TCF], BF16, kind="ExternalOutput").ap()
        dbg["rh"] = nc.dram_tensor("dbg_rh", [128, 4096], BF16, kind="ExternalOutput").ap()
        dbg["dcb"] = nc.dram_tensor("dbg_dcb", [128, 2080], F32, kind="ExternalOutput").ap()

    sm = SbufMap(nc)
    A = sm.alloc
    eyeS = A("eyeS", [128, 128], F32)
    onesK = A("onesK", [1, 128], F32)
    ones_s = A("ones_s", [128, 1], F32)
    vecT = A("vecT", [128, 48], F32)
    dcoefT = A("dcoefT", [128, 8], F32)
    euT = A("euT", [128, 8], F32)
    ebT = A("ebT", [128, 128], F32)
    rb2c = A("rb2c", [NE, 1], F32)
    Pc8 = A("Pc8", [128, 8], F32)
    Qc8 = A("Qc8", [128, 8], F32)
    M8 = A("M8", [128, 8], F32)
    kmax8 = A("kmax8", [128, 8], F32)
    dMs = A("dMs", [128, 8], F32)
    sc8 = A("sc8", [128, 8], F32)
    negM8 = A("negM8", [128, 8], F32)
    rowsum = A("rowsum", [1, T], F32)
    rowsq = A("rowsq", [1, T], F32)
    rstdR = A("rstdR", [1, T], F32)
    mrsR = A("mrsR", [1, T], F32)
    stgA0 = A("stgA0", [128, 2048], F32)
    stgA1 = A("stgA1", [128, 2048], F32)
    junk = A("junk", [128, 16], F32)
    sm.mark("overlay")

    Wbk = A("Wbk", [128, 8192], BF16)
    Wbv = A("Wbv", [128, 8192], BF16)
    Wbr = A("Wbr", [128, 8192], BF16)
    Wbo = A("Wbo", [128, 8192], BF16)
    xsb = [A("xs0", [128, 8 * 257], F32), A("xs1", [128, 8 * 257], F32)]
    xn = A("xn", [128, 8 * 257], F32)
    xkb = A("xkb", [128, 2048], BF16)
    xvb = A("xvb", [128, 2048], BF16)
    xrb = A("xrb", [128, 2048], BF16)
    kf = A("kf", [128, 2048], F32)
    vf = A("vf", [128, 2048], F32)
    erf = A("erf", [128, 2048], F32)
    dd = A("dd", [128, 256], F32)
    Escr = A("Escr", [128, 256], F32)
    Gscr = A("Gscr", [128, 256], F32)
    Nscr = A("Nscr", [128, 256], F32)
    Dscr = A("Dscr", [128, 256], F32)
    Pbuf = A("Pbuf", [128, 256], F32)
    Qbuf = A("Qbuf", [128, 256], F32)
    sqf = A("sqf", [128, 512], F32)
    mean2R = A("mean2R", [1, TCA], F32)
    scr2 = A("scr2", [1, TCA], F32)
    rstd2R = A("rstd2R", [1, TCA], F32)
    mrs2R = A("mrs2R", [1, TCA], F32)
    aofR = A("aofR", [128, 2048], F32)
    xbc0 = A("xbc0", [128, 264], F32)
    xbc1 = A("xbc1", [128, 264], F32)
    rwkvbR = A("rwkvbR", [128, 2048], BF16)
    attn_end = sm.cursor
    sm.reset_to("overlay")

    yacc = A("yacc", [128, 8 * TCF], F32)
    t_bf = A("t_bf", [128, 8 * T], BF16)
    e1b = A("e1b", [128, 8 * 2048], BF16)
    ehw = A("ehw", [128, NHS * 1024], BF16)
    e2b = [A("e2b0", [128, 1024], BF16), A("e2b1", [128, 1024], BF16)]
    rw2b = A("rw2b", [128, 64], BF16)
    wbuf = A("wbuf", [128, TCF], BF16)
    scrX = A("scrX", [128, 512], F32)
    eb2sb = A("eb2sb", [NE, C], F32)
    w_fm = A("w_fm", [NE, T], F32)
    lgT = A("lgT", [128, 8], F32)
    wTj = A("wTj", [128, 8], F32)
    rw2stg = A("rw2stg", [128, 64], F32)
    bcF = [A("bcF0", [128, 512], F32), A("bcF1", [128, 512], F32),
           A("bcF2", [128, 512], F32), A("bcF3", [128, 512], F32)]
    ffn_end = sm.cursor
    assert max(attn_end, ffn_end) <= sm.cap, (attn_end, ffn_end, sm.cap)

    rhb = sm.at("rhb", [128, 4096], BF16, sm.offs["ehw"])
    wfmb = sm.at("wfmb", [NE, T], BF16, sm.offs["ehw"])
    stgCa = sm.at("stgCa", [128, 2048], BF16, sm.offs["stgA0"])
    logits_sb = sm.at("logits_sb", [NE, T], F32, sm.offs["ehw"] + 8192)
    rw1b = sm.at("rw1b", [128, 8192], BF16, sm.offs["ehw"] + 16384)
    aof = sm.at("aof", [128, 2048], F32, sm.offs["aofR"])
    rwkvb = sm.at("rwkvb", [128, 2048], BF16, sm.offs["rwkvbR"])
    vstg = sm.at("vstg", [8, 768], F32, sm.offs["vf"])
    outbuf = sm.at("outbuf", [128, 16384], F32, sm.offs["e1b"])
    ebstg = sm.at("ebstg", [8, 2048], F32, sm.offs["kf"])

    PS = nc.alloc_psum_tensor("PS", [128, 4096], F32)

    pb = PB(nc)
    for s in ("dS", "dW", "dO", "dG", "dF", "dB", "dR", "dV0", "dV1", "dX",
              "tE", "vE", "sE", "pE", "x0", "x1s", "sl0", "sl1", "sl2", "sl3",
              "c0s", "c1s", "eA", "e20", "e21"):
        pb.sem(s)
    P = pb.plan
    tok = pb.tok

    def TT(e, out, a, b, op):
        return e.tensor_tensor(out=out, in0=a, in1=b, op=op)

    def TS(e, out, a, s1, op):
        return e.tensor_scalar(out=out, in0=a, scalar1=s1, scalar2=None, op0=op)

    def vsp(n=2, inc=None):
        for q in range(n):
            P("vector", lambda e: e.tensor_scalar(
                out=junk[:], in0=junk[:], scalar1=0.0, scalar2=None, op0=ALU.mult),
              inc=inc if q == n - 1 else None)

    P("sync", lambda e: e.dma_start(out=eyeS[:], in_=deye), inc="dW", n=16)
    for g in range(6):
        P("sync", lambda e, g=g: e.dma_start(
            out=vstg[:, g * 128:(g + 1) * 128], in_=dvec[g * 8:(g + 1) * 8, :]),
          inc="dW", n=16)
    P("sync", lambda e: e.dma_start(out=ebstg[:], in_=deb1), inc="dW", n=16)
    P("sync", lambda e: e.dma_start(out=rb2c[:], in_=drb2), inc="dW", n=16)
    w_const = tok("dW")

    P("vector", lambda e: e.memset(junk[:], 0.0))
    P("vector", lambda e: e.memset(onesK[:], 1.0))
    P("vector", lambda e: e.memset(ones_s[:], 1.0))
    P("vector", lambda e: e.memset(M8[:], NEG))
    P("vector", lambda e: e.memset(Pc8[:], 0.0))
    P("vector", lambda e: e.memset(Qc8[:], 0.0), inc="vE")
    init_tok = tok("vE")

    for j in range(16):
        P("tensor", lambda e, j=j: e.transpose(
            PS[:, 8 * j:8 * j + 8], ebstg[0:8, j * 128:(j + 1) * 128],
            eyeS[0:8, 0:8]), waits=[w_const] if j == 0 else ())
    for g in range(6):
        P("tensor", lambda e, g=g: e.transpose(
            PS[:, 128 + 8 * g:136 + 8 * g], vstg[0:8, g * 128:(g + 1) * 128],
            eyeS[0:8, 0:8]), inc="tE" if g == 5 else None)
    tr_tok = tok("tE")
    P("scalar", lambda e: e.copy(ebT[:], PS[:, 0:128]), waits=[tr_tok])
    P("scalar", lambda e: e.copy(vecT[:], PS[:, 128:176]), inc="sE")
    vecT_tok = tok("sE")
    P("scalar", lambda e: e.activation(out=dcoefT[:], in_=vecT[:, 32:40], func=AF.Exp))
    P("scalar", lambda e: e.activation(out=dcoefT[:], in_=dcoefT[:], func=AF.Exp, scale=-1.0))
    P("scalar", lambda e: e.activation(out=euT[:], in_=vecT[:, 40:48], func=AF.Exp))
    P("scalar", lambda e: e.copy(dMs[:], dcoefT[:]))
    P("scalar", lambda e: e.copy(sc8[:], euT[:]), inc="sE")
    dco_tok = tok("sE")

    half = [(stgA0, 0), (stgA0, 1024), (stgA1, 0), (stgA1, 1024)]
    hfree = [init_tok] * 4
    hfill = [("dW", 0)] * 4
    hrr = [0]
    prr = [0]

    def load1024(src, free_extra=(), sem=None):
        i = hrr[0] % 4
        hrr[0] += 1
        buf, off = half[i]
        ssem = f"sl{i}"
        h = hfree[i] if isinstance(hfree[i], list) else [hfree[i]]
        P("sync", lambda e, buf=buf, off=off, src=src: e.dma_start(
            out=buf[:, off:off + 1024], in_=src),
          waits=h + list(free_extra), inc=ssem, n=16)
        hfill[i] = tok(ssem)
        return i, buf, off

    def load2048(src, free_extra=(), sem=None):
        i = 0 if (prr[0] % 2 == 0) else 2
        prr[0] += 1
        buf = half[i][0]
        ssem = f"sl{i}"
        h0 = hfree[i] if isinstance(hfree[i], list) else [hfree[i]]
        h1 = hfree[i + 1] if isinstance(hfree[i + 1], list) else [hfree[i + 1]]
        P("sync", lambda e, buf=buf, src=src: e.dma_start(
            out=buf[:, 0:2048], in_=src),
          waits=h0 + h1 + list(free_extra), inc=ssem, n=16)
        hfill[i] = hfill[i + 1] = tok(ssem)
        return i, buf

    def free_half(i, t):
        hfree[i] = t

    def free_pair(i, t):
        hfree[i] = t
        hfree[i + 1] = t

    wbt = {}
    for nm, dst in (("WkT", Wbk), ("WvT", Wbv), ("WrT", Wbr), ("WoT", Wbo)):
        for ct in range(NCT):
            i, buf, off = load1024(dW[nm][ct * 128:(ct + 1) * 128, :])
            P("scalar", lambda e, buf=buf, off=off, dst=dst, ct=ct: e.copy(
                dst[:, ct * 1024:(ct + 1) * 1024], buf[:, off:off + 1024]),
              waits=[hfill[i]], inc="sE", fast=(ct != 7))
            free_half(i, tok("sE"))
        wbt[nm] = tok("sE")

    stgC_free = [wbt["WoT"], wbt["WoT"]]
    stgC_sem = ["c0s", "c1s"]
    pc_in_free = [wbt["WoT"], wbt["WoT"]]
    ci = [0]

    def precast(srcap, dstap):
        q = ci[0] % 2
        ci[0] += 1
        isem = "sl2" if q == 0 else "sl3"
        P("sync", lambda e, srcap=srcap, q=q: e.dma_start(
            out=stgA1[:, q * 1024:(q + 1) * 1024], in_=srcap),
          waits=[pc_in_free[q]], inc=isem, n=16)
        ld = tok(isem)
        P("gpsimd", lambda e, q=q: e.tensor_copy(
            stgCa[:, q * 1024:(q + 1) * 1024], stgA1[:, q * 1024:(q + 1) * 1024]),
          waits=[ld, stgC_free[q]], inc="pE", fast=True)
        cst = tok("pE")
        pc_in_free[q] = cst
        P("sync", lambda e, dstap=dstap, q=q: e.dma_start(
            out=dstap, in_=stgCa[:, q * 1024:(q + 1) * 1024]),
          waits=[cst], inc=stgC_sem[q], n=16)
        stgC_free[q] = tok(stgC_sem[q])

    precast_jobs = []
    for eidx in range(NE):
        for ct in range(NCT):
            r0 = eidx * C + ct * 128
            for hh in range(2):
                precast_jobs.append(
                    (dew1[r0:r0 + 128, hh * 1024:(hh + 1) * 1024],
                     dew1b.ap()[r0:r0 + 128, hh * 1024:(hh + 1) * 1024]))
    for eidx in range(NE):
        for hs in range(NHS):
            r0 = eidx * HD + hs * 128
            precast_jobs.append(
                (dew2[r0:r0 + 128, :], dew2b.ap()[r0:r0 + 128, :]))
    pc_pos = [0]

    def pump_precast(n):
        while n > 0 and pc_pos[0] < len(precast_jobs):
            precast(*precast_jobs[pc_pos[0]])
            pc_pos[0] += 1
            n -= 1

    ph0_mm_tok = {}
    for ct in range(NCT):
        xs = xsb[ct % 2]
        xw = [ph0_mm_tok[ct - 2]] if ct - 2 in ph0_mm_tok else [init_tok]
        P("sync", lambda e, xs=xs, ct=ct: e.dma_start(
            out=xs[:, 0:2048], in_=dx[ct * 128:(ct + 1) * 128, :]),
          waits=xw, inc="x0" if ct % 2 == 0 else "x1s", n=16)
        ld = tok("x0" if ct % 2 == 0 else "x1s")
        P("gpsimd", lambda e, xs=xs: e.tensor_mul(
            xn[:, 0:2048], xs[:, 0:2048], xs[:, 0:2048]),
          waits=[ld] + ([ph0_mm_tok[ct - 1]] if ct - 1 in ph0_mm_tok else []),
          inc="pE")
        sq = tok("pE")
        for s in range(4):
            P("tensor", lambda e, xs=xs, s=s, ct=ct: e.matmul(
                PS[0:1, s * 512:(s + 1) * 512], ones_s[:],
                xs[:, s * 512:(s + 1) * 512], start=(ct == 0), stop=(ct == 7)),
              waits=[ld, vecT_tok] if s == 0 else ())
        for s in range(4):
            P("tensor", lambda e, s=s, ct=ct: e.matmul(
                PS[0:1, 2048 + s * 512:2048 + (s + 1) * 512], ones_s[:],
                xn[:, s * 512:(s + 1) * 512], start=(ct == 0), stop=(ct == 7)),
              waits=[sq] if s == 0 else (), inc="tE" if s == 3 else None)
        ph0_mm_tok[ct] = tok("tE")
    ph0_mm = tok("tE")

    def ln_rows(sum_ap, sq_ap, rstd_ap, mrs_ap, mean_ap, scr_ap, waits_v):
        vsp(2)
        P("vector", lambda e: TS(e, mean_ap, sum_ap, 1.0 / C, ALU.mult), waits=waits_v)
        P("vector", lambda e: TS(e, scr_ap, sq_ap, 1.0 / C, ALU.mult))
        vsp(1)
        P("vector", lambda e: TT(e, rstd_ap, mean_ap, mean_ap, ALU.mult))
        P("vector", lambda e: TS(e, scr_ap, scr_ap, EPS, ALU.add))
        vsp(1)
        P("vector", lambda e: TT(e, scr_ap, scr_ap, rstd_ap, ALU.subtract))
        vsp(2, inc="vE")
        tv = tok("vE")
        P("scalar", lambda e: e.activation(out=rstd_ap, in_=scr_ap, func=AF.Ln),
          waits=[tv])
        P("scalar", lambda e: e.activation(out=rstd_ap, in_=rstd_ap, func=AF.Exp, scale=-0.5))
        P("scalar", lambda e: e.copy(junk[:], junk[:]))
        P("scalar", lambda e: e.copy(junk[:], junk[:]), inc="sE")
        ts_ = tok("sE")
        P("vector", lambda e: TT(e, mrs_ap, mean_ap, rstd_ap, ALU.mult), waits=[ts_])
        vsp(2, inc="vE")
        return tok("vE"), tv

    for s in range(4):
        P("vector", lambda e, s=s: e.tensor_copy(
            rowsum[0:1, s * 512:(s + 1) * 512], PS[0:1, s * 512:(s + 1) * 512]),
          waits=[ph0_mm] if s == 0 else ())
        P("vector", lambda e, s=s: e.tensor_copy(
            rowsq[0:1, s * 512:(s + 1) * 512],
            PS[0:1, 2048 + s * 512:2048 + (s + 1) * 512]),
          inc="vE" if s == 3 else None)
    ph0_ev = tok("vE")
    ln1_tok, _ = ln_rows(rowsum[0:1, 0:T], rowsq[0:1, 0:T], rstdR[0:1, 0:T],
                         mrsR[0:1, 0:T], rowsum[0:1, 0:T], rowsq[0:1, 0:T],
                         [ph0_ev])
    P("sync", lambda e: e.dma_start(out=drB.ap()[0:1, :], in_=rstdR[0:1, 0:T]),
      waits=[ln1_tok], inc="dR", n=16)
    P("sync", lambda e: e.dma_start(out=drB.ap()[1:2, :], in_=mrsR[0:1, 0:T]),
      inc="dR", n=16)
    rows1_tok = tok("dR")

    pp = [0, 512, 1024, 1536]
    pp_free = [ph0_ev] * 4
    pp_rr = [0]
    bc = [3072, 3584]
    bc_free = [ph0_ev, ph0_ev]
    stA_free = ph0_ev
    stB_free = ph0_ev
    sq_free = [ph0_ev, ph0_ev]
    tok_x1ops = {}
    tok_x1dma = {}
    tok_statB = {}
    tok_xn = {}
    st = {}

    def a_loads(tau):
        s = st[tau] = {}
        s["xs"] = xsb[tau % 2]
        s["c0"] = tau * TCA
        xs, c0 = s["xs"], s["c0"]
        xw = [tok_x1ops[tau - 2]] if tau - 2 in tok_x1ops else [ph0_ev]
        xsem = "x0" if tau % 2 == 0 else "x1s"
        for ct in range(NCT):
            if tau == 0:
                P("sync", lambda e, ct=ct, xs=xs: e.dma_start(
                    out=xs[:, ct * 257 + 1:ct * 257 + 257],
                    in_=dx[ct * 128:(ct + 1) * 128, 0:256]),
                  waits=xw, inc=xsem, n=16)
            else:
                P("sync", lambda e, ct=ct, xs=xs, c0=c0: e.dma_start(
                    out=xs[:, ct * 257:ct * 257 + 257],
                    in_=dx[ct * 128:(ct + 1) * 128, c0 - 1:c0 + 256]),
                  waits=xw, inc=xsem, n=16)
        s["xs_tok"] = tok(xsem)
        lo = max(c0 - 1, 0)
        o0h = 1 if tau == 0 else 0
        xbw = [rows1_tok]
        if tau - 1 in tok_xn:
            xbw.append(tok_xn[tau - 1])
        P("sync", lambda e, lo=lo, o0h=o0h, c0=c0: e.dma_start(
            out=xbc0[:, o0h:257],
            in_=drB.ap()[0:1, lo:c0 + 256].broadcast_to((128, 257 - o0h))),
          waits=xbw, inc="dX", n=16)
        P("sync", lambda e, lo=lo, o0h=o0h, c0=c0: e.dma_start(
            out=xbc1[:, o0h:257],
            in_=drB.ap()[1:2, lo:c0 + 256].broadcast_to((128, 257 - o0h))),
          inc="dX", n=16)
        s["bcast_tok"] = tok("dX")
        pump_precast(8)

    def a_xnmix(tau):
        s = st[tau]
        xs = s["xs"]
        o0 = 1 if tau == 0 else 0
        for ct in range(NCT):
            P("vector", lambda e, ct=ct, xs=xs, o0=o0: TT(
                e, xn[:, ct * 257 + o0:(ct + 1) * 257],
                xs[:, ct * 257 + o0:(ct + 1) * 257],
                xbc0[:, o0:257], ALU.mult),
              waits=[s["xs_tok"], s["bcast_tok"]] if ct == 0 else ())
            last = (ct == 7) and (tau != 0)
            P("vector", lambda e, ct=ct, o0=o0: TT(
                e, xn[:, ct * 257 + o0:(ct + 1) * 257],
                xn[:, ct * 257 + o0:(ct + 1) * 257],
                xbc1[:, o0:257], ALU.subtract),
              inc="vE" if last else None)
        if tau == 0:
            for ct in range(NCT):
                P("vector", lambda e, ct=ct: e.memset(xn[:, ct * 257:ct * 257 + 1], 0.0),
                  inc="vE" if ct == 7 else None)
        tok_xn[tau] = tok("vE")
        mix_w = [st[tau - 1]["projr_tok"]] if tau - 1 in st else []
        for ct in range(NCT):
            cur = xn[:, ct * 257 + 1:ct * 257 + 257]
            prv = xn[:, ct * 257:ct * 257 + 256]
            P("vector", lambda e, cur=cur, prv=prv: TT(e, dd[:], cur, prv, ALU.subtract),
              waits=mix_w if ct == 0 else ())
            for dst, col in ((xkb, 0), (xvb, 8), (xrb, 16)):
                lastop = (ct == 7 and col == 16)
                P("vector", lambda e, dst=dst, col=col, ct=ct, prv=prv:
                  e.scalar_tensor_tensor(
                      out=dst[:, ct * 256:(ct + 1) * 256], in0=dd[:],
                      scalar=vecT[:, col + ct:col + ct + 1], in1=prv,
                      op0=ALU.mult, op1=ALU.add),
                  inc="vE" if lastop else None)
        s["mix_tok"] = tok("vE")
        pump_precast(8)

    def a_proj(tau):
        s = st[tau]
        rw_prev = st[tau - 1]["rw_toks"] if tau - 1 in st else {}
        ev_toks = {}
        for pi, nm in enumerate(("k", "v", "r")):
            Wb = (Wbk, Wbv, Wbr)[pi]
            srcb = (xkb, xvb, xrb)[pi]
            wkey = ("WkT", "WvT", "WrT")[pi]
            dstb = (kf, vf, None)[pi]
            for dt in range(NCT):
                sli = pp_rr[0] % 4
                sl = pp[sli]
                fw = pp_free[sli]
                pp_rr[0] += 1
                for ctk in range(NCT):
                    P("tensor", lambda e, sl=sl, Wb=Wb, ctk=ctk, dt=dt, srcb=srcb: e.matmul(
                        PS[:, sl:sl + 256],
                        Wb[:, ctk * 1024 + dt * 128:ctk * 1024 + dt * 128 + 128],
                        srcb[:, ctk * 256:(ctk + 1) * 256],
                        start=(ctk == 0), stop=(ctk == 7)),
                      waits=[s["mix_tok"], fw, wbt[wkey]] if ctk == 0 else (),
                      inc="tE" if ctk == 7 else None)
                tk = tok("tE")
                guard = [rw_prev[dt]] if dt in rw_prev else []
                if nm == "r":
                    P("scalar", lambda e, sl=sl, dt=dt: e.activation(
                        out=erf[:, dt * 256:(dt + 1) * 256], in_=PS[:, sl:sl + 256],
                        func=AF.Exp, scale=-1.0), waits=[tk] + guard, inc="sE")
                else:
                    P("scalar", lambda e, sl=sl, dstb=dstb, dt=dt: e.copy(
                        dstb[:, dt * 256:(dt + 1) * 256], PS[:, sl:sl + 256]),
                      waits=[tk] + guard, inc="sE")
                ev_toks[(nm, dt)] = tok("sE")
                pp_free[sli] = tok("sE")
        s["ev_toks"] = ev_toks
        s["projr_tok"] = tok("tE")
        pump_precast(8)
        if debug:
            for dt in range(NCT):
                P("sync", lambda e, dt=dt, c0=s["c0"]: e.dma_start(
                    out=dbg["k"][dt * 128:(dt + 1) * 128, c0:c0 + 256],
                    in_=kf[:, dt * 256:(dt + 1) * 256]),
                  waits=[ev_toks[("k", dt)]], inc="dS", n=16)
            s["kdump_tok"] = tok("dS")

    def a_scanprep(tau):
        s = st[tau]
        ev_toks = s["ev_toks"]
        for dt in range(NCT):
            P("vector", lambda e, dt=dt: e.reduce_max(
                out=kmax8[:, dt:dt + 1], in_=kf[:, dt * 256:(dt + 1) * 256], axis=AX),
              waits=[ev_toks[("k", dt)], dco_tok])
        vsp(2)
        fprev = [st[tau - 1]["F7"]] if tau - 1 in st and "F7" in st[tau - 1] else []
        P("vector", lambda e: TT(e, kmax8[:], kmax8[:], M8[:], ALU.max), waits=fprev)
        vsp(2)
        P("vector", lambda e: TT(e, dMs[:], M8[:], kmax8[:], ALU.subtract))
        P("vector", lambda e: e.tensor_copy(M8[:], kmax8[:]))
        P("vector", lambda e: TS(e, negM8[:], kmax8[:], -1.0, ALU.mult))
        vsp(2, inc="vE")
        mb_tok = tok("vE")
        P("scalar", lambda e: e.activation(out=sc8[:], in_=dMs[:], func=AF.Exp),
          waits=[mb_tok], inc="sE")
        sc_tok = tok("sE")
        P("vector", lambda e: TT(e, Pc8[:], Pc8[:], sc8[:], ALU.mult), waits=[sc_tok])
        P("vector", lambda e: TT(e, Qc8[:], Qc8[:], sc8[:], ALU.mult))
        vsp(2, inc="vE")
        s["resc_tok"] = tok("vE")
        if debug:
            P("sync", lambda e, tau=tau: e.dma_start(
                out=dbg["carry"][:, tau * 32:tau * 32 + 8], in_=Pc8[:]),
              waits=[s["resc_tok"]], inc="dS", n=16)
            P("sync", lambda e, tau=tau: e.dma_start(
                out=dbg["carry"][:, tau * 32 + 8:tau * 32 + 16], in_=Qc8[:]),
              inc="dS", n=16)
            carry_dump = tok("dS")
            P("vector", lambda e: e.nop(), waits=[carry_dump])

    def a_wkv(tau):
        s = st[tau]
        ev_toks = s["ev_toks"]
        F_toks = {}
        for dt in range(NCT):
            fwaits = [s["resc_tok"]] if dt == 0 else []
            if debug and dt == 0 and "kdump_tok" in s:
                fwaits.append(s["kdump_tok"])
            P("scalar", lambda e, dt=dt: e.activation(
                out=kf[:, dt * 256:(dt + 1) * 256], in_=kf[:, dt * 256:(dt + 1) * 256],
                func=AF.Exp, bias=negM8[:, dt:dt + 1]), waits=fwaits, inc="sE")
            F_toks[dt] = tok("sE")
        s["F7"] = F_toks[7]
        wo_prev = [st[tau - 1]["wo_last"]] if tau - 1 in st and "wo_last" in st[tau - 1] else []
        rw_toks = {}
        for dt in range(NCT):
            kfs = kf[:, dt * 256:(dt + 1) * 256]
            vfs = vf[:, dt * 256:(dt + 1) * 256]
            dcbv = dcoefT[:, dt:dt + 1].broadcast_to((128, 256))
            w0 = [F_toks[dt]] + (wo_prev if dt == 0 else [])
            P("vector", lambda e, kfs=kfs, vfs=vfs: TT(e, Escr[:], kfs, vfs, ALU.mult),
              waits=w0)
            P("vector", lambda e, dt=dt, dcbv=dcbv: e.tensor_tensor_scan(
                out=Pbuf[:], data0=dcbv, data1=Escr[:],
                initial=Pc8[:, dt:dt + 1], op0=ALU.mult, op1=ALU.add))
            P("vector", lambda e, dt=dt, kfs=kfs, dcbv=dcbv: e.tensor_tensor_scan(
                out=Qbuf[:], data0=dcbv, data1=kfs,
                initial=Qc8[:, dt:dt + 1], op0=ALU.mult, op1=ALU.add))
            P("vector", lambda e, dt=dt, kfs=kfs, vfs=vfs: e.scalar_tensor_tensor(
                out=Gscr[:], in0=kfs, scalar=euT[:, dt:dt + 1], in1=vfs,
                op0=ALU.mult, op1=ALU.mult))
            P("vector", lambda e: TT(e, Nscr[:, 1:256], Gscr[:, 1:256],
                                     Pbuf[:, 0:255], ALU.add))
            P("vector", lambda e, dt=dt: TT(e, Nscr[:, 0:1], Gscr[:, 0:1],
                                            Pc8[:, dt:dt + 1], ALU.add))
            P("vector", lambda e, dt=dt, kfs=kfs: e.scalar_tensor_tensor(
                out=Dscr[:, 1:256], in0=kfs[:, 1:256], scalar=euT[:, dt:dt + 1],
                in1=Qbuf[:, 0:255], op0=ALU.mult, op1=ALU.add))
            P("vector", lambda e, dt=dt, kfs=kfs: e.scalar_tensor_tensor(
                out=Dscr[:, 0:1], in0=kfs[:, 0:1], scalar=euT[:, dt:dt + 1],
                in1=Qc8[:, dt:dt + 1], op0=ALU.mult, op1=ALU.add))
            P("vector", lambda e, dt=dt: e.tensor_copy(Pc8[:, dt:dt + 1], Pbuf[:, 255:256]),
              fast=True)
            P("vector", lambda e, dt=dt: e.tensor_copy(Qc8[:, dt:dt + 1], Qbuf[:, 255:256]),
              fast=True)
            P("vector", lambda e, dt=dt: e.scalar_tensor_tensor(
                out=Gscr[:], in0=erf[:, dt * 256:(dt + 1) * 256], scalar=1.0,
                in1=Dscr[:], op0=ALU.add, op1=ALU.mult),
              waits=[ev_toks[("r", dt)]])
            P("vector", lambda e: e.reciprocal(out=Escr[:], in_=Gscr[:]))
            vsp(1)
            P("vector", lambda e, dt=dt: TT(
                e, rwkvb[:, dt * 256:(dt + 1) * 256], Nscr[:], Escr[:], ALU.mult),
              inc="vE")
            rw_toks[dt] = tok("vE")
            if debug:
                P("vector", lambda e: TT(e, Nscr[:], Nscr[:], Escr[:], ALU.mult), inc="vE")
                dtk = tok("vE")
                P("sync", lambda e, dt=dt, c0=s["c0"]: e.dma_start(
                    out=dbg["wkv"][dt * 128:(dt + 1) * 128, c0:c0 + 256], in_=Nscr[:]),
                  waits=[dtk], inc="dS", n=16)
        s["rw_toks"] = rw_toks

    def a_tail(tau):
        nonlocal stA_free, stB_free
        s = st[tau]
        xs, c0 = s["xs"], s["c0"]
        rw_toks = s["rw_toks"]
        ao_toks = {}
        sq_toks = {}
        aof_guard = []
        if tau - 1 in tok_x1dma:
            aof_guard = [("dO", tok_x1dma[tau - 1]), tok_x1ops[tau - 1]]
        for dt in range(NCT):
            sli = pp_rr[0] % 4
            sl = pp[sli]
            fw = pp_free[sli]
            pp_rr[0] += 1
            for ctk in range(NCT):
                P("tensor", lambda e, sl=sl, ctk=ctk, dt=dt: e.matmul(
                    PS[:, sl:sl + 256],
                    Wbo[:, ctk * 1024 + dt * 128:ctk * 1024 + dt * 128 + 128],
                    rwkvb[:, ctk * 256:(ctk + 1) * 256],
                    start=(ctk == 0), stop=(ctk == 7)),
                  waits=[rw_toks[7], fw, wbt["WoT"]] if ctk == 0 else (),
                  inc="tE" if ctk == 7 else None)
            tk = tok("tE")
            P("scalar", lambda e, sl=sl, dt=dt: e.copy(
                aof[:, dt * 256:(dt + 1) * 256], PS[:, sl:sl + 256]),
              waits=[tk] + (aof_guard if dt == 0 else []), inc="sE")
            ao_toks[dt] = tok("sE")
            pp_free[sli] = tok("sE")
            if dt == NCT - 1:
                s["wo_last"] = tk
            P("scalar", lambda e, dt=dt: e.activation(
                out=sqf[:, (dt % 2) * 256:(dt % 2) * 256 + 256],
                in_=aof[:, dt * 256:(dt + 1) * 256], func=AF.Square),
              waits=[sq_free[dt % 2]], inc="sE")
            sq_toks[dt] = tok("sE")
            P("tensor", lambda e, dt=dt: e.matmul(
                PS[0:1, 2048:2304], ones_s[:], aof[:, dt * 256:(dt + 1) * 256],
                start=(dt == 0), stop=(dt == 7)),
              waits=[ao_toks[dt], stA_free] if dt == 0 else [ao_toks[dt]])
            P("tensor", lambda e, dt=dt: e.matmul(
                PS[0:1, 2560:2816], ones_s[:],
                sqf[:, (dt % 2) * 256:(dt % 2) * 256 + 256],
                start=(dt == 0), stop=(dt == 7)),
              waits=[sq_toks[dt]], inc="tE")
            sq_free[dt % 2] = tok("tE")
        statA_tok = tok("tE")

        mrs2_tok, var2_tok = ln_rows(
            PS[0:1, 2048:2304], PS[0:1, 2560:2816], rstd2R[0:1, 0:TCA],
            mrs2R[0:1, 0:TCA], mean2R[0:1, 0:TCA], scr2[0:1, 0:TCA],
            [statA_tok])
        stA_free = var2_tok

        for i, srcr in ((0, rstd2R), (1, mrs2R)):
            P("tensor", lambda e, i=i, srcr=srcr: e.matmul(
                PS[:, bc[i]:bc[i] + 256], onesK[:], srcr[0:1, 0:256],
                start=True, stop=True),
              waits=[bc_free[i], mrs2_tok], inc="tE" if i == 1 else None)
        bcast2_tok = tok("tE")

        x1_toks = {}
        sqB_toks = {}
        for dt in range(NCT):
            P("vector", lambda e, dt=dt: TT(
                e, dd[:], aof[:, dt * 256:(dt + 1) * 256],
                PS[:, bc[0]:bc[0] + 256], ALU.mult),
              waits=[bcast2_tok] if dt == 0 else ())
            P("vector", lambda e: TT(e, dd[:], dd[:], PS[:, bc[1]:bc[1] + 256],
                                     ALU.subtract))
            P("vector", lambda e, dt=dt, xs=xs: TT(
                e, aof[:, dt * 256:(dt + 1) * 256], dd[:],
                xs[:, dt * 257 + 1:dt * 257 + 257], ALU.add), inc="vE")
            x1_toks[dt] = tok("vE")
            P("scalar", lambda e, dt=dt: e.activation(
                out=sqf[:, (dt % 2) * 256:(dt % 2) * 256 + 256],
                in_=aof[:, dt * 256:(dt + 1) * 256], func=AF.Square),
              waits=[x1_toks[dt], sq_free[dt % 2]], inc="sE")
            sqB_toks[dt] = tok("sE")
            P("sync", lambda e, dt=dt, c0=c0: e.dma_start(
                out=dx1[dt * 128:(dt + 1) * 128, c0:c0 + 256],
                in_=aof[:, dt * 256:(dt + 1) * 256]),
              waits=[x1_toks[dt]], inc="dO", n=16)
            if debug:
                P("sync", lambda e, dt=dt, c0=c0: e.dma_start(
                    out=dbg["x1"][dt * 128:(dt + 1) * 128, c0:c0 + 256],
                    in_=aof[:, dt * 256:(dt + 1) * 256]), inc="dO", n=16)
            P("tensor", lambda e, dt=dt: e.matmul(
                PS[0:1, 2304:2560], ones_s[:], aof[:, dt * 256:(dt + 1) * 256],
                start=(dt == 0), stop=(dt == 7)),
              waits=[x1_toks[dt], stB_free] if dt == 0 else [x1_toks[dt]])
            P("tensor", lambda e, dt=dt: e.matmul(
                PS[0:1, 2816:3072], ones_s[:],
                sqf[:, (dt % 2) * 256:(dt % 2) * 256 + 256],
                start=(dt == 0), stop=(dt == 7)),
              waits=[sqB_toks[dt]], inc="tE")
            sq_free[dt % 2] = tok("tE")
        tok_x1dma[tau] = pb.cnt["dO"]
        tok_statB[tau] = tok("tE")
        P("vector", lambda e, c0=c0: e.tensor_copy(
            rowsum[0:1, c0:c0 + 256], PS[0:1, 2304:2560]),
          waits=[tok_statB[tau]])
        P("vector", lambda e, c0=c0: e.tensor_copy(
            rowsq[0:1, c0:c0 + 256], PS[0:1, 2816:3072]), inc="vE")
        stB_free = tok("vE")
        tok_x1ops[tau] = tok("vE")
        bc_free[0] = tok_x1ops[tau]
        bc_free[1] = tok_x1ops[tau]
        pump_precast(8)

    for tau in range(NCH + 1):
        if tau < NCH:
            a_loads(tau)
            a_xnmix(tau)
        if tau >= 1:
            a_wkv(tau - 1)
        if tau < NCH:
            a_proj(tau)
            a_scanprep(tau)
        if tau >= 1:
            a_tail(tau - 1)

    pump_precast(len(precast_jobs))
    wcast_tok = (tok("c0s"), tok("c1s"))
    attn_done = [tok("tE"), tok("vE"), tok("sE"), tok("pE"),
                 ("dO", pb.cnt["dO"]), tok("dS")]

    ln2_tok, _ = ln_rows(rowsum[0:1, 0:T], rowsq[0:1, 0:T], rstdR[0:1, 0:T],
                         mrsR[0:1, 0:T], rowsum[0:1, 0:T], rowsq[0:1, 0:T],
                         [tok_x1ops[NCH - 1]])
    P("sync", lambda e: e.dma_start(out=drB.ap()[0:1, :], in_=rstdR[0:1, 0:T]),
      waits=[ln2_tok, ("dX", pb.cnt["dX"])], inc="dR", n=16)
    P("sync", lambda e: e.dma_start(out=drB.ap()[1:2, :], in_=mrsR[0:1, 0:T]),
      inc="dR", n=16)
    rows2_tok = tok("dR")

    hfree[0] = [wcast_tok[0], wcast_tok[1]]
    hfree[1] = [wcast_tok[0], wcast_tok[1]]
    hfree[2] = [wcast_tok[0], wcast_tok[1]]
    hfree[3] = [wcast_tok[0], wcast_tok[1]]

    for ct in range(NCT):
        i, buf, off = load1024(dW["rw1"][ct * 128:(ct + 1) * 128, :])
        P("scalar", lambda e, buf=buf, off=off, ct=ct: e.copy(
            rw1b[:, ct * 1024:(ct + 1) * 1024], buf[:, off:off + 1024]),
          waits=[hfill[i]] + (attn_done if ct == 0 else []), inc="sE", fast=(ct != 7))
        free_half(i, tok("sE"))
    rw1_tok = tok("sE")
    for ct in range(NCT):
        P("sync", lambda e, ct=ct: e.dma_start(
            out=rw2stg[:, ct * 8:(ct + 1) * 8], in_=drw2[ct * 128:(ct + 1) * 128, :]),
          waits=attn_done if ct == 0 else (), inc="dW", n=16)
    rw2d = tok("dW")
    P("scalar", lambda e: e.copy(rw2b[:], rw2stg[:]),
      waits=[rw2d], inc="sE")
    rw2_tok = tok("sE")


    bc_free2 = [attn_done[1], attn_done[1]]
    bc_idx = [0]

    def bc_load(c0, width, rows_tok, gates=()):
        p = bc_idx[0] % 2
        bc_idx[0] += 1
        sname = "dV%d" % p
        P("sync", lambda e, c0=c0, width=width, p=p: e.dma_start(
            out=bcF[2 * p][:, 0:width],
            in_=drB.ap()[0:1, c0:c0 + width].broadcast_to((128, width))),
          waits=[rows_tok, bc_free2[p]] + list(gates), inc=sname, n=16)
        P("sync", lambda e, c0=c0, width=width, p=p: e.dma_start(
            out=bcF[2 * p + 1][:, 0:width],
            in_=drB.ap()[1:2, c0:c0 + width].broadcast_to((128, width))),
          inc=sname, n=16)
        return p, tok(sname)

    for ct in range(NCT):
        fx = [attn_done[4]] if ct == 0 else []
        i, buf = load2048(dx1[ct * 128:(ct + 1) * 128, :], free_extra=fx, sem="dS")
        ld = hfill[i]
        for s in range(4):
            p, bct = bc_load(s * 512, 512, rows2_tok)
            P("vector", lambda e, buf=buf, s=s, p=p: TT(
                e, scrX[:], buf[:, s * 512:(s + 1) * 512], bcF[2 * p][:, 0:512],
                ALU.mult), waits=[ld, bct])
            P("vector", lambda e, ct=ct, s=s, p=p: TT(
                e, t_bf[:, ct * 2048 + s * 512:ct * 2048 + (s + 1) * 512],
                scrX[:], bcF[2 * p + 1][:, 0:512], ALU.subtract), inc="vE")
            bc_free2[p] = tok("vE")
        free_pair(i, tok("vE"))
    tbf_tok = tok("vE")
    if debug:
        for ct in range(NCT):
            P("sync", lambda e, ct=ct: e.dma_start(
                out=dbg["t"][ct * 128:(ct + 1) * 128, :],
                in_=t_bf[:, ct * 2048:(ct + 1) * 2048]),
              waits=[tbf_tok], inc="dS", n=16)

    rh_sl = [1024, 1536]
    rh_free = [attn_done[1], attn_done[1]]
    lg_free = attn_done[1]
    lg_toks = {}
    for tr in range(4):
        rhb_toks = {}
        for ht in range(NCT):
            sl = rh_sl[ht % 2]
            fw = rh_free[ht % 2]
            for ctk in range(NCT):
                P("tensor", lambda e, sl=sl, ht=ht, ctk=ctk, tr=tr: e.matmul(
                    PS[:, sl:sl + 512],
                    rw1b[:, ctk * 1024 + ht * 128:ctk * 1024 + ht * 128 + 128],
                    t_bf[:, ctk * 2048 + tr * 512:ctk * 2048 + (tr + 1) * 512],
                    start=(ctk == 0), stop=(ctk == 7)),
                  waits=[tbf_tok, fw, rw1_tok] if ctk == 0 else (),
                  inc="tE" if ctk == 7 else None)
            mmt = tok("tE")
            P("scalar", lambda e, sl=sl, ht=ht: e.activation(
                out=rhb[:, ht * 512:(ht + 1) * 512], in_=PS[:, sl:sl + 512],
                func=AF.Silu, bias=vecT[:, 24 + ht:25 + ht]),
              waits=[mmt], inc="sE")
            rhb_toks[ht] = tok("sE")
            rh_free[ht % 2] = tok("sE")
            if debug and tr == 0:
                P("sync", lambda e, ht=ht: e.dma_start(
                    out=dbg["rh"][:, ht * 512:(ht + 1) * 512],
                    in_=rhb[:, ht * 512:(ht + 1) * 512]),
                  waits=[rhb_toks[ht]], inc="dS", n=16)
        for hc in range(NCT):
            w0 = [rhb_toks[hc]]
            if hc == 0:
                w0 += [rw2_tok, lg_free]
            P("tensor", lambda e, hc=hc: e.matmul(
                PS[0:8, 2048:2560], rw2b[:, hc * 8:(hc + 1) * 8],
                rhb[:, hc * 512:(hc + 1) * 512], start=(hc == 0), stop=(hc == 7)),
              waits=w0, inc="tE" if hc == 7 else None)
        lgmm = tok("tE")
        P("vector", lambda e, tr=tr: e.tensor_scalar(
            out=logits_sb[0:8, tr * 512:(tr + 1) * 512], in0=PS[0:8, 2048:2560],
            scalar1=rb2c[0:8, 0:1], scalar2=None, op0=ALU.add),
          waits=[lgmm], inc="vE")
        lg_toks[tr] = tok("vE")
        lg_free = tok("vE")
    logits_tok = tok("vE")
    if debug:
        P("sync", lambda e: e.dma_start(out=dbg["logits"], in_=logits_sb[:]),
          waits=[logits_tok], inc="dS", n=16)

    tp_sl = [2560, 2568]
    tp_free = [attn_done[1], attn_done[1]]
    wt_sl = 2576
    wt_free = attn_done[1]
    for j in range(16):
        P("tensor", lambda e, j=j: e.transpose(
            PS[:, tp_sl[j % 2]:tp_sl[j % 2] + 8],
            logits_sb[0:8, j * 128:(j + 1) * 128], eyeS[0:8, 0:8]),
          waits=[logits_tok, tp_free[j % 2]], inc="tE")
        tpt = tok("tE")
        P("vector", lambda e, j=j: e.tensor_copy(
            lgT[:], PS[:, tp_sl[j % 2]:tp_sl[j % 2] + 8]), waits=[tpt])
        vsp(2)
        P("vector", lambda e: e.reduce_max(out=kmax8[:, 0:1], in_=lgT[:], axis=AX))
        vsp(2)
        P("vector", lambda e: TS(e, wTj[:], lgT[:], kmax8[:, 0:1], ALU.is_ge))
        vsp(2)
        P("vector", lambda e: e.scalar_tensor_tensor(
            out=sc8[:], in0=wTj[:], scalar=-1e30, in1=lgT[:],
            op0=ALU.mult, op1=ALU.add))
        vsp(2)
        P("vector", lambda e: e.reduce_max(out=kmax8[:, 1:2], in_=sc8[:], axis=AX))
        vsp(2)
        P("vector", lambda e: TS(e, wTj[:], lgT[:], kmax8[:, 1:2], ALU.is_ge))
        P("vector", lambda e: TS(e, kmax8[:, 2:3], kmax8[:, 0:1], -1.0, ALU.mult))
        vsp(2, inc="vE")
        gv = tok("vE")
        P("scalar", lambda e: e.activation(out=sc8[:], in_=lgT[:], func=AF.Exp,
                                           bias=kmax8[:, 2:3]), waits=[gv], inc="sE")
        gs = tok("sE")
        P("vector", lambda e: TT(e, wTj[:], wTj[:], sc8[:], ALU.mult), waits=[gs])
        vsp(2)
        P("vector", lambda e: e.tensor_reduce(out=kmax8[:, 3:4], in_=wTj[:],
                                              axis=AX, op=ALU.add))
        vsp(2)
        P("vector", lambda e: e.reciprocal(out=kmax8[:, 4:5], in_=kmax8[:, 3:4]))
        vsp(2)
        P("vector", lambda e: TS(e, wTj[:], wTj[:], kmax8[:, 4:5], ALU.mult))
        vsp(2, inc="vE")
        gw = tok("vE")
        tp_free[j % 2] = gw
        P("tensor", lambda e, j=j: e.transpose(
            PS[0:8, wt_sl:wt_sl + 128], wTj[:], eyeS[:]),
          waits=[gw, wt_free], inc="tE")
        wtt = tok("tE")
        P("vector", lambda e, j=j: e.tensor_copy(
            w_fm[0:8, j * 128:(j + 1) * 128], PS[0:8, wt_sl:wt_sl + 128]),
          waits=[wtt], inc="vE")
        wt_free = tok("vE")
    gates_tok = tok("vE")
    P("scalar", lambda e: e.copy(wfmb[:], w_fm[:]),
      waits=[gates_tok, tok("tE"), logits_tok], inc="sE")
    wfmb_tok = tok("sE")
    P("sync", lambda e: e.dma_start(out=dwfm.ap(), in_=wfmb[:]),
      waits=[wfmb_tok], inc="dF", n=16)
    wfmd_tok = tok("dF")
    P("sync", lambda e: e.dma_start(out=eb2sb[:], in_=deb2), inc="dB", n=16)
    eb2_tok = tok("dB")
    if debug:
        P("sync", lambda e: e.dma_start(out=dbg["w"], in_=w_fm[:]),
          waits=[gates_tok], inc="dS", n=16)

    def plan_e1b_load(eidx, guard_toks):
        for ct in range(NCT):
            P("sync", lambda e, eidx=eidx, ct=ct: e.dma_start(
                out=e1b[:, ct * 2048:(ct + 1) * 2048],
                in_=dew1b.ap()[eidx * C + ct * 128:eidx * C + (ct + 1) * 128, :]),
              waits=(guard_toks + [wcast_tok[0], wcast_tok[1]]) if ct == 0 else (),
              inc="eA", n=16)
        return tok("eA")

    e1b_tok = plan_e1b_load(0, attn_done)
    ehw_guard = [gates_tok, logits_tok, wfmd_tok]
    spill_tok = None
    yev_last = {}
    wbuf_tok = None
    e2slot_free = [attn_done[0], attn_done[0]]

    for tf in range(2):
        tc0 = tf * TCF
        for ct in range(NCT):
            fx = []
            if tf == 1 and ct == 0:
                fx = [spill_tok]
            i, buf = load2048(dx1[ct * 128:(ct + 1) * 128, :], free_extra=fx, sem="dS")
            ld = hfill[i]
            for s2 in range(2):
                sidx = tf * 2 + s2
                gates = [spill_tok] if (tf == 1 and ct == 0 and s2 == 0) else []
                p, bct = bc_load(sidx * 512, 512, rows2_tok, gates)
                wv = [ld, bct] + gates
                P("vector", lambda e, buf=buf, s2=s2, tf=tf, p=p: TT(
                    e, scrX[:], buf[:, tf * 1024 + s2 * 512:tf * 1024 + (s2 + 1) * 512],
                    bcF[2 * p][:, 0:512], ALU.mult), waits=wv)
                P("vector", lambda e, ct=ct, s2=s2, p=p: TT(
                    e, yacc[:, ct * 1024 + s2 * 512:ct * 1024 + (s2 + 1) * 512],
                    scrX[:], bcF[2 * p + 1][:, 0:512], ALU.subtract), inc="vE")
                bc_free2[p] = tok("vE")
            free_pair(i, tok("vE"))
        yinit_tok = tok("vE")

        for eidx in range(NE):
            wg = [wfmd_tok, yinit_tok, tok("pE")]
            if wbuf_tok is not None:
                wg.append(wbuf_tok)
            P("sync", lambda e, eidx=eidx, tc0=tc0: e.dma_start(
                out=wbuf[:], in_=dwfm.ap()[eidx:eidx + 1, tc0:tc0 + TCF]
                .broadcast_to((128, TCF))),
              waits=wg, inc="dG", n=16)
            wbuf_tok = tok("dG")

            eh_sl = [0, 1024]
            ehw_toks = {}
            for hs in range(NHS):
                sl = eh_sl[hs % 2]
                if hs - 2 in ehw_toks:
                    slot_w = [ehw_toks[hs - 2][1]]
                else:
                    slot_w = [wbuf_tok]
                    for dsg in (2 * hs, 2 * hs + 1):
                        if (1, dsg) in yev_last:
                            slot_w.append(yev_last[(1, dsg)])
                        if (0, dsg) in yev_last:
                            slot_w.append(yev_last[(0, dsg)])
                    if not yev_last and eidx == 0 and tf == 0:
                        slot_w.append(tok("vE"))
                for nn in range(2):
                    for ctk in range(NCT):
                        w0 = []
                        if nn == 0 and ctk == 0:
                            w0 = slot_w + [e1b_tok, tbf_tok, wbuf_tok]
                            if hs == 0 and eidx == 0 and tf == 0:
                                w0 += ehw_guard
                        P("tensor", lambda e, sl=sl, nn=nn, ctk=ctk, hs=hs, tf=tf: e.matmul(
                            PS[:, sl + nn * 512:sl + (nn + 1) * 512],
                            e1b[:, ctk * 2048 + hs * 128:ctk * 2048 + hs * 128 + 128],
                            t_bf[:, ctk * 2048 + tf * 1024 + nn * 512:
                                 ctk * 2048 + tf * 1024 + (nn + 1) * 512],
                            start=(ctk == 0), stop=(ctk == 7)),
                          waits=w0, inc="tE" if (nn == 1 and ctk == 7) else None)
                mmt = tok("tE")
                P("scalar", lambda e, sl=sl, hs=hs, eidx=eidx: e.activation(
                    out=ehw[:, hs * 1024:(hs + 1) * 1024], in_=PS[:, sl:sl + 1024],
                    func=AF.Silu, bias=ebT[:, hs * 8 + eidx:hs * 8 + eidx + 1]),
                  waits=[mmt], inc="sE")
                silu_t = tok("sE")
                P("gpsimd", lambda e, hs=hs: e.tensor_mul(
                    ehw[:, hs * 1024:(hs + 1) * 1024],
                    ehw[:, hs * 1024:(hs + 1) * 1024], wbuf[:]),
                  waits=[silu_t], inc="pE", fast=True)
                ehw_toks[hs] = (tok("pE"), silu_t)
                if debug and tf == 0 and eidx == 1:
                    P("sync", lambda e, hs=hs: e.dma_start(
                        out=dbg["ehw"][hs * 128:(hs + 1) * 128, :],
                        in_=ehw[:, hs * 1024:(hs + 1) * 1024]),
                      waits=[ehw_toks[hs][0]], inc="dS", n=16)
            c1_tok = tok("tE")

            nxt = eidx + 1
            if nxt < NE:
                e1b_tok = plan_e1b_load(nxt, [c1_tok])
            elif tf == 0:
                e1b_tok = plan_e1b_load(0, [c1_tok])

            for nn in range(2):
                for hs in range(NHS):
                    ssem = "e2%d" % (hs % 2)
                    P("sync", lambda e, eidx=eidx, hs=hs: e.dma_start(
                        out=e2b[hs % 2][:],
                        in_=dew2b.ap()[eidx * HD + hs * 128:eidx * HD + (hs + 1) * 128, :]),
                      waits=[e2slot_free[hs % 2], wcast_tok[0], wcast_tok[1]],
                      inc=ssem, n=16)
                    ct_tok = tok(ssem)
                    mm_w = [ct_tok, ehw_toks[hs][0]]
                    if hs == 0:
                        mm_w += [tok("vE"), ehw_toks[NHS - 1][1], ehw_toks[NHS - 2][1],
                                 wbuf_tok]
                    for ds in range(8):
                        P("tensor", lambda e, ds=ds, hs=hs, nn=nn, eidx=eidx: e.matmul(
                            PS[:, ds * 512:(ds + 1) * 512],
                            e2b[hs % 2][:, ds * 128:(ds + 1) * 128],
                            ehw[:, hs * 1024 + nn * 512:hs * 1024 + (nn + 1) * 512],
                            start=(hs == 0),
                            stop=(hs == NHS - 1 and eidx != NE - 1)),
                          waits=mm_w if ds == 0 else (),
                          inc="tE" if ds == 7 else None)
                    e2slot_free[hs % 2] = tok("tE")
                if eidx == NE - 1:
                    for ds in range(8):
                        P("tensor", lambda e, ds=ds, nn=nn, tc0=tc0: e.matmul(
                            PS[:, ds * 512:(ds + 1) * 512],
                            eb2sb[0:8, ds * 128:(ds + 1) * 128],
                            w_fm[0:8, tc0 + nn * 512:tc0 + (nn + 1) * 512],
                            start=False, stop=True),
                          waits=[eb2_tok, gates_tok] if ds == 0 else (),
                          inc="tE" if ds == 7 else None)
                c2_tok = tok("tE")
                for ds in range(8):
                    P("vector", lambda e, ds=ds, nn=nn: TT(
                        e, yacc[:, ds * 1024 + nn * 512:ds * 1024 + (nn + 1) * 512],
                        PS[:, ds * 512:(ds + 1) * 512],
                        yacc[:, ds * 1024 + nn * 512:ds * 1024 + (nn + 1) * 512],
                        ALU.add),
                      waits=[c2_tok] if ds == 0 else (), inc="vE")
                    yev_last[(nn, ds)] = tok("vE")
        if tf == 0:
            for ds in range(8):
                P("sync", lambda e, ds=ds: e.dma_start(
                    out=dyd[ds * 128:(ds + 1) * 128, :],
                    in_=yacc[:, ds * 1024:(ds + 1) * 1024]),
                  waits=[yev_last[(0, ds)], yev_last[(1, ds)]], inc="dO", n=16)
            spill_tok = ("dO", pb.cnt["dO"])

    yfin = dict(yev_last)
    if debug:
        for ds in range(8):
            P("sync", lambda e, ds=ds: e.dma_start(
                out=dbg["y"][ds * 128:(ds + 1) * 128, TCF:2 * TCF],
                in_=yacc[:, ds * 1024:(ds + 1) * 1024]),
              waits=[yfin[(0, ds)], yfin[(1, ds)], spill_tok], inc="dS", n=16)
            P("sync", lambda e, ds=ds: e.dma_start(
                out=dbg["y"][ds * 128:(ds + 1) * 128, 0:TCF],
                in_=dyd[ds * 128:(ds + 1) * 128, :]), inc="dS", n=16)
        ydump = tok("dS")
        P("vector", lambda e: e.nop(), waits=[ydump])

    d_mm_last = None
    for hf in range(2):
        for ct in range(NCT):
            if hf == 0:
                fx = [spill_tok] if ct == 0 else []
                i, buf, off = load1024(dyd[ct * 128:(ct + 1) * 128, :],
                                       free_extra=fx, sem="dS")
                srcf = lambda s2, buf=buf, off=off: buf[:, off + s2 * 512:off + (s2 + 1) * 512]
                ld = [hfill[i]]
            else:
                srcf = lambda s2, ct=ct: yacc[:, ct * 1024 + s2 * 512:ct * 1024 + (s2 + 1) * 512]
                ld = [yfin[(0, ct)], yfin[(1, ct)]]
            for s2 in range(2):
                pw = [tok("vE")] if d_mm_last is None else [d_mm_last]
                P("gpsimd", lambda e, srcf=srcf, s2=s2: e.tensor_mul(
                    scrX[:], srcf(s2), srcf(s2)), waits=ld + pw, inc="pE")
                sqt = tok("pE")
                bidx = hf * 2 + s2
                P("tensor", lambda e, srcf=srcf, s2=s2, bidx=bidx, ct=ct: e.matmul(
                    PS[0:1, bidx * 512:(bidx + 1) * 512], ones_s[:], srcf(s2),
                    start=(ct == 0), stop=(ct == 7)),
                  waits=ld + ([tok("vE"), e2slot_free[0], e2slot_free[1]]
                              if hf == 0 and ct == 0 and s2 == 0 else []))
                P("tensor", lambda e, bidx=bidx, ct=ct: e.matmul(
                    PS[0:1, 2048 + bidx * 512:2048 + (bidx + 1) * 512], ones_s[:],
                    scrX[:], start=(ct == 0), stop=(ct == 7)),
                  waits=[sqt], inc="tE")
                d_mm_last = tok("tE")
            if hf == 0:
                free_half(i, tok("tE"))
    d_mm = tok("tE")
    for s in range(4):
        P("vector", lambda e, s=s: e.tensor_copy(
            rowsum[0:1, s * 512:(s + 1) * 512], PS[0:1, s * 512:(s + 1) * 512]),
          waits=[d_mm] if s == 0 else ())
        P("vector", lambda e, s=s: e.tensor_copy(
            rowsq[0:1, s * 512:(s + 1) * 512],
            PS[0:1, 2048 + s * 512:2048 + (s + 1) * 512]),
          inc="vE" if s == 3 else None)
    d_ev = tok("vE")
    ln3_tok, _ = ln_rows(rowsum[0:1, 0:T], rowsq[0:1, 0:T], rstdR[0:1, 0:T],
                         mrsR[0:1, 0:T], rowsum[0:1, 0:T], rowsq[0:1, 0:T],
                         [d_ev])
    P("sync", lambda e: e.dma_start(out=drB.ap()[0:1, :], in_=rstdR[0:1, 0:T]),
      waits=[ln3_tok, ("dV0", pb.cnt["dV0"]), ("dV1", pb.cnt["dV1"])], inc="dR", n=16)
    P("sync", lambda e: e.dma_start(out=drB.ap()[1:2, :], in_=mrsR[0:1, 0:T]),
      inc="dR", n=16)
    rows3_tok = tok("dR")


    for ct in range(NCT):
        for hf in range(2):
            i1, b1, o1 = load1024(dx1[ct * 128:(ct + 1) * 128,
                                      hf * 1024:(hf + 1) * 1024], sem="dS")
            ldx = hfill[i1]
            if hf == 0:
                i2, b2, o2 = load1024(dyd[ct * 128:(ct + 1) * 128, :], sem="dS")
                ysrc = lambda s2, b2=b2, o2=o2: b2[:, o2 + s2 * 512:o2 + (s2 + 1) * 512]
                ldy = [hfill[i2]]
            else:
                ysrc = lambda s2, ct=ct: yacc[:, ct * 1024 + s2 * 512:ct * 1024 + (s2 + 1) * 512]
                ldy = []
            last_ot = None
            for s2 in range(2):
                sidx = hf * 2 + s2
                p, bct = bc_load(sidx * 512, 512, rows3_tok)
                ob = outbuf[:, ct * 2048 + hf * 1024 + s2 * 512:
                            ct * 2048 + hf * 1024 + (s2 + 1) * 512]
                P("vector", lambda e, ysrc=ysrc, s2=s2, p=p: TT(
                    e, scrX[:], ysrc(s2), bcF[2 * p][:, 0:512], ALU.mult),
                  waits=[bct, ldx] + ldy)
                P("vector", lambda e, p=p: TT(e, scrX[:], scrX[:],
                                              bcF[2 * p + 1][:, 0:512],
                                              ALU.subtract))
                P("vector", lambda e, ob=ob, s2=s2, b1=b1, o1=o1: TT(
                    e, ob, scrX[:], b1[:, o1 + s2 * 512:o1 + (s2 + 1) * 512],
                    ALU.add), inc="vE")
                ot = tok("vE")
                bc_free2[p] = ot
                last_ot = ot
                P("sync", lambda e, ob=ob, s2=s2, ct=ct, hf=hf: e.dma_start(
                    out=dout[ct * 128:(ct + 1) * 128,
                             hf * 1024 + s2 * 512:hf * 1024 + (s2 + 1) * 512],
                    in_=ob), waits=[ot], inc="dO", n=16)
            free_half(i1, last_ot)
            if hf == 0:
                free_half(i2, last_ot)

    final_dO = tok("dO")
    P("sync", lambda e: e.nop(), waits=[final_dO])

    pb.emit_all()
    return nc


def kernel(**inputs):
    x = np.asarray(inputs["L0_attn_x"], np.float32)
    Wk = np.asarray(inputs["L0_attn_Wk"], np.float32)
    Wv = np.asarray(inputs["L0_attn_Wv"], np.float32)
    Wr = np.asarray(inputs["L0_attn_Wr"], np.float32)
    Wo = np.asarray(inputs["L0_attn_Wo"], np.float32)
    vec6 = np.concatenate([
        np.asarray(inputs["L0_attn_mix_k"], np.float32).reshape(8, 128),
        np.asarray(inputs["L0_attn_mix_v"], np.float32).reshape(8, 128),
        np.asarray(inputs["L0_attn_mix_r"], np.float32).reshape(8, 128),
        np.asarray(inputs["L0_ffn_rb1"], np.float32).reshape(8, 128),
        np.asarray(inputs["L0_attn_w"], np.float32).reshape(8, 128),
        np.asarray(inputs["L0_attn_u"], np.float32).reshape(8, 128),
    ], axis=0)
    common = dict(
        WkT=np.ascontiguousarray(Wk.T),
        WvT=np.ascontiguousarray(Wv.T),
        WrT=np.ascontiguousarray(Wr.T),
        WoT=np.ascontiguousarray(Wo.T),
        rw1=np.ascontiguousarray(np.asarray(inputs["L0_ffn_rw1"], np.float32)),
        rw2=np.ascontiguousarray(np.asarray(inputs["L0_ffn_rw2"], np.float32)),
        vec6=np.ascontiguousarray(vec6),
        eb1=np.ascontiguousarray(np.asarray(inputs["L0_ffn_eb1"], np.float32)),
        eb2=np.ascontiguousarray(np.asarray(inputs["L0_ffn_eb2"], np.float32)),
        rb2=np.ascontiguousarray(np.asarray(inputs["L0_ffn_rb2"], np.float32).reshape(8, 1)),
        ew1=np.ascontiguousarray(np.asarray(inputs["L0_ffn_ew1"], np.float32).reshape(NE * C, HD)),
        ew2=np.ascontiguousarray(np.asarray(inputs["L0_ffn_ew2"], np.float32).reshape(NE * HD, C)),
        eye=np.eye(128, dtype=np.float32),
    )
    nc = build_program(debug=False)
    in_maps = []
    for b in range(8):
        m = dict(common)
        m["x"] = np.ascontiguousarray(x[b].T)
        in_maps.append(m)
    res = run_bass_kernel_spmd(nc, in_maps, core_ids=list(range(8)))
    out = np.stack([np.asarray(r["out"], np.float32).T for r in res.results], axis=0)
    return np.ascontiguousarray(out)


# version 14
# speedup vs baseline: 1.3006x; 1.2514x over previous
import contextlib
import numpy as np
import concourse.bass as bass
import concourse.mybir as mybir
from concourse.bass_utils import run_bass_kernel_spmd

F32 = mybir.dt.float32
BF16 = mybir.dt.bfloat16
AF = mybir.ActivationFunctionType
ALU = mybir.AluOpType
AX = mybir.AxisListType.X

C = 1024
T = 2048
NCT = 8
TCA = 256
NCH = T // TCA
TCF = 1024
NE = 8
HD = 2048
NHS = 16
EPS = 1e-5
NEG = -1e30

ENGINES = ("sync", "tensor", "vector", "scalar", "gpsimd")


class PB:
    def __init__(self, nc):
        self.nc = nc
        self.steps = {e: [] for e in ENGINES}
        self.sems = {}
        self.cnt = {}
        self._es = contextlib.ExitStack()

    def sem(self, name):
        self.sems[name] = self._es.enter_context(self.nc.semaphore(name))
        self.cnt[name] = 0

    def plan(self, eng, emit, waits=(), inc=None, n=1, fast=False):
        incs = ()
        if inc is not None:
            self.cnt[inc] += n
            incs = ((inc, n),)
        for s, v in waits:
            assert v <= self.cnt[s], f"future wait {s}:{v}>{self.cnt[s]}"
        ws = tuple((s, v) for (s, v) in waits if v > 0)
        self.steps[eng].append((ws, emit, incs, fast))

    def tok(self, s):
        return (s, self.cnt[s])

    def emit_all(self):
        with self.nc.Block() as block:
            for e in ENGINES:
                steps = self.steps[e]
                sems = self.sems
                drained = e in ("vector", "scalar", "gpsimd")

                def body(eng, steps=steps, drained=drained):
                    last = {}
                    for waits, emit, incs, fast in steps:
                        for s, v in waits:
                            if last.get(s, 0) < v:
                                eng.wait_ge(sems[s], v)
                                last[s] = v
                        instr = emit(eng)
                        dma_incs = [(s, nn) for s, nn in incs if nn >= 16]
                        eng_incs = [(s, nn) for s, nn in incs if nn < 16]
                        for s, nn in dma_incs:
                            instr.then_inc(sems[s], nn)
                        if drained and not dma_incs and not fast:
                            instr = eng.drain()
                        for s, nn in eng_incs:
                            instr.then_inc(sems[s], nn)
                getattr(block, e)(body)
        self._es.close()


class SbufMap:
    def __init__(self, nc):
        self.nc = nc
        base = nc.SBUF_PARTITION_SIZE_BYTES - nc.sbuf_bytes_remaining
        self.base = (base + 127) // 128 * 128
        self.cap = nc.SBUF_PARTITION_SIZE_BYTES - self.base - 512
        self.cursor = 0
        self.marks = {}
        self.offs = {}

    def alloc(self, name, shape, dtype, align=64):
        itm = 2 if dtype == BF16 else 4
        self.cursor = (self.cursor + align - 1) // align * align
        off = self.cursor
        self.cursor += shape[-1] * itm
        assert self.cursor <= self.cap, (name, self.cursor, self.cap)
        self.offs[name] = off
        return self.nc.alloc_sbuf_tensor_at(name, shape, dtype,
                                            offset=self.base + off)

    def at(self, name, shape, dtype, off):
        itm = 2 if dtype == BF16 else 4
        assert off + shape[-1] * itm <= self.cap, name
        self.offs[name] = off
        return self.nc.alloc_sbuf_tensor_at(name, shape, dtype,
                                            offset=self.base + off)

    def mark(self, name):
        self.marks[name] = self.cursor

    def reset_to(self, name):
        self.cursor = self.marks[name]


def build_program(debug=False):
    nc = bass.Bass("TRN2", target_bir_lowering=False)

    dx = nc.dram_tensor("x", [C, T], F32, kind="ExternalInput").ap()
    dW = {}
    for nm in ("WkT", "WvT", "WrT", "WoT", "rw1"):
        dW[nm] = nc.dram_tensor(nm, [C, C], F32, kind="ExternalInput").ap()
    drw2 = nc.dram_tensor("rw2", [C, NE], F32, kind="ExternalInput").ap()
    dvec = nc.dram_tensor("vec6", [48, 128], F32, kind="ExternalInput").ap()
    deb1 = nc.dram_tensor("eb1", [NE, HD], F32, kind="ExternalInput").ap()
    deb2 = nc.dram_tensor("eb2", [NE, C], F32, kind="ExternalInput").ap()
    drb2 = nc.dram_tensor("rb2", [NE, 1], F32, kind="ExternalInput").ap()
    dew1 = nc.dram_tensor("ew1", [NE * C, HD], F32, kind="ExternalInput").ap()
    dew2 = nc.dram_tensor("ew2", [NE * HD, C], F32, kind="ExternalInput").ap()
    deye = nc.dram_tensor("eye", [128, 128], F32, kind="ExternalInput").ap()

    dout = nc.dram_tensor("out", [C, T], F32, kind="ExternalOutput").ap()
    dx1 = nc.dram_tensor("x1d", [C, T], F32, kind="Internal").ap()
    dyd = nc.dram_tensor("yd", [C, TCF], F32, kind="Internal").ap()
    dwfm = nc.dram_tensor("wfm_d", [NE, T], BF16, kind="Internal")
    drB = nc.dram_tensor("rows_d", [2, T], F32, kind="Internal")
    dbg = {}
    if debug:
        for nm, shp in (("x1", [C, T]), ("logits", [NE, T]), ("w", [NE, T]),
                        ("k", [C, T]), ("wkv", [C, T]), ("carry", [128, 256]),
                        ("stats", [2, T])):
            dbg[nm] = nc.dram_tensor("dbg_" + nm, shp, F32,
                                     kind="ExternalOutput").ap()
        dbg["t"] = nc.dram_tensor("dbg_t", [C, T], BF16, kind="ExternalOutput").ap()
        dbg["y"] = nc.dram_tensor("dbg_y", [C, T], F32, kind="ExternalOutput").ap()
        dbg["ehw"] = nc.dram_tensor("dbg_ehw", [HD, TCF], BF16, kind="ExternalOutput").ap()
        dbg["rh"] = nc.dram_tensor("dbg_rh", [128, 4096], BF16, kind="ExternalOutput").ap()
        dbg["dcb"] = nc.dram_tensor("dbg_dcb", [128, 2080], F32, kind="ExternalOutput").ap()

    sm = SbufMap(nc)
    A = sm.alloc
    eyeS = A("eyeS", [128, 128], F32)
    onesK = A("onesK", [1, 128], F32)
    ones_s = A("ones_s", [128, 1], F32)
    vecT = A("vecT", [128, 48], F32)
    dcoefT = A("dcoefT", [128, 8], F32)
    euT = A("euT", [128, 8], F32)
    ebT = A("ebT", [128, 128], F32)
    rb2c = A("rb2c", [NE, 1], F32)
    Pc8 = A("Pc8", [128, 8], F32)
    Qc8 = A("Qc8", [128, 8], F32)
    M8 = A("M8", [128, 8], F32)
    kmax8 = A("kmax8", [128, 8], F32)
    dMs = A("dMs", [128, 8], F32)
    sc8 = A("sc8", [128, 8], F32)
    negM8 = A("negM8", [128, 8], F32)
    rowsum = A("rowsum", [1, T], F32)
    rowsq = A("rowsq", [1, T], F32)
    rstdR = A("rstdR", [1, T], F32)
    mrsR = A("mrsR", [1, T], F32)
    stgA0 = A("stgA0", [128, 2048], F32)
    stgA1 = A("stgA1", [128, 2048], F32)
    junk = A("junk", [128, 16], F32)
    sm.mark("overlay")

    Wbk = A("Wbk", [128, 8192], BF16)
    Wbv = A("Wbv", [128, 8192], BF16)
    Wbr = A("Wbr", [128, 8192], BF16)
    Wbo = A("Wbo", [128, 8192], BF16)
    xsb = [A("xs0", [128, 8 * 257], F32), A("xs1", [128, 8 * 257], F32)]
    xn = A("xn", [128, 8 * 257], F32)
    xkb = A("xkb", [128, 2048], BF16)
    xvb = A("xvb", [128, 2048], BF16)
    xrb = A("xrb", [128, 2048], BF16)
    kf = A("kf", [128, 2048], F32)
    vf = A("vf", [128, 2048], F32)
    erf = A("erf", [128, 2048], F32)
    dd = A("dd", [128, 256], F32)
    Escr = A("Escr", [128, 256], F32)
    Gscr = A("Gscr", [128, 256], F32)
    Nscr = A("Nscr", [128, 256], F32)
    Dscr = A("Dscr", [128, 256], F32)
    Pbuf = A("Pbuf", [128, 256], F32)
    Qbuf = A("Qbuf", [128, 256], F32)
    sqf = A("sqf", [128, 512], F32)
    mean2R = A("mean2R", [1, TCA], F32)
    scr2 = A("scr2", [1, TCA], F32)
    rstd2R = A("rstd2R", [1, TCA], F32)
    mrs2R = A("mrs2R", [1, TCA], F32)
    aofR = A("aofR", [128, 2048], F32)
    xbc0 = A("xbc0", [128, 264], F32)
    xbc1 = A("xbc1", [128, 264], F32)
    rwkvbR = A("rwkvbR", [128, 2048], BF16)
    attn_end = sm.cursor
    sm.reset_to("overlay")

    yacc = A("yacc", [128, 8 * TCF], F32)
    t_bf = A("t_bf", [128, 8 * T], BF16)
    e1b = A("e1b", [128, 8 * 2048], BF16)
    ehw = A("ehw", [128, NHS * 1024], BF16)
    e2b = [A("e2b0", [128, 1024], BF16), A("e2b1", [128, 1024], BF16)]
    rw2b = A("rw2b", [128, 64], BF16)
    wbuf = A("wbuf", [128, TCF], BF16)
    scrX = A("scrX", [128, 512], F32)
    eb2sb = A("eb2sb", [NE, C], F32)
    w_fm = A("w_fm", [NE, T], F32)
    lgT = A("lgT", [128, 8], F32)
    wTj = A("wTj", [128, 8], F32)
    rw2stg = A("rw2stg", [128, 64], F32)
    bcF = [A("bcF0", [128, 512], F32), A("bcF1", [128, 512], F32),
           A("bcF2", [128, 512], F32), A("bcF3", [128, 512], F32)]
    ffn_end = sm.cursor
    assert max(attn_end, ffn_end) <= sm.cap, (attn_end, ffn_end, sm.cap)

    rhb = sm.at("rhb", [128, 4096], BF16, sm.offs["ehw"])
    wfmb = sm.at("wfmb", [NE, T], BF16, sm.offs["ehw"])
    logits_sb = sm.at("logits_sb", [NE, T], F32, sm.offs["ehw"] + 8192)
    rw1b = sm.at("rw1b", [128, 8192], BF16, sm.offs["ehw"] + 16384)
    aof = sm.at("aof", [128, 2048], F32, sm.offs["aofR"])
    rwkvb = sm.at("rwkvb", [128, 2048], BF16, sm.offs["rwkvbR"])
    vstg = sm.at("vstg", [8, 768], F32, sm.offs["vf"])
    outbuf = sm.at("outbuf", [128, 16384], F32, sm.offs["e1b"])
    ebstg = sm.at("ebstg", [8, 2048], F32, sm.offs["kf"])

    PS = nc.alloc_psum_tensor("PS", [128, 4096], F32)

    pb = PB(nc)
    for s in ("dS", "dW", "dO", "dG", "dF", "dB", "dR", "dV0", "dV1", "dX",
              "tE", "vE", "sE", "pE", "x0", "x1s", "sl0", "sl1", "sl2", "sl3"):
        pb.sem(s)
    P = pb.plan
    tok = pb.tok

    def TT(e, out, a, b, op):
        return e.tensor_tensor(out=out, in0=a, in1=b, op=op)

    def TS(e, out, a, s1, op):
        return e.tensor_scalar(out=out, in0=a, scalar1=s1, scalar2=None, op0=op)

    def vsp(n=2, inc=None):
        for q in range(n):
            P("vector", lambda e: e.tensor_scalar(
                out=junk[:], in0=junk[:], scalar1=0.0, scalar2=None, op0=ALU.mult),
              inc=inc if q == n - 1 else None)

    P("sync", lambda e: e.dma_start(out=eyeS[:], in_=deye), inc="dW", n=16)
    for g in range(6):
        P("sync", lambda e, g=g: e.dma_start(
            out=vstg[:, g * 128:(g + 1) * 128], in_=dvec[g * 8:(g + 1) * 8, :]),
          inc="dW", n=16)
    P("sync", lambda e: e.dma_start(out=ebstg[:], in_=deb1), inc="dW", n=16)
    P("sync", lambda e: e.dma_start(out=rb2c[:], in_=drb2), inc="dW", n=16)
    w_const = tok("dW")

    P("vector", lambda e: e.memset(junk[:], 0.0))
    P("vector", lambda e: e.memset(onesK[:], 1.0))
    P("vector", lambda e: e.memset(ones_s[:], 1.0))
    P("vector", lambda e: e.memset(M8[:], NEG))
    P("vector", lambda e: e.memset(Pc8[:], 0.0))
    P("vector", lambda e: e.memset(Qc8[:], 0.0), inc="vE")
    init_tok = tok("vE")

    for j in range(16):
        P("tensor", lambda e, j=j: e.transpose(
            PS[:, 8 * j:8 * j + 8], ebstg[0:8, j * 128:(j + 1) * 128],
            eyeS[0:8, 0:8]), waits=[w_const] if j == 0 else ())
    for g in range(6):
        P("tensor", lambda e, g=g: e.transpose(
            PS[:, 128 + 8 * g:136 + 8 * g], vstg[0:8, g * 128:(g + 1) * 128],
            eyeS[0:8, 0:8]), inc="tE" if g == 5 else None)
    tr_tok = tok("tE")
    P("scalar", lambda e: e.copy(ebT[:], PS[:, 0:128]), waits=[tr_tok])
    P("scalar", lambda e: e.copy(vecT[:], PS[:, 128:176]), inc="sE")
    vecT_tok = tok("sE")
    P("scalar", lambda e: e.activation(out=dcoefT[:], in_=vecT[:, 32:40], func=AF.Exp))
    P("scalar", lambda e: e.activation(out=dcoefT[:], in_=dcoefT[:], func=AF.Exp, scale=-1.0))
    P("scalar", lambda e: e.activation(out=euT[:], in_=vecT[:, 40:48], func=AF.Exp))
    P("scalar", lambda e: e.copy(dMs[:], dcoefT[:]))
    P("scalar", lambda e: e.copy(sc8[:], euT[:]), inc="sE")
    dco_tok = tok("sE")

    half = [(stgA0, 0), (stgA0, 1024), (stgA1, 0), (stgA1, 1024)]
    hfree = [init_tok] * 4
    hfill = [("dW", 0)] * 4
    hrr = [0]
    prr = [0]

    def load1024(src, free_extra=(), sem=None):
        i = hrr[0] % 4
        hrr[0] += 1
        buf, off = half[i]
        ssem = f"sl{i}"
        P("sync", lambda e, buf=buf, off=off, src=src: e.dma_start(
            out=buf[:, off:off + 1024], in_=src),
          waits=[hfree[i]] + list(free_extra), inc=ssem, n=16)
        hfill[i] = tok(ssem)
        return i, buf, off

    def load512(src, free_extra=(), sem=None):
        i = hrr[0] % 4
        hrr[0] += 1
        buf, off = half[i]
        ssem = f"sl{i}"
        h = hfree[i] if isinstance(hfree[i], list) else [hfree[i]]
        P("sync", lambda e, buf=buf, off=off, src=src: e.dma_start(
            out=buf[:, off:off + 512], in_=src),
          waits=h + list(free_extra), inc=ssem, n=16)
        hfill[i] = tok(ssem)
        return i, buf, off

    def load2048(src, free_extra=(), sem=None):
        i = 0 if (prr[0] % 2 == 0) else 2
        prr[0] += 1
        buf = half[i][0]
        ssem = f"sl{i}"
        P("sync", lambda e, buf=buf, src=src: e.dma_start(
            out=buf[:, 0:2048], in_=src),
          waits=[hfree[i], hfree[i + 1]] + list(free_extra), inc=ssem, n=16)
        hfill[i] = hfill[i + 1] = tok(ssem)
        return i, buf

    def free_half(i, t):
        hfree[i] = t

    def free_pair(i, t):
        hfree[i] = t
        hfree[i + 1] = t

    wbt = {}
    for nm, dst in (("WkT", Wbk), ("WvT", Wbv), ("WrT", Wbr), ("WoT", Wbo)):
        for ct in range(NCT):
            i, buf, off = load1024(dW[nm][ct * 128:(ct + 1) * 128, :])
            P("scalar", lambda e, buf=buf, off=off, dst=dst, ct=ct: e.copy(
                dst[:, ct * 1024:(ct + 1) * 1024], buf[:, off:off + 1024]),
              waits=[hfill[i]], inc="sE", fast=(ct != 7))
            free_half(i, tok("sE"))
        wbt[nm] = tok("sE")

    ph0_mm_tok = {}
    for ct in range(NCT):
        xs = xsb[ct % 2]
        xw = [ph0_mm_tok[ct - 2]] if ct - 2 in ph0_mm_tok else [init_tok]
        P("sync", lambda e, xs=xs, ct=ct: e.dma_start(
            out=xs[:, 0:2048], in_=dx[ct * 128:(ct + 1) * 128, :]),
          waits=xw, inc="x0" if ct % 2 == 0 else "x1s", n=16)
        ld = tok("x0" if ct % 2 == 0 else "x1s")
        P("gpsimd", lambda e, xs=xs: e.tensor_mul(
            xn[:, 0:2048], xs[:, 0:2048], xs[:, 0:2048]),
          waits=[ld] + ([ph0_mm_tok[ct - 1]] if ct - 1 in ph0_mm_tok else []),
          inc="pE")
        sq = tok("pE")
        for s in range(4):
            P("tensor", lambda e, xs=xs, s=s, ct=ct: e.matmul(
                PS[0:1, s * 512:(s + 1) * 512], ones_s[:],
                xs[:, s * 512:(s + 1) * 512], start=(ct == 0), stop=(ct == 7)),
              waits=[ld, vecT_tok] if s == 0 else ())
        for s in range(4):
            P("tensor", lambda e, s=s, ct=ct: e.matmul(
                PS[0:1, 2048 + s * 512:2048 + (s + 1) * 512], ones_s[:],
                xn[:, s * 512:(s + 1) * 512], start=(ct == 0), stop=(ct == 7)),
              waits=[sq] if s == 0 else (), inc="tE" if s == 3 else None)
        ph0_mm_tok[ct] = tok("tE")
    ph0_mm = tok("tE")

    def ln_rows(sum_ap, sq_ap, rstd_ap, mrs_ap, mean_ap, scr_ap, waits_v):
        vsp(2)
        P("vector", lambda e: TS(e, mean_ap, sum_ap, 1.0 / C, ALU.mult), waits=waits_v)
        P("vector", lambda e: TS(e, scr_ap, sq_ap, 1.0 / C, ALU.mult))
        vsp(1)
        P("vector", lambda e: TT(e, rstd_ap, mean_ap, mean_ap, ALU.mult))
        P("vector", lambda e: TS(e, scr_ap, scr_ap, EPS, ALU.add))
        vsp(1)
        P("vector", lambda e: TT(e, scr_ap, scr_ap, rstd_ap, ALU.subtract))
        vsp(2, inc="vE")
        tv = tok("vE")
        P("scalar", lambda e: e.activation(out=rstd_ap, in_=scr_ap, func=AF.Ln),
          waits=[tv])
        P("scalar", lambda e: e.activation(out=rstd_ap, in_=rstd_ap, func=AF.Exp, scale=-0.5))
        P("scalar", lambda e: e.copy(junk[:], junk[:]))
        P("scalar", lambda e: e.copy(junk[:], junk[:]), inc="sE")
        ts_ = tok("sE")
        P("vector", lambda e: TT(e, mrs_ap, mean_ap, rstd_ap, ALU.mult), waits=[ts_])
        vsp(2, inc="vE")
        return tok("vE"), tv

    for s in range(4):
        P("vector", lambda e, s=s: e.tensor_copy(
            rowsum[0:1, s * 512:(s + 1) * 512], PS[0:1, s * 512:(s + 1) * 512]),
          waits=[ph0_mm] if s == 0 else ())
        P("vector", lambda e, s=s: e.tensor_copy(
            rowsq[0:1, s * 512:(s + 1) * 512],
            PS[0:1, 2048 + s * 512:2048 + (s + 1) * 512]),
          inc="vE" if s == 3 else None)
    ph0_ev = tok("vE")
    ln1_tok, _ = ln_rows(rowsum[0:1, 0:T], rowsq[0:1, 0:T], rstdR[0:1, 0:T],
                         mrsR[0:1, 0:T], rowsum[0:1, 0:T], rowsq[0:1, 0:T],
                         [ph0_ev])
    P("sync", lambda e: e.dma_start(out=drB.ap()[0:1, :], in_=rstdR[0:1, 0:T]),
      waits=[ln1_tok], inc="dR", n=16)
    P("sync", lambda e: e.dma_start(out=drB.ap()[1:2, :], in_=mrsR[0:1, 0:T]),
      inc="dR", n=16)
    rows1_tok = tok("dR")

    pp = [0, 512, 1024, 1536]
    pp_free = [ph0_ev] * 4
    pp_rr = [0]
    bc = [3072, 3584]
    bc_free = [ph0_ev, ph0_ev]
    stA_free = ph0_ev
    stB_free = ph0_ev
    sq_free = [ph0_ev, ph0_ev]
    tok_x1ops = {}
    tok_x1dma = {}
    tok_statB = {}
    tok_xn = {}
    st = {}

    def a_loads(tau):
        s = st[tau] = {}
        s["xs"] = xsb[tau % 2]
        s["c0"] = tau * TCA
        xs, c0 = s["xs"], s["c0"]
        xw = [tok_x1ops[tau - 2]] if tau - 2 in tok_x1ops else [ph0_ev]
        xsem = "x0" if tau % 2 == 0 else "x1s"
        for ct in range(NCT):
            if tau == 0:
                P("sync", lambda e, ct=ct, xs=xs: e.dma_start(
                    out=xs[:, ct * 257 + 1:ct * 257 + 257],
                    in_=dx[ct * 128:(ct + 1) * 128, 0:256]),
                  waits=xw, inc=xsem, n=16)
            else:
                P("sync", lambda e, ct=ct, xs=xs, c0=c0: e.dma_start(
                    out=xs[:, ct * 257:ct * 257 + 257],
                    in_=dx[ct * 128:(ct + 1) * 128, c0 - 1:c0 + 256]),
                  waits=xw, inc=xsem, n=16)
        s["xs_tok"] = tok(xsem)
        lo = max(c0 - 1, 0)
        o0h = 1 if tau == 0 else 0
        xbw = [rows1_tok]
        if tau - 1 in tok_xn:
            xbw.append(tok_xn[tau - 1])
        P("sync", lambda e, lo=lo, o0h=o0h, c0=c0: e.dma_start(
            out=xbc0[:, o0h:257],
            in_=drB.ap()[0:1, lo:c0 + 256].broadcast_to((128, 257 - o0h))),
          waits=xbw, inc="dX", n=16)
        P("sync", lambda e, lo=lo, o0h=o0h, c0=c0: e.dma_start(
            out=xbc1[:, o0h:257],
            in_=drB.ap()[1:2, lo:c0 + 256].broadcast_to((128, 257 - o0h))),
          inc="dX", n=16)
        s["bcast_tok"] = tok("dX")

    def a_xnmix(tau):
        s = st[tau]
        xs = s["xs"]
        o0 = 1 if tau == 0 else 0
        for ct in range(NCT):
            P("vector", lambda e, ct=ct, xs=xs, o0=o0: TT(
                e, xn[:, ct * 257 + o0:(ct + 1) * 257],
                xs[:, ct * 257 + o0:(ct + 1) * 257],
                xbc0[:, o0:257], ALU.mult),
              waits=[s["xs_tok"], s["bcast_tok"]] if ct == 0 else ())
            last = (ct == 7) and (tau != 0)
            P("vector", lambda e, ct=ct, o0=o0: TT(
                e, xn[:, ct * 257 + o0:(ct + 1) * 257],
                xn[:, ct * 257 + o0:(ct + 1) * 257],
                xbc1[:, o0:257], ALU.subtract),
              inc="vE" if last else None)
        if tau == 0:
            for ct in range(NCT):
                P("vector", lambda e, ct=ct: e.memset(xn[:, ct * 257:ct * 257 + 1], 0.0),
                  inc="vE" if ct == 7 else None)
        tok_xn[tau] = tok("vE")
        mix_w = [st[tau - 1]["projr_tok"]] if tau - 1 in st else []
        for ct in range(NCT):
            cur = xn[:, ct * 257 + 1:ct * 257 + 257]
            prv = xn[:, ct * 257:ct * 257 + 256]
            P("vector", lambda e, cur=cur, prv=prv: TT(e, dd[:], cur, prv, ALU.subtract),
              waits=mix_w if ct == 0 else ())
            for dst, col in ((xkb, 0), (xvb, 8), (xrb, 16)):
                lastop = (ct == 7 and col == 16)
                P("vector", lambda e, dst=dst, col=col, ct=ct, prv=prv:
                  e.scalar_tensor_tensor(
                      out=dst[:, ct * 256:(ct + 1) * 256], in0=dd[:],
                      scalar=vecT[:, col + ct:col + ct + 1], in1=prv,
                      op0=ALU.mult, op1=ALU.add),
                  inc="vE" if lastop else None)
        s["mix_tok"] = tok("vE")

    def a_proj(tau):
        s = st[tau]
        rw_prev = st[tau - 1]["rw_toks"] if tau - 1 in st else {}
        ev_toks = {}
        for pi, nm in enumerate(("k", "v", "r")):
            Wb = (Wbk, Wbv, Wbr)[pi]
            srcb = (xkb, xvb, xrb)[pi]
            wkey = ("WkT", "WvT", "WrT")[pi]
            dstb = (kf, vf, None)[pi]
            for dt in range(NCT):
                sli = pp_rr[0] % 4
                sl = pp[sli]
                fw = pp_free[sli]
                pp_rr[0] += 1
                for ctk in range(NCT):
                    P("tensor", lambda e, sl=sl, Wb=Wb, ctk=ctk, dt=dt, srcb=srcb: e.matmul(
                        PS[:, sl:sl + 256],
                        Wb[:, ctk * 1024 + dt * 128:ctk * 1024 + dt * 128 + 128],
                        srcb[:, ctk * 256:(ctk + 1) * 256],
                        start=(ctk == 0), stop=(ctk == 7)),
                      waits=[s["mix_tok"], fw, wbt[wkey]] if ctk == 0 else (),
                      inc="tE" if ctk == 7 else None)
                tk = tok("tE")
                guard = [rw_prev[dt]] if dt in rw_prev else []
                if nm == "r":
                    P("scalar", lambda e, sl=sl, dt=dt: e.activation(
                        out=erf[:, dt * 256:(dt + 1) * 256], in_=PS[:, sl:sl + 256],
                        func=AF.Exp, scale=-1.0), waits=[tk] + guard, inc="sE")
                else:
                    P("scalar", lambda e, sl=sl, dstb=dstb, dt=dt: e.copy(
                        dstb[:, dt * 256:(dt + 1) * 256], PS[:, sl:sl + 256]),
                      waits=[tk] + guard, inc="sE")
                ev_toks[(nm, dt)] = tok("sE")
                pp_free[sli] = tok("sE")
        s["ev_toks"] = ev_toks
        s["projr_tok"] = tok("tE")
        if debug:
            for dt in range(NCT):
                P("sync", lambda e, dt=dt, c0=s["c0"]: e.dma_start(
                    out=dbg["k"][dt * 128:(dt + 1) * 128, c0:c0 + 256],
                    in_=kf[:, dt * 256:(dt + 1) * 256]),
                  waits=[ev_toks[("k", dt)]], inc="dS", n=16)
            s["kdump_tok"] = tok("dS")

    def a_scanprep(tau):
        s = st[tau]
        ev_toks = s["ev_toks"]
        for dt in range(NCT):
            P("vector", lambda e, dt=dt: e.reduce_max(
                out=kmax8[:, dt:dt + 1], in_=kf[:, dt * 256:(dt + 1) * 256], axis=AX),
              waits=[ev_toks[("k", dt)], dco_tok])
        vsp(2)
        fprev = [st[tau - 1]["F7"]] if tau - 1 in st and "F7" in st[tau - 1] else []
        P("vector", lambda e: TT(e, kmax8[:], kmax8[:], M8[:], ALU.max), waits=fprev)
        vsp(2)
        P("vector", lambda e: TT(e, dMs[:], M8[:], kmax8[:], ALU.subtract))
        P("vector", lambda e: e.tensor_copy(M8[:], kmax8[:]))
        P("vector", lambda e: TS(e, negM8[:], kmax8[:], -1.0, ALU.mult))
        vsp(2, inc="vE")
        mb_tok = tok("vE")
        P("scalar", lambda e: e.activation(out=sc8[:], in_=dMs[:], func=AF.Exp),
          waits=[mb_tok], inc="sE")
        sc_tok = tok("sE")
        P("vector", lambda e: TT(e, Pc8[:], Pc8[:], sc8[:], ALU.mult), waits=[sc_tok])
        P("vector", lambda e: TT(e, Qc8[:], Qc8[:], sc8[:], ALU.mult))
        vsp(2, inc="vE")
        s["resc_tok"] = tok("vE")
        if debug:
            P("sync", lambda e, tau=tau: e.dma_start(
                out=dbg["carry"][:, tau * 32:tau * 32 + 8], in_=Pc8[:]),
              waits=[s["resc_tok"]], inc="dS", n=16)
            P("sync", lambda e, tau=tau: e.dma_start(
                out=dbg["carry"][:, tau * 32 + 8:tau * 32 + 16], in_=Qc8[:]),
              inc="dS", n=16)
            carry_dump = tok("dS")
            P("vector", lambda e: e.nop(), waits=[carry_dump])

    def a_wkv(tau):
        s = st[tau]
        ev_toks = s["ev_toks"]
        F_toks = {}
        for dt in range(NCT):
            fwaits = [s["resc_tok"]] if dt == 0 else []
            if debug and dt == 0 and "kdump_tok" in s:
                fwaits.append(s["kdump_tok"])
            P("scalar", lambda e, dt=dt: e.activation(
                out=kf[:, dt * 256:(dt + 1) * 256], in_=kf[:, dt * 256:(dt + 1) * 256],
                func=AF.Exp, bias=negM8[:, dt:dt + 1]), waits=fwaits, inc="sE")
            F_toks[dt] = tok("sE")
        s["F7"] = F_toks[7]
        wo_prev = [st[tau - 1]["wo_last"]] if tau - 1 in st and "wo_last" in st[tau - 1] else []
        rw_toks = {}
        for dt in range(NCT):
            kfs = kf[:, dt * 256:(dt + 1) * 256]
            vfs = vf[:, dt * 256:(dt + 1) * 256]
            dcbv = dcoefT[:, dt:dt + 1].broadcast_to((128, 256))
            w0 = [F_toks[dt]] + (wo_prev if dt == 0 else [])
            P("vector", lambda e, kfs=kfs, vfs=vfs: TT(e, Escr[:], kfs, vfs, ALU.mult),
              waits=w0)
            P("vector", lambda e, dt=dt, dcbv=dcbv: e.tensor_tensor_scan(
                out=Pbuf[:], data0=dcbv, data1=Escr[:],
                initial=Pc8[:, dt:dt + 1], op0=ALU.mult, op1=ALU.add))
            P("vector", lambda e, dt=dt, kfs=kfs, dcbv=dcbv: e.tensor_tensor_scan(
                out=Qbuf[:], data0=dcbv, data1=kfs,
                initial=Qc8[:, dt:dt + 1], op0=ALU.mult, op1=ALU.add))
            P("vector", lambda e, dt=dt, kfs=kfs, vfs=vfs: e.scalar_tensor_tensor(
                out=Gscr[:], in0=kfs, scalar=euT[:, dt:dt + 1], in1=vfs,
                op0=ALU.mult, op1=ALU.mult))
            P("vector", lambda e: TT(e, Nscr[:, 1:256], Gscr[:, 1:256],
                                     Pbuf[:, 0:255], ALU.add))
            P("vector", lambda e, dt=dt: TT(e, Nscr[:, 0:1], Gscr[:, 0:1],
                                            Pc8[:, dt:dt + 1], ALU.add))
            P("vector", lambda e, dt=dt, kfs=kfs: e.scalar_tensor_tensor(
                out=Dscr[:, 1:256], in0=kfs[:, 1:256], scalar=euT[:, dt:dt + 1],
                in1=Qbuf[:, 0:255], op0=ALU.mult, op1=ALU.add))
            P("vector", lambda e, dt=dt, kfs=kfs: e.scalar_tensor_tensor(
                out=Dscr[:, 0:1], in0=kfs[:, 0:1], scalar=euT[:, dt:dt + 1],
                in1=Qc8[:, dt:dt + 1], op0=ALU.mult, op1=ALU.add))
            P("vector", lambda e, dt=dt: e.tensor_copy(Pc8[:, dt:dt + 1], Pbuf[:, 255:256]),
              fast=True)
            P("vector", lambda e, dt=dt: e.tensor_copy(Qc8[:, dt:dt + 1], Qbuf[:, 255:256]),
              fast=True)
            P("vector", lambda e, dt=dt: e.scalar_tensor_tensor(
                out=Gscr[:], in0=erf[:, dt * 256:(dt + 1) * 256], scalar=1.0,
                in1=Dscr[:], op0=ALU.add, op1=ALU.mult),
              waits=[ev_toks[("r", dt)]])
            P("vector", lambda e: e.reciprocal(out=Escr[:], in_=Gscr[:]))
            vsp(1)
            P("vector", lambda e, dt=dt: TT(
                e, rwkvb[:, dt * 256:(dt + 1) * 256], Nscr[:], Escr[:], ALU.mult),
              inc="vE")
            rw_toks[dt] = tok("vE")
            if debug:
                P("vector", lambda e: TT(e, Nscr[:], Nscr[:], Escr[:], ALU.mult), inc="vE")
                dtk = tok("vE")
                P("sync", lambda e, dt=dt, c0=s["c0"]: e.dma_start(
                    out=dbg["wkv"][dt * 128:(dt + 1) * 128, c0:c0 + 256], in_=Nscr[:]),
                  waits=[dtk], inc="dS", n=16)
        s["rw_toks"] = rw_toks

    def a_tail(tau):
        nonlocal stA_free, stB_free
        s = st[tau]
        xs, c0 = s["xs"], s["c0"]
        rw_toks = s["rw_toks"]
        ao_toks = {}
        sq_toks = {}
        aof_guard = []
        if tau - 1 in tok_x1dma:
            aof_guard = [("dO", tok_x1dma[tau - 1]), tok_x1ops[tau - 1]]
        for dt in range(NCT):
            sli = pp_rr[0] % 4
            sl = pp[sli]
            fw = pp_free[sli]
            pp_rr[0] += 1
            for ctk in range(NCT):
                P("tensor", lambda e, sl=sl, ctk=ctk, dt=dt: e.matmul(
                    PS[:, sl:sl + 256],
                    Wbo[:, ctk * 1024 + dt * 128:ctk * 1024 + dt * 128 + 128],
                    rwkvb[:, ctk * 256:(ctk + 1) * 256],
                    start=(ctk == 0), stop=(ctk == 7)),
                  waits=[rw_toks[7], fw, wbt["WoT"]] if ctk == 0 else (),
                  inc="tE" if ctk == 7 else None)
            tk = tok("tE")
            P("scalar", lambda e, sl=sl, dt=dt: e.copy(
                aof[:, dt * 256:(dt + 1) * 256], PS[:, sl:sl + 256]),
              waits=[tk] + (aof_guard if dt == 0 else []), inc="sE")
            ao_toks[dt] = tok("sE")
            pp_free[sli] = tok("sE")
            if dt == NCT - 1:
                s["wo_last"] = tk
            P("scalar", lambda e, dt=dt: e.activation(
                out=sqf[:, (dt % 2) * 256:(dt % 2) * 256 + 256],
                in_=aof[:, dt * 256:(dt + 1) * 256], func=AF.Square),
              waits=[sq_free[dt % 2]], inc="sE")
            sq_toks[dt] = tok("sE")
            P("tensor", lambda e, dt=dt: e.matmul(
                PS[0:1, 2048:2304], ones_s[:], aof[:, dt * 256:(dt + 1) * 256],
                start=(dt == 0), stop=(dt == 7)),
              waits=[ao_toks[dt], stA_free] if dt == 0 else [ao_toks[dt]])
            P("tensor", lambda e, dt=dt: e.matmul(
                PS[0:1, 2560:2816], ones_s[:],
                sqf[:, (dt % 2) * 256:(dt % 2) * 256 + 256],
                start=(dt == 0), stop=(dt == 7)),
              waits=[sq_toks[dt]], inc="tE")
            sq_free[dt % 2] = tok("tE")
        statA_tok = tok("tE")

        mrs2_tok, var2_tok = ln_rows(
            PS[0:1, 2048:2304], PS[0:1, 2560:2816], rstd2R[0:1, 0:TCA],
            mrs2R[0:1, 0:TCA], mean2R[0:1, 0:TCA], scr2[0:1, 0:TCA],
            [statA_tok])
        stA_free = var2_tok

        for i, srcr in ((0, rstd2R), (1, mrs2R)):
            P("tensor", lambda e, i=i, srcr=srcr: e.matmul(
                PS[:, bc[i]:bc[i] + 256], onesK[:], srcr[0:1, 0:256],
                start=True, stop=True),
              waits=[bc_free[i], mrs2_tok], inc="tE" if i == 1 else None)
        bcast2_tok = tok("tE")

        x1_toks = {}
        sqB_toks = {}
        for dt in range(NCT):
            P("vector", lambda e, dt=dt: TT(
                e, dd[:], aof[:, dt * 256:(dt + 1) * 256],
                PS[:, bc[0]:bc[0] + 256], ALU.mult),
              waits=[bcast2_tok] if dt == 0 else ())
            P("vector", lambda e: TT(e, dd[:], dd[:], PS[:, bc[1]:bc[1] + 256],
                                     ALU.subtract))
            P("vector", lambda e, dt=dt, xs=xs: TT(
                e, aof[:, dt * 256:(dt + 1) * 256], dd[:],
                xs[:, dt * 257 + 1:dt * 257 + 257], ALU.add), inc="vE")
            x1_toks[dt] = tok("vE")
            P("scalar", lambda e, dt=dt: e.activation(
                out=sqf[:, (dt % 2) * 256:(dt % 2) * 256 + 256],
                in_=aof[:, dt * 256:(dt + 1) * 256], func=AF.Square),
              waits=[x1_toks[dt], sq_free[dt % 2]], inc="sE")
            sqB_toks[dt] = tok("sE")
            P("sync", lambda e, dt=dt, c0=c0: e.dma_start(
                out=dx1[dt * 128:(dt + 1) * 128, c0:c0 + 256],
                in_=aof[:, dt * 256:(dt + 1) * 256]),
              waits=[x1_toks[dt]], inc="dO", n=16)
            if debug:
                P("sync", lambda e, dt=dt, c0=c0: e.dma_start(
                    out=dbg["x1"][dt * 128:(dt + 1) * 128, c0:c0 + 256],
                    in_=aof[:, dt * 256:(dt + 1) * 256]), inc="dO", n=16)
            P("tensor", lambda e, dt=dt: e.matmul(
                PS[0:1, 2304:2560], ones_s[:], aof[:, dt * 256:(dt + 1) * 256],
                start=(dt == 0), stop=(dt == 7)),
              waits=[x1_toks[dt], stB_free] if dt == 0 else [x1_toks[dt]])
            P("tensor", lambda e, dt=dt: e.matmul(
                PS[0:1, 2816:3072], ones_s[:],
                sqf[:, (dt % 2) * 256:(dt % 2) * 256 + 256],
                start=(dt == 0), stop=(dt == 7)),
              waits=[sqB_toks[dt]], inc="tE")
            sq_free[dt % 2] = tok("tE")
        tok_x1dma[tau] = pb.cnt["dO"]
        tok_statB[tau] = tok("tE")
        P("vector", lambda e, c0=c0: e.tensor_copy(
            rowsum[0:1, c0:c0 + 256], PS[0:1, 2304:2560]),
          waits=[tok_statB[tau]])
        P("vector", lambda e, c0=c0: e.tensor_copy(
            rowsq[0:1, c0:c0 + 256], PS[0:1, 2816:3072]), inc="vE")
        stB_free = tok("vE")
        tok_x1ops[tau] = tok("vE")
        bc_free[0] = tok_x1ops[tau]
        bc_free[1] = tok_x1ops[tau]

    for tau in range(NCH + 1):
        if tau < NCH:
            a_loads(tau)
            a_xnmix(tau)
        if tau >= 1:
            a_wkv(tau - 1)
        if tau < NCH:
            a_proj(tau)
            a_scanprep(tau)
        if tau >= 1:
            a_tail(tau - 1)

    attn_done = [tok("tE"), tok("vE"), tok("sE"), tok("pE"),
                 ("dO", pb.cnt["dO"]), tok("dS")]

    ln2_tok, _ = ln_rows(rowsum[0:1, 0:T], rowsq[0:1, 0:T], rstdR[0:1, 0:T],
                         mrsR[0:1, 0:T], rowsum[0:1, 0:T], rowsq[0:1, 0:T],
                         [tok_x1ops[NCH - 1]])
    P("sync", lambda e: e.dma_start(out=drB.ap()[0:1, :], in_=rstdR[0:1, 0:T]),
      waits=[ln2_tok, ("dX", pb.cnt["dX"])], inc="dR", n=16)
    P("sync", lambda e: e.dma_start(out=drB.ap()[1:2, :], in_=mrsR[0:1, 0:T]),
      inc="dR", n=16)
    rows2_tok = tok("dR")

    for i in range(4):
        hfree[i] = attn_done[1]

    for ct in range(NCT):
        i, buf, off = load1024(dW["rw1"][ct * 128:(ct + 1) * 128, :])
        P("scalar", lambda e, buf=buf, off=off, ct=ct: e.copy(
            rw1b[:, ct * 1024:(ct + 1) * 1024], buf[:, off:off + 1024]),
          waits=[hfill[i]] + (attn_done if ct == 0 else []), inc="sE", fast=(ct != 7))
        free_half(i, tok("sE"))
    rw1_tok = tok("sE")
    for ct in range(NCT):
        P("sync", lambda e, ct=ct: e.dma_start(
            out=rw2stg[:, ct * 8:(ct + 1) * 8], in_=drw2[ct * 128:(ct + 1) * 128, :]),
          waits=attn_done if ct == 0 else (), inc="dW", n=16)
    rw2d = tok("dW")
    P("scalar", lambda e: e.copy(rw2b[:], rw2stg[:]),
      waits=[rw2d], inc="sE")
    rw2_tok = tok("sE")


    bc_free2 = [attn_done[1], attn_done[1]]
    bc_idx = [0]

    def bc_load(c0, width, rows_tok, gates=()):
        p = bc_idx[0] % 2
        bc_idx[0] += 1
        sname = "dV%d" % p
        P("sync", lambda e, c0=c0, width=width, p=p: e.dma_start(
            out=bcF[2 * p][:, 0:width],
            in_=drB.ap()[0:1, c0:c0 + width].broadcast_to((128, width))),
          waits=[rows_tok, bc_free2[p]] + list(gates), inc=sname, n=16)
        P("sync", lambda e, c0=c0, width=width, p=p: e.dma_start(
            out=bcF[2 * p + 1][:, 0:width],
            in_=drB.ap()[1:2, c0:c0 + width].broadcast_to((128, width))),
          inc=sname, n=16)
        return p, tok(sname)

    tbf_s_tok = {}
    for s in range(4):
        for ct in range(NCT):
            fx = [attn_done[4]] if (s == 0 and ct == 0) else []
            i, buf, off = load512(
                dx1[ct * 128:(ct + 1) * 128, s * 512:(s + 1) * 512],
                free_extra=fx, sem="dS")
            ld = hfill[i]
            p, bct = bc_load(s * 512, 512, rows2_tok)
            P("vector", lambda e, buf=buf, off=off, p=p: TT(
                e, scrX[:], buf[:, off:off + 512], bcF[2 * p][:, 0:512],
                ALU.mult), waits=[ld, bct])
            P("vector", lambda e, ct=ct, s=s, p=p: TT(
                e, t_bf[:, ct * 2048 + s * 512:ct * 2048 + (s + 1) * 512],
                scrX[:], bcF[2 * p + 1][:, 0:512], ALU.subtract), inc="vE")
            bc_free2[p] = tok("vE")
            free_half(i, tok("vE"))
        tbf_s_tok[s] = tok("vE")
    tbf_tok = tok("vE")
    if debug:
        for ct in range(NCT):
            P("sync", lambda e, ct=ct: e.dma_start(
                out=dbg["t"][ct * 128:(ct + 1) * 128, :],
                in_=t_bf[:, ct * 2048:(ct + 1) * 2048]),
              waits=[tbf_tok], inc="dS", n=16)

    rh_sl = [1024, 1536]
    rh_free = [attn_done[1], attn_done[1]]
    lg_free = attn_done[1]
    lg_toks = {}
    for tr in range(4):
        rhb_toks = {}
        for ht in range(NCT):
            sl = rh_sl[ht % 2]
            fw = rh_free[ht % 2]
            for ctk in range(NCT):
                P("tensor", lambda e, sl=sl, ht=ht, ctk=ctk, tr=tr: e.matmul(
                    PS[:, sl:sl + 512],
                    rw1b[:, ctk * 1024 + ht * 128:ctk * 1024 + ht * 128 + 128],
                    t_bf[:, ctk * 2048 + tr * 512:ctk * 2048 + (tr + 1) * 512],
                    start=(ctk == 0), stop=(ctk == 7)),
                  waits=[tbf_s_tok[tr], fw, rw1_tok] if ctk == 0 else (),
                  inc="tE" if ctk == 7 else None)
            mmt = tok("tE")
            P("scalar", lambda e, sl=sl, ht=ht: e.activation(
                out=rhb[:, ht * 512:(ht + 1) * 512], in_=PS[:, sl:sl + 512],
                func=AF.Silu, bias=vecT[:, 24 + ht:25 + ht]),
              waits=[mmt], inc="sE")
            rhb_toks[ht] = tok("sE")
            rh_free[ht % 2] = tok("sE")
            if debug and tr == 0:
                P("sync", lambda e, ht=ht: e.dma_start(
                    out=dbg["rh"][:, ht * 512:(ht + 1) * 512],
                    in_=rhb[:, ht * 512:(ht + 1) * 512]),
                  waits=[rhb_toks[ht]], inc="dS", n=16)
        for hc in range(NCT):
            w0 = [rhb_toks[hc]]
            if hc == 0:
                w0 += [rw2_tok, lg_free]
            P("tensor", lambda e, hc=hc: e.matmul(
                PS[0:8, 2048:2560], rw2b[:, hc * 8:(hc + 1) * 8],
                rhb[:, hc * 512:(hc + 1) * 512], start=(hc == 0), stop=(hc == 7)),
              waits=w0, inc="tE" if hc == 7 else None)
        lgmm = tok("tE")
        P("vector", lambda e, tr=tr: e.tensor_scalar(
            out=logits_sb[0:8, tr * 512:(tr + 1) * 512], in0=PS[0:8, 2048:2560],
            scalar1=rb2c[0:8, 0:1], scalar2=None, op0=ALU.add),
          waits=[lgmm], inc="vE")
        lg_toks[tr] = tok("vE")
        lg_free = tok("vE")
    logits_tok = tok("vE")
    if debug:
        P("sync", lambda e: e.dma_start(out=dbg["logits"], in_=logits_sb[:]),
          waits=[logits_tok], inc="dS", n=16)

    tp_sl = [2560, 2568]
    tp_free = [attn_done[1], attn_done[1]]
    wt_sl = 2576
    wt_free = attn_done[1]
    for j in range(16):
        P("tensor", lambda e, j=j: e.transpose(
            PS[:, tp_sl[j % 2]:tp_sl[j % 2] + 8],
            logits_sb[0:8, j * 128:(j + 1) * 128], eyeS[0:8, 0:8]),
          waits=[logits_tok, tp_free[j % 2]], inc="tE")
        tpt = tok("tE")
        P("vector", lambda e, j=j: e.tensor_copy(
            lgT[:], PS[:, tp_sl[j % 2]:tp_sl[j % 2] + 8]), waits=[tpt])
        vsp(2)
        P("vector", lambda e: e.reduce_max(out=kmax8[:, 0:1], in_=lgT[:], axis=AX))
        vsp(2)
        P("vector", lambda e: TS(e, wTj[:], lgT[:], kmax8[:, 0:1], ALU.is_ge))
        vsp(2)
        P("vector", lambda e: e.scalar_tensor_tensor(
            out=sc8[:], in0=wTj[:], scalar=-1e30, in1=lgT[:],
            op0=ALU.mult, op1=ALU.add))
        vsp(2)
        P("vector", lambda e: e.reduce_max(out=kmax8[:, 1:2], in_=sc8[:], axis=AX))
        vsp(2)
        P("vector", lambda e: TS(e, wTj[:], lgT[:], kmax8[:, 1:2], ALU.is_ge))
        P("vector", lambda e: TS(e, kmax8[:, 2:3], kmax8[:, 0:1], -1.0, ALU.mult))
        vsp(2, inc="vE")
        gv = tok("vE")
        P("scalar", lambda e: e.activation(out=sc8[:], in_=lgT[:], func=AF.Exp,
                                           bias=kmax8[:, 2:3]), waits=[gv], inc="sE")
        gs = tok("sE")
        P("vector", lambda e: TT(e, wTj[:], wTj[:], sc8[:], ALU.mult), waits=[gs])
        vsp(2)
        P("vector", lambda e: e.tensor_reduce(out=kmax8[:, 3:4], in_=wTj[:],
                                              axis=AX, op=ALU.add))
        vsp(2)
        P("vector", lambda e: e.reciprocal(out=kmax8[:, 4:5], in_=kmax8[:, 3:4]))
        vsp(2)
        P("vector", lambda e: TS(e, wTj[:], wTj[:], kmax8[:, 4:5], ALU.mult))
        vsp(2, inc="vE")
        gw = tok("vE")
        tp_free[j % 2] = gw
        P("tensor", lambda e, j=j: e.transpose(
            PS[0:8, wt_sl:wt_sl + 128], wTj[:], eyeS[:]),
          waits=[gw, wt_free], inc="tE")
        wtt = tok("tE")
        P("vector", lambda e, j=j: e.tensor_copy(
            w_fm[0:8, j * 128:(j + 1) * 128], PS[0:8, wt_sl:wt_sl + 128]),
          waits=[wtt], inc="vE")
        wt_free = tok("vE")
    gates_tok = tok("vE")
    P("scalar", lambda e: e.copy(wfmb[:], w_fm[:]),
      waits=[gates_tok, tok("tE"), logits_tok], inc="sE")
    wfmb_tok = tok("sE")
    P("sync", lambda e: e.dma_start(out=dwfm.ap(), in_=wfmb[:]),
      waits=[wfmb_tok], inc="dF", n=16)
    wfmd_tok = tok("dF")
    P("sync", lambda e: e.dma_start(out=eb2sb[:], in_=deb2), inc="dB", n=16)
    eb2_tok = tok("dB")
    if debug:
        P("sync", lambda e: e.dma_start(out=dbg["w"], in_=w_fm[:]),
          waits=[gates_tok], inc="dS", n=16)

    def plan_e1b_load(eidx, guard_toks):
        for ct in range(NCT):
            i, buf = load2048(dew1[eidx * C + ct * 128:eidx * C + (ct + 1) * 128, :])
            P("scalar", lambda e, buf=buf, ct=ct: e.copy(
                e1b[:, ct * 2048:(ct + 1) * 2048], buf[:, 0:2048]),
              waits=[hfill[i]] + (guard_toks if ct == 0 else []), inc="sE",
              fast=(ct != 7))
            free_pair(i, tok("sE"))
        return tok("sE")

    e1b_tok = plan_e1b_load(0, attn_done)
    ehw_guard = [gates_tok, logits_tok, wfmd_tok]
    spill_tok = None
    yev_last = {}
    wbuf_tok = None
    e2slot_free = [attn_done[0], attn_done[0]]

    for tf in range(2):
        tc0 = tf * TCF
        for ct in range(NCT):
            fx = []
            if tf == 1 and ct == 0:
                fx = [spill_tok]
            i, buf = load2048(dx1[ct * 128:(ct + 1) * 128, :], free_extra=fx, sem="dS")
            ld = hfill[i]
            for s2 in range(2):
                sidx = tf * 2 + s2
                gates = [spill_tok] if (tf == 1 and ct == 0 and s2 == 0) else []
                p, bct = bc_load(sidx * 512, 512, rows2_tok, gates)
                wv = [ld, bct] + gates
                P("vector", lambda e, buf=buf, s2=s2, tf=tf, p=p: TT(
                    e, scrX[:], buf[:, tf * 1024 + s2 * 512:tf * 1024 + (s2 + 1) * 512],
                    bcF[2 * p][:, 0:512], ALU.mult), waits=wv)
                P("vector", lambda e, ct=ct, s2=s2, p=p: TT(
                    e, yacc[:, ct * 1024 + s2 * 512:ct * 1024 + (s2 + 1) * 512],
                    scrX[:], bcF[2 * p + 1][:, 0:512], ALU.subtract), inc="vE")
                bc_free2[p] = tok("vE")
            free_pair(i, tok("vE"))
        yinit_tok = tok("vE")

        for eidx in range(NE):
            wg = [wfmd_tok, yinit_tok, tok("pE")]
            if wbuf_tok is not None:
                wg.append(wbuf_tok)
            P("sync", lambda e, eidx=eidx, tc0=tc0: e.dma_start(
                out=wbuf[:], in_=dwfm.ap()[eidx:eidx + 1, tc0:tc0 + TCF]
                .broadcast_to((128, TCF))),
              waits=wg, inc="dG", n=16)
            wbuf_tok = tok("dG")

            eh_sl = [0, 1024]
            ehw_toks = {}
            for hs in range(NHS):
                sl = eh_sl[hs % 2]
                if hs - 2 in ehw_toks:
                    slot_w = [ehw_toks[hs - 2][1]]
                else:
                    slot_w = [wbuf_tok]
                    for dsg in (2 * hs, 2 * hs + 1):
                        if (1, dsg) in yev_last:
                            slot_w.append(yev_last[(1, dsg)])
                        if (0, dsg) in yev_last:
                            slot_w.append(yev_last[(0, dsg)])
                    if not yev_last and eidx == 0 and tf == 0:
                        slot_w.append(tok("vE"))
                for nn in range(2):
                    for ctk in range(NCT):
                        w0 = []
                        if nn == 0 and ctk == 0:
                            w0 = slot_w + [e1b_tok, tbf_tok, wbuf_tok]
                            if hs == 0 and eidx == 0 and tf == 0:
                                w0 += ehw_guard
                        P("tensor", lambda e, sl=sl, nn=nn, ctk=ctk, hs=hs, tf=tf: e.matmul(
                            PS[:, sl + nn * 512:sl + (nn + 1) * 512],
                            e1b[:, ctk * 2048 + hs * 128:ctk * 2048 + hs * 128 + 128],
                            t_bf[:, ctk * 2048 + tf * 1024 + nn * 512:
                                 ctk * 2048 + tf * 1024 + (nn + 1) * 512],
                            start=(ctk == 0), stop=(ctk == 7)),
                          waits=w0, inc="tE" if (nn == 1 and ctk == 7) else None)
                mmt = tok("tE")
                P("scalar", lambda e, sl=sl, hs=hs, eidx=eidx: e.activation(
                    out=ehw[:, hs * 1024:(hs + 1) * 1024], in_=PS[:, sl:sl + 1024],
                    func=AF.Silu, bias=ebT[:, hs * 8 + eidx:hs * 8 + eidx + 1]),
                  waits=[mmt], inc="sE")
                silu_t = tok("sE")
                P("gpsimd", lambda e, hs=hs: e.tensor_mul(
                    ehw[:, hs * 1024:(hs + 1) * 1024],
                    ehw[:, hs * 1024:(hs + 1) * 1024], wbuf[:]),
                  waits=[silu_t], inc="pE", fast=True)
                ehw_toks[hs] = (tok("pE"), silu_t)
                if debug and tf == 0 and eidx == 1:
                    P("sync", lambda e, hs=hs: e.dma_start(
                        out=dbg["ehw"][hs * 128:(hs + 1) * 128, :],
                        in_=ehw[:, hs * 1024:(hs + 1) * 1024]),
                      waits=[ehw_toks[hs][0]], inc="dS", n=16)
            c1_tok = tok("tE")

            nxt = eidx + 1
            if nxt < NE:
                e1b_tok = plan_e1b_load(nxt, [c1_tok])
            elif tf == 0:
                e1b_tok = plan_e1b_load(0, [c1_tok])

            for nn in range(2):
                for hs in range(NHS):
                    i, buf, off = load1024(
                        dew2[eidx * HD + hs * 128:eidx * HD + (hs + 1) * 128, :])
                    cast_w = [hfill[i], e2slot_free[hs % 2]]
                    P("scalar", lambda e, buf=buf, off=off, hs=hs: e.copy(
                        e2b[hs % 2][:], buf[:, off:off + 1024]),
                      waits=cast_w, inc="sE")
                    free_half(i, tok("sE"))
                    ct_tok = tok("sE")
                    mm_w = [ct_tok, ehw_toks[hs][0]]
                    if hs == 0:
                        mm_w += [tok("vE"), ehw_toks[NHS - 1][1], ehw_toks[NHS - 2][1],
                                 wbuf_tok]
                    for ds in range(8):
                        P("tensor", lambda e, ds=ds, hs=hs, nn=nn, eidx=eidx: e.matmul(
                            PS[:, ds * 512:(ds + 1) * 512],
                            e2b[hs % 2][:, ds * 128:(ds + 1) * 128],
                            ehw[:, hs * 1024 + nn * 512:hs * 1024 + (nn + 1) * 512],
                            start=(hs == 0),
                            stop=(hs == NHS - 1 and eidx != NE - 1)),
                          waits=mm_w if ds == 0 else (),
                          inc="tE" if ds == 7 else None)
                    e2slot_free[hs % 2] = tok("tE")
                if eidx == NE - 1:
                    for ds in range(8):
                        P("tensor", lambda e, ds=ds, nn=nn, tc0=tc0: e.matmul(
                            PS[:, ds * 512:(ds + 1) * 512],
                            eb2sb[0:8, ds * 128:(ds + 1) * 128],
                            w_fm[0:8, tc0 + nn * 512:tc0 + (nn + 1) * 512],
                            start=False, stop=True),
                          waits=[eb2_tok, gates_tok] if ds == 0 else (),
                          inc="tE" if ds == 7 else None)
                c2_tok = tok("tE")
                for ds in range(8):
                    P("vector", lambda e, ds=ds, nn=nn: TT(
                        e, yacc[:, ds * 1024 + nn * 512:ds * 1024 + (nn + 1) * 512],
                        PS[:, ds * 512:(ds + 1) * 512],
                        yacc[:, ds * 1024 + nn * 512:ds * 1024 + (nn + 1) * 512],
                        ALU.add),
                      waits=[c2_tok] if ds == 0 else (), inc="vE")
                    yev_last[(nn, ds)] = tok("vE")
        if tf == 0:
            for ds in range(8):
                P("sync", lambda e, ds=ds: e.dma_start(
                    out=dyd[ds * 128:(ds + 1) * 128, :],
                    in_=yacc[:, ds * 1024:(ds + 1) * 1024]),
                  waits=[yev_last[(0, ds)], yev_last[(1, ds)]], inc="dO", n=16)
            spill_tok = ("dO", pb.cnt["dO"])

    yfin = dict(yev_last)
    if debug:
        for ds in range(8):
            P("sync", lambda e, ds=ds: e.dma_start(
                out=dbg["y"][ds * 128:(ds + 1) * 128, TCF:2 * TCF],
                in_=yacc[:, ds * 1024:(ds + 1) * 1024]),
              waits=[yfin[(0, ds)], yfin[(1, ds)], spill_tok], inc="dS", n=16)
            P("sync", lambda e, ds=ds: e.dma_start(
                out=dbg["y"][ds * 128:(ds + 1) * 128, 0:TCF],
                in_=dyd[ds * 128:(ds + 1) * 128, :]), inc="dS", n=16)
        ydump = tok("dS")
        P("vector", lambda e: e.nop(), waits=[ydump])

    d_mm_last = None
    for hf in range(2):
        for ct in range(NCT):
            if hf == 0:
                fx = [spill_tok] if ct == 0 else []
                i, buf, off = load1024(dyd[ct * 128:(ct + 1) * 128, :],
                                       free_extra=fx, sem="dS")
                srcf = lambda s2, buf=buf, off=off: buf[:, off + s2 * 512:off + (s2 + 1) * 512]
                ld = [hfill[i]]
            else:
                srcf = lambda s2, ct=ct: yacc[:, ct * 1024 + s2 * 512:ct * 1024 + (s2 + 1) * 512]
                ld = [yfin[(0, ct)], yfin[(1, ct)]]
            for s2 in range(2):
                pw = [tok("vE")] if d_mm_last is None else [d_mm_last]
                P("gpsimd", lambda e, srcf=srcf, s2=s2: e.tensor_mul(
                    scrX[:], srcf(s2), srcf(s2)), waits=ld + pw, inc="pE")
                sqt = tok("pE")
                bidx = hf * 2 + s2
                P("tensor", lambda e, srcf=srcf, s2=s2, bidx=bidx, ct=ct: e.matmul(
                    PS[0:1, bidx * 512:(bidx + 1) * 512], ones_s[:], srcf(s2),
                    start=(ct == 0), stop=(ct == 7)),
                  waits=ld + ([tok("vE"), e2slot_free[0], e2slot_free[1]]
                              if hf == 0 and ct == 0 and s2 == 0 else []))
                P("tensor", lambda e, bidx=bidx, ct=ct: e.matmul(
                    PS[0:1, 2048 + bidx * 512:2048 + (bidx + 1) * 512], ones_s[:],
                    scrX[:], start=(ct == 0), stop=(ct == 7)),
                  waits=[sqt], inc="tE")
                d_mm_last = tok("tE")
            if hf == 0:
                free_half(i, tok("tE"))
    d_mm = tok("tE")
    for s in range(4):
        P("vector", lambda e, s=s: e.tensor_copy(
            rowsum[0:1, s * 512:(s + 1) * 512], PS[0:1, s * 512:(s + 1) * 512]),
          waits=[d_mm] if s == 0 else ())
        P("vector", lambda e, s=s: e.tensor_copy(
            rowsq[0:1, s * 512:(s + 1) * 512],
            PS[0:1, 2048 + s * 512:2048 + (s + 1) * 512]),
          inc="vE" if s == 3 else None)
    d_ev = tok("vE")
    ln3_tok, _ = ln_rows(rowsum[0:1, 0:T], rowsq[0:1, 0:T], rstdR[0:1, 0:T],
                         mrsR[0:1, 0:T], rowsum[0:1, 0:T], rowsq[0:1, 0:T],
                         [d_ev])
    P("sync", lambda e: e.dma_start(out=drB.ap()[0:1, :], in_=rstdR[0:1, 0:T]),
      waits=[ln3_tok, ("dV0", pb.cnt["dV0"]), ("dV1", pb.cnt["dV1"])], inc="dR", n=16)
    P("sync", lambda e: e.dma_start(out=drB.ap()[1:2, :], in_=mrsR[0:1, 0:T]),
      inc="dR", n=16)
    rows3_tok = tok("dR")


    for ct in range(NCT):
        for hf in range(2):
            i1, b1, o1 = load1024(dx1[ct * 128:(ct + 1) * 128,
                                      hf * 1024:(hf + 1) * 1024], sem="dS")
            ldx = hfill[i1]
            if hf == 0:
                i2, b2, o2 = load1024(dyd[ct * 128:(ct + 1) * 128, :], sem="dS")
                ysrc = lambda s2, b2=b2, o2=o2: b2[:, o2 + s2 * 512:o2 + (s2 + 1) * 512]
                ldy = [hfill[i2]]
            else:
                ysrc = lambda s2, ct=ct: yacc[:, ct * 1024 + s2 * 512:ct * 1024 + (s2 + 1) * 512]
                ldy = []
            last_ot = None
            for s2 in range(2):
                sidx = hf * 2 + s2
                p, bct = bc_load(sidx * 512, 512, rows3_tok)
                ob = outbuf[:, ct * 2048 + hf * 1024 + s2 * 512:
                            ct * 2048 + hf * 1024 + (s2 + 1) * 512]
                P("vector", lambda e, ysrc=ysrc, s2=s2, p=p: TT(
                    e, scrX[:], ysrc(s2), bcF[2 * p][:, 0:512], ALU.mult),
                  waits=[bct, ldx] + ldy)
                P("vector", lambda e, p=p: TT(e, scrX[:], scrX[:],
                                              bcF[2 * p + 1][:, 0:512],
                                              ALU.subtract))
                P("vector", lambda e, ob=ob, s2=s2, b1=b1, o1=o1: TT(
                    e, ob, scrX[:], b1[:, o1 + s2 * 512:o1 + (s2 + 1) * 512],
                    ALU.add), inc="vE")
                ot = tok("vE")
                bc_free2[p] = ot
                last_ot = ot
                P("sync", lambda e, ob=ob, s2=s2, ct=ct, hf=hf: e.dma_start(
                    out=dout[ct * 128:(ct + 1) * 128,
                             hf * 1024 + s2 * 512:hf * 1024 + (s2 + 1) * 512],
                    in_=ob), waits=[ot], inc="dO", n=16)
            free_half(i1, last_ot)
            if hf == 0:
                free_half(i2, last_ot)

    final_dO = tok("dO")
    P("sync", lambda e: e.nop(), waits=[final_dO])

    pb.emit_all()
    return nc


def kernel(**inputs):
    x = np.asarray(inputs["L0_attn_x"], np.float32)
    Wk = np.asarray(inputs["L0_attn_Wk"], np.float32)
    Wv = np.asarray(inputs["L0_attn_Wv"], np.float32)
    Wr = np.asarray(inputs["L0_attn_Wr"], np.float32)
    Wo = np.asarray(inputs["L0_attn_Wo"], np.float32)
    vec6 = np.concatenate([
        np.asarray(inputs["L0_attn_mix_k"], np.float32).reshape(8, 128),
        np.asarray(inputs["L0_attn_mix_v"], np.float32).reshape(8, 128),
        np.asarray(inputs["L0_attn_mix_r"], np.float32).reshape(8, 128),
        np.asarray(inputs["L0_ffn_rb1"], np.float32).reshape(8, 128),
        np.asarray(inputs["L0_attn_w"], np.float32).reshape(8, 128),
        np.asarray(inputs["L0_attn_u"], np.float32).reshape(8, 128),
    ], axis=0)
    common = dict(
        WkT=np.ascontiguousarray(Wk.T),
        WvT=np.ascontiguousarray(Wv.T),
        WrT=np.ascontiguousarray(Wr.T),
        WoT=np.ascontiguousarray(Wo.T),
        rw1=np.ascontiguousarray(np.asarray(inputs["L0_ffn_rw1"], np.float32)),
        rw2=np.ascontiguousarray(np.asarray(inputs["L0_ffn_rw2"], np.float32)),
        vec6=np.ascontiguousarray(vec6),
        eb1=np.ascontiguousarray(np.asarray(inputs["L0_ffn_eb1"], np.float32)),
        eb2=np.ascontiguousarray(np.asarray(inputs["L0_ffn_eb2"], np.float32)),
        rb2=np.ascontiguousarray(np.asarray(inputs["L0_ffn_rb2"], np.float32).reshape(8, 1)),
        ew1=np.ascontiguousarray(np.asarray(inputs["L0_ffn_ew1"], np.float32).reshape(NE * C, HD)),
        ew2=np.ascontiguousarray(np.asarray(inputs["L0_ffn_ew2"], np.float32).reshape(NE * HD, C)),
        eye=np.eye(128, dtype=np.float32),
    )
    nc = build_program(debug=False)
    in_maps = []
    for b in range(8):
        m = dict(common)
        m["x"] = np.ascontiguousarray(x[b].T)
        in_maps.append(m)
    res = run_bass_kernel_spmd(nc, in_maps, core_ids=list(range(8)))
    out = np.stack([np.asarray(r["out"], np.float32).T for r in res.results], axis=0)
    return np.ascontiguousarray(out)


# version 15
# speedup vs baseline: 1.4392x; 1.1065x over previous
import contextlib
import numpy as np
import concourse.bass as bass
import concourse.mybir as mybir
from concourse.bass_utils import run_bass_kernel_spmd

F32 = mybir.dt.float32
BF16 = mybir.dt.bfloat16
AF = mybir.ActivationFunctionType
ALU = mybir.AluOpType
AX = mybir.AxisListType.X

C = 1024
T = 2048
NCT = 8
TCA = 256
NCH = T // TCA
TCF = 1024
NE = 8
HD = 2048
NHS = 16
EPS = 1e-5
NEG = -1e30

ENGINES = ("sync", "tensor", "vector", "scalar", "gpsimd")


class PB:
    def __init__(self, nc):
        self.nc = nc
        self.steps = {e: [] for e in ENGINES}
        self.sems = {}
        self.cnt = {}
        self._es = contextlib.ExitStack()

    def sem(self, name):
        self.sems[name] = self._es.enter_context(self.nc.semaphore(name))
        self.cnt[name] = 0

    def plan(self, eng, emit, waits=(), inc=None, n=1, fast=False):
        incs = ()
        if inc is not None:
            self.cnt[inc] += n
            incs = ((inc, n),)
        for s, v in waits:
            assert v <= self.cnt[s], f"future wait {s}:{v}>{self.cnt[s]}"
        ws = tuple((s, v) for (s, v) in waits if v > 0)
        self.steps[eng].append((ws, emit, incs, fast))

    def tok(self, s):
        return (s, self.cnt[s])

    def emit_all(self):
        with self.nc.Block() as block:
            for e in ENGINES:
                steps = self.steps[e]
                sems = self.sems
                drained = e in ("vector", "scalar", "gpsimd")

                def body(eng, steps=steps, drained=drained):
                    last = {}
                    for waits, emit, incs, fast in steps:
                        for s, v in waits:
                            if last.get(s, 0) < v:
                                eng.wait_ge(sems[s], v)
                                last[s] = v
                        instr = emit(eng)
                        dma_incs = [(s, nn) for s, nn in incs if nn >= 16]
                        eng_incs = [(s, nn) for s, nn in incs if nn < 16]
                        for s, nn in dma_incs:
                            instr.then_inc(sems[s], nn)
                        if drained and not dma_incs and not fast:
                            instr = eng.drain()
                        for s, nn in eng_incs:
                            instr.then_inc(sems[s], nn)
                getattr(block, e)(body)
        self._es.close()


class SbufMap:
    def __init__(self, nc):
        self.nc = nc
        base = nc.SBUF_PARTITION_SIZE_BYTES - nc.sbuf_bytes_remaining
        self.base = (base + 127) // 128 * 128
        self.cap = nc.SBUF_PARTITION_SIZE_BYTES - self.base - 512
        self.cursor = 0
        self.marks = {}
        self.offs = {}

    def alloc(self, name, shape, dtype, align=64):
        itm = 2 if dtype == BF16 else 4
        self.cursor = (self.cursor + align - 1) // align * align
        off = self.cursor
        self.cursor += shape[-1] * itm
        assert self.cursor <= self.cap, (name, self.cursor, self.cap)
        self.offs[name] = off
        return self.nc.alloc_sbuf_tensor_at(name, shape, dtype,
                                            offset=self.base + off)

    def at(self, name, shape, dtype, off):
        itm = 2 if dtype == BF16 else 4
        assert off + shape[-1] * itm <= self.cap, name
        self.offs[name] = off
        return self.nc.alloc_sbuf_tensor_at(name, shape, dtype,
                                            offset=self.base + off)

    def mark(self, name):
        self.marks[name] = self.cursor

    def reset_to(self, name):
        self.cursor = self.marks[name]


def build_program(debug=False):
    nc = bass.Bass("TRN2", target_bir_lowering=False)

    dx = nc.dram_tensor("x", [C, T], F32, kind="ExternalInput").ap()
    dW = {}
    for nm in ("WkT", "WvT", "WrT", "WoT", "rw1"):
        dW[nm] = nc.dram_tensor(nm, [C, C], F32, kind="ExternalInput").ap()
    drw2 = nc.dram_tensor("rw2", [C, NE], F32, kind="ExternalInput").ap()
    dvec = nc.dram_tensor("vec6", [48, 128], F32, kind="ExternalInput").ap()
    deb1 = nc.dram_tensor("eb1", [NE, HD], F32, kind="ExternalInput").ap()
    deb2 = nc.dram_tensor("eb2", [NE, C], F32, kind="ExternalInput").ap()
    drb2 = nc.dram_tensor("rb2", [NE, 1], F32, kind="ExternalInput").ap()
    dew1 = nc.dram_tensor("ew1", [NE * C, HD], F32, kind="ExternalInput").ap()
    dew2 = nc.dram_tensor("ew2", [NE * HD, C], F32, kind="ExternalInput").ap()
    deye = nc.dram_tensor("eye", [128, 128], F32, kind="ExternalInput").ap()

    dout = nc.dram_tensor("out", [C, T], F32, kind="ExternalOutput").ap()
    dx1 = nc.dram_tensor("x1d", [C, T], F32, kind="Internal").ap()
    dyd = nc.dram_tensor("yd", [C, TCF], F32, kind="Internal").ap()
    dwfm = nc.dram_tensor("wfm_d", [NE, T], BF16, kind="Internal")
    drB = nc.dram_tensor("rows_d", [2, T], F32, kind="Internal")
    dbg = {}
    if debug:
        for nm, shp in (("x1", [C, T]), ("logits", [NE, T]), ("w", [NE, T]),
                        ("k", [C, T]), ("wkv", [C, T]), ("carry", [128, 256]),
                        ("stats", [2, T])):
            dbg[nm] = nc.dram_tensor("dbg_" + nm, shp, F32,
                                     kind="ExternalOutput").ap()
        dbg["t"] = nc.dram_tensor("dbg_t", [C, T], BF16, kind="ExternalOutput").ap()
        dbg["y"] = nc.dram_tensor("dbg_y", [C, T], F32, kind="ExternalOutput").ap()
        dbg["ehw"] = nc.dram_tensor("dbg_ehw", [HD, TCF], BF16, kind="ExternalOutput").ap()
        dbg["rh"] = nc.dram_tensor("dbg_rh", [128, 4096], BF16, kind="ExternalOutput").ap()
        dbg["dcb"] = nc.dram_tensor("dbg_dcb", [128, 2080], F32, kind="ExternalOutput").ap()

    sm = SbufMap(nc)
    A = sm.alloc
    eyeS = A("eyeS", [128, 128], F32)
    onesK = A("onesK", [1, 128], F32)
    ones_s = A("ones_s", [128, 1], F32)
    vecT = A("vecT", [128, 48], F32)
    dcoefT = A("dcoefT", [128, 8], F32)
    euT = A("euT", [128, 8], F32)
    ebT = A("ebT", [128, 128], F32)
    rb2c = A("rb2c", [NE, 1], F32)
    Pc8 = A("Pc8", [128, 8], F32)
    Qc8 = A("Qc8", [128, 8], F32)
    M8 = A("M8", [128, 8], F32)
    kmax8 = A("kmax8", [128, 8], F32)
    dMs = A("dMs", [128, 8], F32)
    sc8 = A("sc8", [128, 8], F32)
    negM8 = A("negM8", [128, 8], F32)
    rowsum = A("rowsum", [1, T], F32)
    rowsq = A("rowsq", [1, T], F32)
    rstdR = A("rstdR", [1, T], F32)
    mrsR = A("mrsR", [1, T], F32)
    stgA0 = A("stgA0", [128, 2048], F32)
    stgA1 = A("stgA1", [128, 2048], F32)
    junk = A("junk", [128, 16], F32)
    sm.mark("overlay")

    Wbk = A("Wbk", [128, 8192], BF16)
    Wbv = A("Wbv", [128, 8192], BF16)
    Wbr = A("Wbr", [128, 8192], BF16)
    Wbo = A("Wbo", [128, 8192], BF16)
    xsb = [A("xs0", [128, 8 * 257], F32), A("xs1", [128, 8 * 257], F32)]
    xn = A("xn", [128, 8 * 257], F32)
    xkb = A("xkb", [128, 2048], BF16)
    xvb = A("xvb", [128, 2048], BF16)
    xrb = A("xrb", [128, 2048], BF16)
    kf = A("kf", [128, 2048], F32)
    vf = A("vf", [128, 2048], F32)
    erf = A("erf", [128, 2048], F32)
    dd = A("dd", [128, 256], F32)
    Escr = A("Escr", [128, 256], F32)
    Gscr = A("Gscr", [128, 256], F32)
    Nscr = A("Nscr", [128, 256], F32)
    Dscr = A("Dscr", [128, 256], F32)
    Pbuf = A("Pbuf", [128, 256], F32)
    Qbuf = A("Qbuf", [128, 256], F32)
    sqf = A("sqf", [128, 512], F32)
    mean2R = A("mean2R", [1, TCA], F32)
    scr2 = A("scr2", [1, TCA], F32)
    rstd2R = A("rstd2R", [1, TCA], F32)
    mrs2R = A("mrs2R", [1, TCA], F32)
    aofR = A("aofR", [128, 2048], F32)
    xbc0 = A("xbc0", [128, 264], F32)
    xbc1 = A("xbc1", [128, 264], F32)
    rwkvbR = A("rwkvbR", [128, 2048], BF16)
    attn_end = sm.cursor
    sm.reset_to("overlay")

    yacc = A("yacc", [128, 8 * TCF], F32)
    t_bf = A("t_bf", [128, 8 * T], BF16)
    e1b = A("e1b", [128, 8 * 2048], BF16)
    ehw = A("ehw", [128, NHS * 1024], BF16)
    rw2b = A("rw2b", [128, 64], BF16)
    wbuf = A("wbuf", [128, TCF], BF16)
    scrX = A("scrX", [128, 512], F32)
    eb2sb = A("eb2sb", [NE, C], F32)
    w_fm = A("w_fm", [NE, T], F32)
    lgT = A("lgT", [128, 8], F32)
    wTj = A("wTj", [128, 8], F32)
    rw2stg = A("rw2stg", [128, 64], F32)
    bcF = [A("bcF0", [128, 512], F32), A("bcF1", [128, 512], F32),
           A("bcF2", [128, 512], F32), A("bcF3", [128, 512], F32)]
    ffn_end = sm.cursor
    assert max(attn_end, ffn_end) <= sm.cap, (attn_end, ffn_end, sm.cap)

    rhb = sm.at("rhb", [128, 4096], BF16, sm.offs["ehw"])
    wfmb = sm.at("wfmb", [NE, T], BF16, sm.offs["ehw"])
    logits_sb = sm.at("logits_sb", [NE, T], F32, sm.offs["ehw"] + 8192)
    rw1b = sm.at("rw1b", [128, 8192], BF16, sm.offs["ehw"] + 16384)
    aof = sm.at("aof", [128, 2048], F32, sm.offs["aofR"])
    rwkvb = sm.at("rwkvb", [128, 2048], BF16, sm.offs["rwkvbR"])
    vstg = sm.at("vstg", [8, 768], F32, sm.offs["vf"])
    outbuf = sm.at("outbuf", [128, 16384], F32, sm.offs["e1b"])
    ebstg = sm.at("ebstg", [8, 2048], F32, sm.offs["kf"])
    e2full = sm.at("e2full", [128, NHS * 1024], BF16, sm.offs["rowsum"])

    PS = nc.alloc_psum_tensor("PS", [128, 4096], F32)

    pb = PB(nc)
    for s in ("dS", "dW", "dO", "dG", "dF", "dB", "dR", "dV0", "dV1", "dX",
              "tE", "vE", "sE", "pE", "x0", "x1s", "sl0", "sl1", "sl2", "sl3"):
        pb.sem(s)
    P = pb.plan
    tok = pb.tok

    def TT(e, out, a, b, op):
        return e.tensor_tensor(out=out, in0=a, in1=b, op=op)

    def TS(e, out, a, s1, op):
        return e.tensor_scalar(out=out, in0=a, scalar1=s1, scalar2=None, op0=op)

    def vsp(n=2, inc=None):
        for q in range(n):
            P("vector", lambda e: e.tensor_scalar(
                out=junk[:], in0=junk[:], scalar1=0.0, scalar2=None, op0=ALU.mult),
              inc=inc if q == n - 1 else None)

    P("sync", lambda e: e.dma_start(out=eyeS[:], in_=deye), inc="dW", n=16)
    for g in range(6):
        P("sync", lambda e, g=g: e.dma_start(
            out=vstg[:, g * 128:(g + 1) * 128], in_=dvec[g * 8:(g + 1) * 8, :]),
          inc="dW", n=16)
    P("sync", lambda e: e.dma_start(out=ebstg[:], in_=deb1), inc="dW", n=16)
    P("sync", lambda e: e.dma_start(out=rb2c[:], in_=drb2), inc="dW", n=16)
    w_const = tok("dW")

    P("vector", lambda e: e.memset(junk[:], 0.0))
    P("vector", lambda e: e.memset(onesK[:], 1.0))
    P("vector", lambda e: e.memset(ones_s[:], 1.0))
    P("vector", lambda e: e.memset(M8[:], NEG))
    P("vector", lambda e: e.memset(Pc8[:], 0.0))
    P("vector", lambda e: e.memset(Qc8[:], 0.0), inc="vE")
    init_tok = tok("vE")

    for j in range(16):
        P("tensor", lambda e, j=j: e.transpose(
            PS[:, 8 * j:8 * j + 8], ebstg[0:8, j * 128:(j + 1) * 128],
            eyeS[0:8, 0:8]), waits=[w_const] if j == 0 else ())
    for g in range(6):
        P("tensor", lambda e, g=g: e.transpose(
            PS[:, 128 + 8 * g:136 + 8 * g], vstg[0:8, g * 128:(g + 1) * 128],
            eyeS[0:8, 0:8]), inc="tE" if g == 5 else None)
    tr_tok = tok("tE")
    P("scalar", lambda e: e.copy(ebT[:], PS[:, 0:128]), waits=[tr_tok])
    P("scalar", lambda e: e.copy(vecT[:], PS[:, 128:176]), inc="sE")
    vecT_tok = tok("sE")
    P("scalar", lambda e: e.activation(out=dcoefT[:], in_=vecT[:, 32:40], func=AF.Exp))
    P("scalar", lambda e: e.activation(out=dcoefT[:], in_=dcoefT[:], func=AF.Exp, scale=-1.0))
    P("scalar", lambda e: e.activation(out=euT[:], in_=vecT[:, 40:48], func=AF.Exp))
    P("scalar", lambda e: e.copy(dMs[:], dcoefT[:]))
    P("scalar", lambda e: e.copy(sc8[:], euT[:]), inc="sE")
    dco_tok = tok("sE")

    half = [(stgA0, 0), (stgA0, 1024), (stgA1, 0), (stgA1, 1024)]
    hfree = [init_tok] * 4
    hfill = [("dW", 0)] * 4
    hrr = [0]
    prr = [0]

    def load1024(src, free_extra=(), sem=None):
        i = hrr[0] % 4
        hrr[0] += 1
        buf, off = half[i]
        ssem = f"sl{i}"
        P("sync", lambda e, buf=buf, off=off, src=src: e.dma_start(
            out=buf[:, off:off + 1024], in_=src),
          waits=[hfree[i]] + list(free_extra), inc=ssem, n=16)
        hfill[i] = tok(ssem)
        return i, buf, off

    def load512(src, free_extra=(), sem=None):
        i = hrr[0] % 4
        hrr[0] += 1
        buf, off = half[i]
        ssem = f"sl{i}"
        h = hfree[i] if isinstance(hfree[i], list) else [hfree[i]]
        P("sync", lambda e, buf=buf, off=off, src=src: e.dma_start(
            out=buf[:, off:off + 512], in_=src),
          waits=h + list(free_extra), inc=ssem, n=16)
        hfill[i] = tok(ssem)
        return i, buf, off

    def load2048(src, free_extra=(), sem=None):
        i = 0 if (prr[0] % 2 == 0) else 2
        prr[0] += 1
        buf = half[i][0]
        ssem = f"sl{i}"
        P("sync", lambda e, buf=buf, src=src: e.dma_start(
            out=buf[:, 0:2048], in_=src),
          waits=[hfree[i], hfree[i + 1]] + list(free_extra), inc=ssem, n=16)
        hfill[i] = hfill[i + 1] = tok(ssem)
        return i, buf

    def free_half(i, t):
        hfree[i] = t

    def free_pair(i, t):
        hfree[i] = t
        hfree[i + 1] = t

    wbt = {}
    for nm, dst in (("WkT", Wbk), ("WvT", Wbv), ("WrT", Wbr), ("WoT", Wbo)):
        for ct in range(NCT):
            i, buf, off = load1024(dW[nm][ct * 128:(ct + 1) * 128, :])
            P("scalar", lambda e, buf=buf, off=off, dst=dst, ct=ct: e.copy(
                dst[:, ct * 1024:(ct + 1) * 1024], buf[:, off:off + 1024]),
              waits=[hfill[i]], inc="sE", fast=(ct != 7))
            free_half(i, tok("sE"))
        wbt[nm] = tok("sE")

    ph0_mm_tok = {}
    for ct in range(NCT):
        xs = xsb[ct % 2]
        xw = [ph0_mm_tok[ct - 2]] if ct - 2 in ph0_mm_tok else [init_tok]
        P("sync", lambda e, xs=xs, ct=ct: e.dma_start(
            out=xs[:, 0:2048], in_=dx[ct * 128:(ct + 1) * 128, :]),
          waits=xw, inc="x0" if ct % 2 == 0 else "x1s", n=16)
        ld = tok("x0" if ct % 2 == 0 else "x1s")
        P("gpsimd", lambda e, xs=xs: e.tensor_mul(
            xn[:, 0:2048], xs[:, 0:2048], xs[:, 0:2048]),
          waits=[ld] + ([ph0_mm_tok[ct - 1]] if ct - 1 in ph0_mm_tok else []),
          inc="pE")
        sq = tok("pE")
        for s in range(4):
            P("tensor", lambda e, xs=xs, s=s, ct=ct: e.matmul(
                PS[0:1, s * 512:(s + 1) * 512], ones_s[:],
                xs[:, s * 512:(s + 1) * 512], start=(ct == 0), stop=(ct == 7)),
              waits=[ld, vecT_tok] if s == 0 else ())
        for s in range(4):
            P("tensor", lambda e, s=s, ct=ct: e.matmul(
                PS[0:1, 2048 + s * 512:2048 + (s + 1) * 512], ones_s[:],
                xn[:, s * 512:(s + 1) * 512], start=(ct == 0), stop=(ct == 7)),
              waits=[sq] if s == 0 else (), inc="tE" if s == 3 else None)
        ph0_mm_tok[ct] = tok("tE")
    ph0_mm = tok("tE")

    def ln_rows(sum_ap, sq_ap, rstd_ap, mrs_ap, mean_ap, scr_ap, waits_v):
        vsp(2)
        P("vector", lambda e: TS(e, mean_ap, sum_ap, 1.0 / C, ALU.mult), waits=waits_v)
        P("vector", lambda e: TS(e, scr_ap, sq_ap, 1.0 / C, ALU.mult))
        vsp(1)
        P("vector", lambda e: TT(e, rstd_ap, mean_ap, mean_ap, ALU.mult))
        P("vector", lambda e: TS(e, scr_ap, scr_ap, EPS, ALU.add))
        vsp(1)
        P("vector", lambda e: TT(e, scr_ap, scr_ap, rstd_ap, ALU.subtract))
        vsp(2, inc="vE")
        tv = tok("vE")
        P("scalar", lambda e: e.activation(out=rstd_ap, in_=scr_ap, func=AF.Ln),
          waits=[tv])
        P("scalar", lambda e: e.activation(out=rstd_ap, in_=rstd_ap, func=AF.Exp, scale=-0.5))
        P("scalar", lambda e: e.copy(junk[:], junk[:]))
        P("scalar", lambda e: e.copy(junk[:], junk[:]), inc="sE")
        ts_ = tok("sE")
        P("vector", lambda e: TT(e, mrs_ap, mean_ap, rstd_ap, ALU.mult), waits=[ts_])
        vsp(2, inc="vE")
        return tok("vE"), tv

    for s in range(4):
        P("vector", lambda e, s=s: e.tensor_copy(
            rowsum[0:1, s * 512:(s + 1) * 512], PS[0:1, s * 512:(s + 1) * 512]),
          waits=[ph0_mm] if s == 0 else ())
        P("vector", lambda e, s=s: e.tensor_copy(
            rowsq[0:1, s * 512:(s + 1) * 512],
            PS[0:1, 2048 + s * 512:2048 + (s + 1) * 512]),
          inc="vE" if s == 3 else None)
    ph0_ev = tok("vE")
    ln1_tok, _ = ln_rows(rowsum[0:1, 0:T], rowsq[0:1, 0:T], rstdR[0:1, 0:T],
                         mrsR[0:1, 0:T], rowsum[0:1, 0:T], rowsq[0:1, 0:T],
                         [ph0_ev])
    P("sync", lambda e: e.dma_start(out=drB.ap()[0:1, :], in_=rstdR[0:1, 0:T]),
      waits=[ln1_tok], inc="dR", n=16)
    P("sync", lambda e: e.dma_start(out=drB.ap()[1:2, :], in_=mrsR[0:1, 0:T]),
      inc="dR", n=16)
    rows1_tok = tok("dR")

    pp = [0, 512, 1024, 1536]
    pp_free = [ph0_ev] * 4
    pp_rr = [0]
    bc = [3072, 3584]
    bc_free = [ph0_ev, ph0_ev]
    stA_free = ph0_ev
    stB_free = ph0_ev
    sq_free = [ph0_ev, ph0_ev]
    tok_x1ops = {}
    tok_x1dma = {}
    tok_statB = {}
    tok_xn = {}
    st = {}

    def a_loads(tau):
        s = st[tau] = {}
        s["xs"] = xsb[tau % 2]
        s["c0"] = tau * TCA
        xs, c0 = s["xs"], s["c0"]
        xw = [tok_x1ops[tau - 2]] if tau - 2 in tok_x1ops else [ph0_ev]
        xsem = "x0" if tau % 2 == 0 else "x1s"
        for ct in range(NCT):
            if tau == 0:
                P("sync", lambda e, ct=ct, xs=xs: e.dma_start(
                    out=xs[:, ct * 257 + 1:ct * 257 + 257],
                    in_=dx[ct * 128:(ct + 1) * 128, 0:256]),
                  waits=xw, inc=xsem, n=16)
            else:
                P("sync", lambda e, ct=ct, xs=xs, c0=c0: e.dma_start(
                    out=xs[:, ct * 257:ct * 257 + 257],
                    in_=dx[ct * 128:(ct + 1) * 128, c0 - 1:c0 + 256]),
                  waits=xw, inc=xsem, n=16)
        s["xs_tok"] = tok(xsem)
        lo = max(c0 - 1, 0)
        o0h = 1 if tau == 0 else 0
        xbw = [rows1_tok]
        if tau - 1 in tok_xn:
            xbw.append(tok_xn[tau - 1])
        P("sync", lambda e, lo=lo, o0h=o0h, c0=c0: e.dma_start(
            out=xbc0[:, o0h:257],
            in_=drB.ap()[0:1, lo:c0 + 256].broadcast_to((128, 257 - o0h))),
          waits=xbw, inc="dX", n=16)
        P("sync", lambda e, lo=lo, o0h=o0h, c0=c0: e.dma_start(
            out=xbc1[:, o0h:257],
            in_=drB.ap()[1:2, lo:c0 + 256].broadcast_to((128, 257 - o0h))),
          inc="dX", n=16)
        s["bcast_tok"] = tok("dX")

    def a_xnmix(tau):
        s = st[tau]
        xs = s["xs"]
        o0 = 1 if tau == 0 else 0
        for ct in range(NCT):
            P("vector", lambda e, ct=ct, xs=xs, o0=o0: TT(
                e, xn[:, ct * 257 + o0:(ct + 1) * 257],
                xs[:, ct * 257 + o0:(ct + 1) * 257],
                xbc0[:, o0:257], ALU.mult),
              waits=[s["xs_tok"], s["bcast_tok"]] if ct == 0 else ())
            last = (ct == 7) and (tau != 0)
            P("vector", lambda e, ct=ct, o0=o0: TT(
                e, xn[:, ct * 257 + o0:(ct + 1) * 257],
                xn[:, ct * 257 + o0:(ct + 1) * 257],
                xbc1[:, o0:257], ALU.subtract),
              inc="vE" if last else None)
        if tau == 0:
            for ct in range(NCT):
                P("vector", lambda e, ct=ct: e.memset(xn[:, ct * 257:ct * 257 + 1], 0.0),
                  inc="vE" if ct == 7 else None)
        tok_xn[tau] = tok("vE")
        mix_w = [st[tau - 1]["projr_tok"]] if tau - 1 in st else []
        for ct in range(NCT):
            cur = xn[:, ct * 257 + 1:ct * 257 + 257]
            prv = xn[:, ct * 257:ct * 257 + 256]
            P("vector", lambda e, cur=cur, prv=prv: TT(e, dd[:], cur, prv, ALU.subtract),
              waits=mix_w if ct == 0 else ())
            for dst, col in ((xkb, 0), (xvb, 8), (xrb, 16)):
                lastop = (ct == 7 and col == 16)
                P("vector", lambda e, dst=dst, col=col, ct=ct, prv=prv:
                  e.scalar_tensor_tensor(
                      out=dst[:, ct * 256:(ct + 1) * 256], in0=dd[:],
                      scalar=vecT[:, col + ct:col + ct + 1], in1=prv,
                      op0=ALU.mult, op1=ALU.add),
                  inc="vE" if lastop else None)
        s["mix_tok"] = tok("vE")

    def a_proj(tau):
        s = st[tau]
        rw_prev = st[tau - 1]["rw_toks"] if tau - 1 in st else {}
        ev_toks = {}
        for pi, nm in enumerate(("k", "v", "r")):
            Wb = (Wbk, Wbv, Wbr)[pi]
            srcb = (xkb, xvb, xrb)[pi]
            wkey = ("WkT", "WvT", "WrT")[pi]
            dstb = (kf, vf, None)[pi]
            for dt in range(NCT):
                sli = pp_rr[0] % 4
                sl = pp[sli]
                fw = pp_free[sli]
                pp_rr[0] += 1
                for ctk in range(NCT):
                    P("tensor", lambda e, sl=sl, Wb=Wb, ctk=ctk, dt=dt, srcb=srcb: e.matmul(
                        PS[:, sl:sl + 256],
                        Wb[:, ctk * 1024 + dt * 128:ctk * 1024 + dt * 128 + 128],
                        srcb[:, ctk * 256:(ctk + 1) * 256],
                        start=(ctk == 0), stop=(ctk == 7)),
                      waits=[s["mix_tok"], fw, wbt[wkey]] if ctk == 0 else (),
                      inc="tE" if ctk == 7 else None)
                tk = tok("tE")
                guard = [rw_prev[dt]] if dt in rw_prev else []
                if nm == "r":
                    P("scalar", lambda e, sl=sl, dt=dt: e.activation(
                        out=erf[:, dt * 256:(dt + 1) * 256], in_=PS[:, sl:sl + 256],
                        func=AF.Exp, scale=-1.0), waits=[tk] + guard, inc="sE")
                else:
                    P("scalar", lambda e, sl=sl, dstb=dstb, dt=dt: e.copy(
                        dstb[:, dt * 256:(dt + 1) * 256], PS[:, sl:sl + 256]),
                      waits=[tk] + guard, inc="sE")
                ev_toks[(nm, dt)] = tok("sE")
                pp_free[sli] = tok("sE")
        s["ev_toks"] = ev_toks
        s["projr_tok"] = tok("tE")
        if debug:
            for dt in range(NCT):
                P("sync", lambda e, dt=dt, c0=s["c0"]: e.dma_start(
                    out=dbg["k"][dt * 128:(dt + 1) * 128, c0:c0 + 256],
                    in_=kf[:, dt * 256:(dt + 1) * 256]),
                  waits=[ev_toks[("k", dt)]], inc="dS", n=16)
            s["kdump_tok"] = tok("dS")

    def a_scanprep(tau):
        s = st[tau]
        ev_toks = s["ev_toks"]
        for dt in range(NCT):
            P("vector", lambda e, dt=dt: e.reduce_max(
                out=kmax8[:, dt:dt + 1], in_=kf[:, dt * 256:(dt + 1) * 256], axis=AX),
              waits=[ev_toks[("k", dt)], dco_tok])
        vsp(2)
        fprev = [st[tau - 1]["F7"]] if tau - 1 in st and "F7" in st[tau - 1] else []
        P("vector", lambda e: TT(e, kmax8[:], kmax8[:], M8[:], ALU.max), waits=fprev)
        vsp(2)
        P("vector", lambda e: TT(e, dMs[:], M8[:], kmax8[:], ALU.subtract))
        P("vector", lambda e: e.tensor_copy(M8[:], kmax8[:]))
        P("vector", lambda e: TS(e, negM8[:], kmax8[:], -1.0, ALU.mult))
        vsp(2, inc="vE")
        mb_tok = tok("vE")
        P("scalar", lambda e: e.activation(out=sc8[:], in_=dMs[:], func=AF.Exp),
          waits=[mb_tok], inc="sE")
        sc_tok = tok("sE")
        P("vector", lambda e: TT(e, Pc8[:], Pc8[:], sc8[:], ALU.mult), waits=[sc_tok])
        P("vector", lambda e: TT(e, Qc8[:], Qc8[:], sc8[:], ALU.mult))
        vsp(2, inc="vE")
        s["resc_tok"] = tok("vE")
        if debug:
            P("sync", lambda e, tau=tau: e.dma_start(
                out=dbg["carry"][:, tau * 32:tau * 32 + 8], in_=Pc8[:]),
              waits=[s["resc_tok"]], inc="dS", n=16)
            P("sync", lambda e, tau=tau: e.dma_start(
                out=dbg["carry"][:, tau * 32 + 8:tau * 32 + 16], in_=Qc8[:]),
              inc="dS", n=16)
            carry_dump = tok("dS")
            P("vector", lambda e: e.nop(), waits=[carry_dump])

    def a_wkv(tau):
        s = st[tau]
        ev_toks = s["ev_toks"]
        F_toks = {}
        for dt in range(NCT):
            fwaits = [s["resc_tok"]] if dt == 0 else []
            if debug and dt == 0 and "kdump_tok" in s:
                fwaits.append(s["kdump_tok"])
            P("scalar", lambda e, dt=dt: e.activation(
                out=kf[:, dt * 256:(dt + 1) * 256], in_=kf[:, dt * 256:(dt + 1) * 256],
                func=AF.Exp, bias=negM8[:, dt:dt + 1]), waits=fwaits, inc="sE")
            F_toks[dt] = tok("sE")
        s["F7"] = F_toks[7]
        wo_prev = [st[tau - 1]["wo_last"]] if tau - 1 in st and "wo_last" in st[tau - 1] else []
        rw_toks = {}
        for dt in range(NCT):
            kfs = kf[:, dt * 256:(dt + 1) * 256]
            vfs = vf[:, dt * 256:(dt + 1) * 256]
            dcbv = dcoefT[:, dt:dt + 1].broadcast_to((128, 256))
            w0 = [F_toks[dt]] + (wo_prev if dt == 0 else [])
            P("vector", lambda e, kfs=kfs, vfs=vfs: TT(e, Escr[:], kfs, vfs, ALU.mult),
              waits=w0)
            P("vector", lambda e, dt=dt, dcbv=dcbv: e.tensor_tensor_scan(
                out=Pbuf[:], data0=dcbv, data1=Escr[:],
                initial=Pc8[:, dt:dt + 1], op0=ALU.mult, op1=ALU.add))
            P("vector", lambda e, dt=dt, kfs=kfs, dcbv=dcbv: e.tensor_tensor_scan(
                out=Qbuf[:], data0=dcbv, data1=kfs,
                initial=Qc8[:, dt:dt + 1], op0=ALU.mult, op1=ALU.add))
            P("vector", lambda e, dt=dt, kfs=kfs, vfs=vfs: e.scalar_tensor_tensor(
                out=Gscr[:], in0=kfs, scalar=euT[:, dt:dt + 1], in1=vfs,
                op0=ALU.mult, op1=ALU.mult))
            P("vector", lambda e: TT(e, Nscr[:, 1:256], Gscr[:, 1:256],
                                     Pbuf[:, 0:255], ALU.add))
            P("vector", lambda e, dt=dt: TT(e, Nscr[:, 0:1], Gscr[:, 0:1],
                                            Pc8[:, dt:dt + 1], ALU.add))
            P("vector", lambda e, dt=dt, kfs=kfs: e.scalar_tensor_tensor(
                out=Dscr[:, 1:256], in0=kfs[:, 1:256], scalar=euT[:, dt:dt + 1],
                in1=Qbuf[:, 0:255], op0=ALU.mult, op1=ALU.add))
            P("vector", lambda e, dt=dt, kfs=kfs: e.scalar_tensor_tensor(
                out=Dscr[:, 0:1], in0=kfs[:, 0:1], scalar=euT[:, dt:dt + 1],
                in1=Qc8[:, dt:dt + 1], op0=ALU.mult, op1=ALU.add))
            P("vector", lambda e, dt=dt: e.tensor_copy(Pc8[:, dt:dt + 1], Pbuf[:, 255:256]),
              fast=True)
            P("vector", lambda e, dt=dt: e.tensor_copy(Qc8[:, dt:dt + 1], Qbuf[:, 255:256]),
              fast=True)
            P("vector", lambda e, dt=dt: e.scalar_tensor_tensor(
                out=Gscr[:], in0=erf[:, dt * 256:(dt + 1) * 256], scalar=1.0,
                in1=Dscr[:], op0=ALU.add, op1=ALU.mult),
              waits=[ev_toks[("r", dt)]])
            P("vector", lambda e: e.reciprocal(out=Escr[:], in_=Gscr[:]))
            vsp(1)
            P("vector", lambda e, dt=dt: TT(
                e, rwkvb[:, dt * 256:(dt + 1) * 256], Nscr[:], Escr[:], ALU.mult),
              inc="vE")
            rw_toks[dt] = tok("vE")
            if debug:
                P("vector", lambda e: TT(e, Nscr[:], Nscr[:], Escr[:], ALU.mult), inc="vE")
                dtk = tok("vE")
                P("sync", lambda e, dt=dt, c0=s["c0"]: e.dma_start(
                    out=dbg["wkv"][dt * 128:(dt + 1) * 128, c0:c0 + 256], in_=Nscr[:]),
                  waits=[dtk], inc="dS", n=16)
        s["rw_toks"] = rw_toks

    def a_tail(tau):
        nonlocal stA_free, stB_free
        s = st[tau]
        xs, c0 = s["xs"], s["c0"]
        rw_toks = s["rw_toks"]
        ao_toks = {}
        sq_toks = {}
        aof_guard = []
        if tau - 1 in tok_x1dma:
            aof_guard = [("dO", tok_x1dma[tau - 1]), tok_x1ops[tau - 1]]
        for dt in range(NCT):
            sli = pp_rr[0] % 4
            sl = pp[sli]
            fw = pp_free[sli]
            pp_rr[0] += 1
            for ctk in range(NCT):
                P("tensor", lambda e, sl=sl, ctk=ctk, dt=dt: e.matmul(
                    PS[:, sl:sl + 256],
                    Wbo[:, ctk * 1024 + dt * 128:ctk * 1024 + dt * 128 + 128],
                    rwkvb[:, ctk * 256:(ctk + 1) * 256],
                    start=(ctk == 0), stop=(ctk == 7)),
                  waits=[rw_toks[7], fw, wbt["WoT"]] if ctk == 0 else (),
                  inc="tE" if ctk == 7 else None)
            tk = tok("tE")
            P("scalar", lambda e, sl=sl, dt=dt: e.copy(
                aof[:, dt * 256:(dt + 1) * 256], PS[:, sl:sl + 256]),
              waits=[tk] + (aof_guard if dt == 0 else []), inc="sE")
            ao_toks[dt] = tok("sE")
            pp_free[sli] = tok("sE")
            if dt == NCT - 1:
                s["wo_last"] = tk
            P("scalar", lambda e, dt=dt: e.activation(
                out=sqf[:, (dt % 2) * 256:(dt % 2) * 256 + 256],
                in_=aof[:, dt * 256:(dt + 1) * 256], func=AF.Square),
              waits=[sq_free[dt % 2]], inc="sE")
            sq_toks[dt] = tok("sE")
            P("tensor", lambda e, dt=dt: e.matmul(
                PS[0:1, 2048:2304], ones_s[:], aof[:, dt * 256:(dt + 1) * 256],
                start=(dt == 0), stop=(dt == 7)),
              waits=[ao_toks[dt], stA_free] if dt == 0 else [ao_toks[dt]])
            P("tensor", lambda e, dt=dt: e.matmul(
                PS[0:1, 2560:2816], ones_s[:],
                sqf[:, (dt % 2) * 256:(dt % 2) * 256 + 256],
                start=(dt == 0), stop=(dt == 7)),
              waits=[sq_toks[dt]], inc="tE")
            sq_free[dt % 2] = tok("tE")
        statA_tok = tok("tE")

        mrs2_tok, var2_tok = ln_rows(
            PS[0:1, 2048:2304], PS[0:1, 2560:2816], rstd2R[0:1, 0:TCA],
            mrs2R[0:1, 0:TCA], mean2R[0:1, 0:TCA], scr2[0:1, 0:TCA],
            [statA_tok])
        stA_free = var2_tok

        for i, srcr in ((0, rstd2R), (1, mrs2R)):
            P("tensor", lambda e, i=i, srcr=srcr: e.matmul(
                PS[:, bc[i]:bc[i] + 256], onesK[:], srcr[0:1, 0:256],
                start=True, stop=True),
              waits=[bc_free[i], mrs2_tok], inc="tE" if i == 1 else None)
        bcast2_tok = tok("tE")

        x1_toks = {}
        sqB_toks = {}
        for dt in range(NCT):
            P("vector", lambda e, dt=dt: TT(
                e, dd[:], aof[:, dt * 256:(dt + 1) * 256],
                PS[:, bc[0]:bc[0] + 256], ALU.mult),
              waits=[bcast2_tok] if dt == 0 else ())
            P("vector", lambda e: TT(e, dd[:], dd[:], PS[:, bc[1]:bc[1] + 256],
                                     ALU.subtract))
            P("vector", lambda e, dt=dt, xs=xs: TT(
                e, aof[:, dt * 256:(dt + 1) * 256], dd[:],
                xs[:, dt * 257 + 1:dt * 257 + 257], ALU.add), inc="vE")
            x1_toks[dt] = tok("vE")
            P("scalar", lambda e, dt=dt: e.activation(
                out=sqf[:, (dt % 2) * 256:(dt % 2) * 256 + 256],
                in_=aof[:, dt * 256:(dt + 1) * 256], func=AF.Square),
              waits=[x1_toks[dt], sq_free[dt % 2]], inc="sE")
            sqB_toks[dt] = tok("sE")
            P("sync", lambda e, dt=dt, c0=c0: e.dma_start(
                out=dx1[dt * 128:(dt + 1) * 128, c0:c0 + 256],
                in_=aof[:, dt * 256:(dt + 1) * 256]),
              waits=[x1_toks[dt]], inc="dO", n=16)
            if debug:
                P("sync", lambda e, dt=dt, c0=c0: e.dma_start(
                    out=dbg["x1"][dt * 128:(dt + 1) * 128, c0:c0 + 256],
                    in_=aof[:, dt * 256:(dt + 1) * 256]), inc="dO", n=16)
            P("tensor", lambda e, dt=dt: e.matmul(
                PS[0:1, 2304:2560], ones_s[:], aof[:, dt * 256:(dt + 1) * 256],
                start=(dt == 0), stop=(dt == 7)),
              waits=[x1_toks[dt], stB_free] if dt == 0 else [x1_toks[dt]])
            P("tensor", lambda e, dt=dt: e.matmul(
                PS[0:1, 2816:3072], ones_s[:],
                sqf[:, (dt % 2) * 256:(dt % 2) * 256 + 256],
                start=(dt == 0), stop=(dt == 7)),
              waits=[sqB_toks[dt]], inc="tE")
            sq_free[dt % 2] = tok("tE")
        tok_x1dma[tau] = pb.cnt["dO"]
        tok_statB[tau] = tok("tE")
        P("vector", lambda e, c0=c0: e.tensor_copy(
            rowsum[0:1, c0:c0 + 256], PS[0:1, 2304:2560]),
          waits=[tok_statB[tau]])
        P("vector", lambda e, c0=c0: e.tensor_copy(
            rowsq[0:1, c0:c0 + 256], PS[0:1, 2816:3072]), inc="vE")
        stB_free = tok("vE")
        tok_x1ops[tau] = tok("vE")
        bc_free[0] = tok_x1ops[tau]
        bc_free[1] = tok_x1ops[tau]

    for tau in range(NCH + 1):
        if tau < NCH:
            a_loads(tau)
            a_xnmix(tau)
        if tau >= 1:
            a_wkv(tau - 1)
        if tau < NCH:
            a_proj(tau)
            a_scanprep(tau)
        if tau >= 1:
            a_tail(tau - 1)

    attn_done = [tok("tE"), tok("vE"), tok("sE"), tok("pE"),
                 ("dO", pb.cnt["dO"]), tok("dS")]

    ln2_tok, _ = ln_rows(rowsum[0:1, 0:T], rowsq[0:1, 0:T], rstdR[0:1, 0:T],
                         mrsR[0:1, 0:T], rowsum[0:1, 0:T], rowsq[0:1, 0:T],
                         [tok_x1ops[NCH - 1]])
    P("sync", lambda e: e.dma_start(out=drB.ap()[0:1, :], in_=rstdR[0:1, 0:T]),
      waits=[ln2_tok, ("dX", pb.cnt["dX"])], inc="dR", n=16)
    P("sync", lambda e: e.dma_start(out=drB.ap()[1:2, :], in_=mrsR[0:1, 0:T]),
      inc="dR", n=16)
    rows2_tok = tok("dR")

    for i in range(4):
        hfree[i] = attn_done[1]

    for ct in range(NCT):
        i, buf, off = load1024(dW["rw1"][ct * 128:(ct + 1) * 128, :])
        P("scalar", lambda e, buf=buf, off=off, ct=ct: e.copy(
            rw1b[:, ct * 1024:(ct + 1) * 1024], buf[:, off:off + 1024]),
          waits=[hfill[i]] + (attn_done if ct == 0 else []), inc="sE", fast=(ct != 7))
        free_half(i, tok("sE"))
    rw1_tok = tok("sE")
    for ct in range(NCT):
        P("sync", lambda e, ct=ct: e.dma_start(
            out=rw2stg[:, ct * 8:(ct + 1) * 8], in_=drw2[ct * 128:(ct + 1) * 128, :]),
          waits=attn_done if ct == 0 else (), inc="dW", n=16)
    rw2d = tok("dW")
    P("scalar", lambda e: e.copy(rw2b[:], rw2stg[:]),
      waits=[rw2d], inc="sE")
    rw2_tok = tok("sE")


    bc_free2 = [attn_done[1], attn_done[1]]
    bc_idx = [0]

    def bc_load(c0, width, rows_tok, gates=()):
        p = bc_idx[0] % 2
        bc_idx[0] += 1
        sname = "dV%d" % p
        P("sync", lambda e, c0=c0, width=width, p=p: e.dma_start(
            out=bcF[2 * p][:, 0:width],
            in_=drB.ap()[0:1, c0:c0 + width].broadcast_to((128, width))),
          waits=[rows_tok, bc_free2[p]] + list(gates), inc=sname, n=16)
        P("sync", lambda e, c0=c0, width=width, p=p: e.dma_start(
            out=bcF[2 * p + 1][:, 0:width],
            in_=drB.ap()[1:2, c0:c0 + width].broadcast_to((128, width))),
          inc=sname, n=16)
        return p, tok(sname)

    tbf_s_tok = {}
    for s in range(4):
        for ct in range(NCT):
            fx = [attn_done[4]] if (s == 0 and ct == 0) else []
            i, buf, off = load512(
                dx1[ct * 128:(ct + 1) * 128, s * 512:(s + 1) * 512],
                free_extra=fx, sem="dS")
            ld = hfill[i]
            p, bct = bc_load(s * 512, 512, rows2_tok)
            P("vector", lambda e, buf=buf, off=off, p=p: TT(
                e, scrX[:], buf[:, off:off + 512], bcF[2 * p][:, 0:512],
                ALU.mult), waits=[ld, bct])
            P("vector", lambda e, ct=ct, s=s, p=p: TT(
                e, t_bf[:, ct * 2048 + s * 512:ct * 2048 + (s + 1) * 512],
                scrX[:], bcF[2 * p + 1][:, 0:512], ALU.subtract), inc="vE")
            bc_free2[p] = tok("vE")
            free_half(i, tok("vE"))
        tbf_s_tok[s] = tok("vE")
    tbf_tok = tok("vE")
    if debug:
        for ct in range(NCT):
            P("sync", lambda e, ct=ct: e.dma_start(
                out=dbg["t"][ct * 128:(ct + 1) * 128, :],
                in_=t_bf[:, ct * 2048:(ct + 1) * 2048]),
              waits=[tbf_tok], inc="dS", n=16)

    rh_sl = [1024, 1536]
    rh_free = [attn_done[1], attn_done[1]]
    lg_free = attn_done[1]
    lg_toks = {}
    for tr in range(4):
        rhb_toks = {}
        for ht in range(NCT):
            sl = rh_sl[ht % 2]
            fw = rh_free[ht % 2]
            for ctk in range(NCT):
                P("tensor", lambda e, sl=sl, ht=ht, ctk=ctk, tr=tr: e.matmul(
                    PS[:, sl:sl + 512],
                    rw1b[:, ctk * 1024 + ht * 128:ctk * 1024 + ht * 128 + 128],
                    t_bf[:, ctk * 2048 + tr * 512:ctk * 2048 + (tr + 1) * 512],
                    start=(ctk == 0), stop=(ctk == 7)),
                  waits=[tbf_s_tok[tr], fw, rw1_tok] if ctk == 0 else (),
                  inc="tE" if ctk == 7 else None)
            mmt = tok("tE")
            P("scalar", lambda e, sl=sl, ht=ht: e.activation(
                out=rhb[:, ht * 512:(ht + 1) * 512], in_=PS[:, sl:sl + 512],
                func=AF.Silu, bias=vecT[:, 24 + ht:25 + ht]),
              waits=[mmt], inc="sE")
            rhb_toks[ht] = tok("sE")
            rh_free[ht % 2] = tok("sE")
            if debug and tr == 0:
                P("sync", lambda e, ht=ht: e.dma_start(
                    out=dbg["rh"][:, ht * 512:(ht + 1) * 512],
                    in_=rhb[:, ht * 512:(ht + 1) * 512]),
                  waits=[rhb_toks[ht]], inc="dS", n=16)
        for hc in range(NCT):
            w0 = [rhb_toks[hc]]
            if hc == 0:
                w0 += [rw2_tok, lg_free]
            P("tensor", lambda e, hc=hc: e.matmul(
                PS[0:8, 2048:2560], rw2b[:, hc * 8:(hc + 1) * 8],
                rhb[:, hc * 512:(hc + 1) * 512], start=(hc == 0), stop=(hc == 7)),
              waits=w0, inc="tE" if hc == 7 else None)
        lgmm = tok("tE")
        P("vector", lambda e, tr=tr: e.tensor_scalar(
            out=logits_sb[0:8, tr * 512:(tr + 1) * 512], in0=PS[0:8, 2048:2560],
            scalar1=rb2c[0:8, 0:1], scalar2=None, op0=ALU.add),
          waits=[lgmm], inc="vE")
        lg_toks[tr] = tok("vE")
        lg_free = tok("vE")
    logits_tok = tok("vE")
    if debug:
        P("sync", lambda e: e.dma_start(out=dbg["logits"], in_=logits_sb[:]),
          waits=[logits_tok], inc="dS", n=16)

    tp_sl = [2560, 2568]
    tp_free = [attn_done[1], attn_done[1]]
    wt_sl = 2576
    wt_free = attn_done[1]
    for j in range(16):
        P("tensor", lambda e, j=j: e.transpose(
            PS[:, tp_sl[j % 2]:tp_sl[j % 2] + 8],
            logits_sb[0:8, j * 128:(j + 1) * 128], eyeS[0:8, 0:8]),
          waits=[logits_tok, tp_free[j % 2]], inc="tE")
        tpt = tok("tE")
        P("vector", lambda e, j=j: e.tensor_copy(
            lgT[:], PS[:, tp_sl[j % 2]:tp_sl[j % 2] + 8]), waits=[tpt])
        vsp(2)
        P("vector", lambda e: e.reduce_max(out=kmax8[:, 0:1], in_=lgT[:], axis=AX))
        vsp(2)
        P("vector", lambda e: TS(e, wTj[:], lgT[:], kmax8[:, 0:1], ALU.is_ge))
        vsp(2)
        P("vector", lambda e: e.scalar_tensor_tensor(
            out=sc8[:], in0=wTj[:], scalar=-1e30, in1=lgT[:],
            op0=ALU.mult, op1=ALU.add))
        vsp(2)
        P("vector", lambda e: e.reduce_max(out=kmax8[:, 1:2], in_=sc8[:], axis=AX))
        vsp(2)
        P("vector", lambda e: TS(e, wTj[:], lgT[:], kmax8[:, 1:2], ALU.is_ge))
        P("vector", lambda e: TS(e, kmax8[:, 2:3], kmax8[:, 0:1], -1.0, ALU.mult))
        vsp(2, inc="vE")
        gv = tok("vE")
        P("scalar", lambda e: e.activation(out=sc8[:], in_=lgT[:], func=AF.Exp,
                                           bias=kmax8[:, 2:3]), waits=[gv], inc="sE")
        gs = tok("sE")
        P("vector", lambda e: TT(e, wTj[:], wTj[:], sc8[:], ALU.mult), waits=[gs])
        vsp(2)
        P("vector", lambda e: e.tensor_reduce(out=kmax8[:, 3:4], in_=wTj[:],
                                              axis=AX, op=ALU.add))
        vsp(2)
        P("vector", lambda e: e.reciprocal(out=kmax8[:, 4:5], in_=kmax8[:, 3:4]))
        vsp(2)
        P("vector", lambda e: TS(e, wTj[:], wTj[:], kmax8[:, 4:5], ALU.mult))
        vsp(2, inc="vE")
        gw = tok("vE")
        tp_free[j % 2] = gw
        P("tensor", lambda e, j=j: e.transpose(
            PS[0:8, wt_sl:wt_sl + 128], wTj[:], eyeS[:]),
          waits=[gw, wt_free], inc="tE")
        wtt = tok("tE")
        P("vector", lambda e, j=j: e.tensor_copy(
            w_fm[0:8, j * 128:(j + 1) * 128], PS[0:8, wt_sl:wt_sl + 128]),
          waits=[wtt], inc="vE")
        wt_free = tok("vE")
    gates_tok = tok("vE")
    P("scalar", lambda e: e.copy(wfmb[:], w_fm[:]),
      waits=[gates_tok, tok("tE"), logits_tok], inc="sE")
    wfmb_tok = tok("sE")
    P("sync", lambda e: e.dma_start(out=dwfm.ap(), in_=wfmb[:]),
      waits=[wfmb_tok], inc="dF", n=16)
    wfmd_tok = tok("dF")
    P("sync", lambda e: e.dma_start(out=eb2sb[:], in_=deb2), inc="dB", n=16)
    eb2_tok = tok("dB")
    if debug:
        P("sync", lambda e: e.dma_start(out=dbg["w"], in_=w_fm[:]),
          waits=[gates_tok], inc="dS", n=16)

    def plan_e1b_load(eidx, guard_toks):
        for ct in range(NCT):
            i, buf = load2048(dew1[eidx * C + ct * 128:eidx * C + (ct + 1) * 128, :])
            P("scalar", lambda e, buf=buf, ct=ct: e.copy(
                e1b[:, ct * 2048:(ct + 1) * 2048], buf[:, 0:2048]),
              waits=[hfill[i]] + (guard_toks if ct == 0 else []), inc="sE",
              fast=(ct != 7))
            free_pair(i, tok("sE"))
        return tok("sE")

    e1b_tok = plan_e1b_load(0, attn_done)
    ehw_guard = [gates_tok, logits_tok, wfmd_tok]
    spill_tok = None
    yev_last = {}
    wbuf_tok = None
    e2slot_free = [gates_tok] * NHS

    for tf in range(2):
        tc0 = tf * TCF
        for ct in range(NCT):
            fx = []
            if tf == 1 and ct == 0:
                fx = [spill_tok]
            i, buf = load2048(dx1[ct * 128:(ct + 1) * 128, :], free_extra=fx, sem="dS")
            ld = hfill[i]
            for s2 in range(2):
                sidx = tf * 2 + s2
                gates = [spill_tok] if (tf == 1 and ct == 0 and s2 == 0) else []
                p, bct = bc_load(sidx * 512, 512, rows2_tok, gates)
                wv = [ld, bct] + gates
                P("vector", lambda e, buf=buf, s2=s2, tf=tf, p=p: TT(
                    e, scrX[:], buf[:, tf * 1024 + s2 * 512:tf * 1024 + (s2 + 1) * 512],
                    bcF[2 * p][:, 0:512], ALU.mult), waits=wv)
                P("vector", lambda e, ct=ct, s2=s2, p=p: TT(
                    e, yacc[:, ct * 1024 + s2 * 512:ct * 1024 + (s2 + 1) * 512],
                    scrX[:], bcF[2 * p + 1][:, 0:512], ALU.subtract), inc="vE")
                bc_free2[p] = tok("vE")
            free_pair(i, tok("vE"))
        yinit_tok = tok("vE")

        for eidx in range(NE):
            wg = [wfmd_tok, yinit_tok, tok("pE")]
            if wbuf_tok is not None:
                wg.append(wbuf_tok)
            P("sync", lambda e, eidx=eidx, tc0=tc0: e.dma_start(
                out=wbuf[:], in_=dwfm.ap()[eidx:eidx + 1, tc0:tc0 + TCF]
                .broadcast_to((128, TCF))),
              waits=wg, inc="dG", n=16)
            wbuf_tok = tok("dG")

            eh_sl = [0, 1024]
            ehw_toks = {}
            for hs in range(NHS):
                sl = eh_sl[hs % 2]
                if hs - 2 in ehw_toks:
                    slot_w = [ehw_toks[hs - 2][1]]
                else:
                    slot_w = [wbuf_tok]
                    for dsg in (2 * hs, 2 * hs + 1):
                        if (1, dsg) in yev_last:
                            slot_w.append(yev_last[(1, dsg)])
                        if (0, dsg) in yev_last:
                            slot_w.append(yev_last[(0, dsg)])
                    if not yev_last and eidx == 0 and tf == 0:
                        slot_w.append(tok("vE"))
                for nn in range(2):
                    for ctk in range(NCT):
                        w0 = []
                        if nn == 0 and ctk == 0:
                            w0 = slot_w + [e1b_tok, tbf_tok, wbuf_tok]
                            if hs == 0 and eidx == 0 and tf == 0:
                                w0 += ehw_guard
                        P("tensor", lambda e, sl=sl, nn=nn, ctk=ctk, hs=hs, tf=tf: e.matmul(
                            PS[:, sl + nn * 512:sl + (nn + 1) * 512],
                            e1b[:, ctk * 2048 + hs * 128:ctk * 2048 + hs * 128 + 128],
                            t_bf[:, ctk * 2048 + tf * 1024 + nn * 512:
                                 ctk * 2048 + tf * 1024 + (nn + 1) * 512],
                            start=(ctk == 0), stop=(ctk == 7)),
                          waits=w0, inc="tE" if (nn == 1 and ctk == 7) else None)
                mmt = tok("tE")
                P("scalar", lambda e, sl=sl, hs=hs, eidx=eidx: e.activation(
                    out=ehw[:, hs * 1024:(hs + 1) * 1024], in_=PS[:, sl:sl + 1024],
                    func=AF.Silu, bias=ebT[:, hs * 8 + eidx:hs * 8 + eidx + 1]),
                  waits=[mmt], inc="sE")
                silu_t = tok("sE")
                P("gpsimd", lambda e, hs=hs: e.tensor_mul(
                    ehw[:, hs * 1024:(hs + 1) * 1024],
                    ehw[:, hs * 1024:(hs + 1) * 1024], wbuf[:]),
                  waits=[silu_t], inc="pE", fast=True)
                ehw_toks[hs] = (tok("pE"), silu_t)
                if debug and tf == 0 and eidx == 1:
                    P("sync", lambda e, hs=hs: e.dma_start(
                        out=dbg["ehw"][hs * 128:(hs + 1) * 128, :],
                        in_=ehw[:, hs * 1024:(hs + 1) * 1024]),
                      waits=[ehw_toks[hs][0]], inc="dS", n=16)
            c1_tok = tok("tE")

            e2cast = {}
            for hs in range(NHS):
                i, buf, off = load1024(
                    dew2[eidx * HD + hs * 128:eidx * HD + (hs + 1) * 128, :])
                cast_w = [hfill[i], e2slot_free[hs]]
                P("scalar", lambda e, buf=buf, off=off, hs=hs: e.copy(
                    e2full[:, hs * 1024:(hs + 1) * 1024], buf[:, off:off + 1024]),
                  waits=cast_w, inc="sE")
                free_half(i, tok("sE"))
                e2cast[hs] = tok("sE")

            nxt = eidx + 1
            if nxt < NE:
                e1b_tok = plan_e1b_load(nxt, [c1_tok])
            elif tf == 0:
                e1b_tok = plan_e1b_load(0, [c1_tok])

            for nn in range(2):
                for hs in range(NHS):
                    mm_w = [e2cast[hs], ehw_toks[hs][0]]
                    if hs == 0:
                        mm_w += [tok("vE"), ehw_toks[NHS - 1][1], ehw_toks[NHS - 2][1],
                                 wbuf_tok]
                    for ds in range(8):
                        P("tensor", lambda e, ds=ds, hs=hs, nn=nn, eidx=eidx: e.matmul(
                            PS[:, ds * 512:(ds + 1) * 512],
                            e2full[:, hs * 1024 + ds * 128:hs * 1024 + (ds + 1) * 128],
                            ehw[:, hs * 1024 + nn * 512:hs * 1024 + (nn + 1) * 512],
                            start=(hs == 0),
                            stop=(hs == NHS - 1 and eidx != NE - 1)),
                          waits=mm_w if ds == 0 else (),
                          inc="tE" if ds == 7 else None)
                    if nn == 1:
                        e2slot_free[hs] = tok("tE")
                if eidx == NE - 1:
                    for ds in range(8):
                        P("tensor", lambda e, ds=ds, nn=nn, tc0=tc0: e.matmul(
                            PS[:, ds * 512:(ds + 1) * 512],
                            eb2sb[0:8, ds * 128:(ds + 1) * 128],
                            w_fm[0:8, tc0 + nn * 512:tc0 + (nn + 1) * 512],
                            start=False, stop=True),
                          waits=[eb2_tok, gates_tok] if ds == 0 else (),
                          inc="tE" if ds == 7 else None)
                c2_tok = tok("tE")
                for ds in range(8):
                    P("vector", lambda e, ds=ds, nn=nn: TT(
                        e, yacc[:, ds * 1024 + nn * 512:ds * 1024 + (nn + 1) * 512],
                        PS[:, ds * 512:(ds + 1) * 512],
                        yacc[:, ds * 1024 + nn * 512:ds * 1024 + (nn + 1) * 512],
                        ALU.add),
                      waits=[c2_tok] if ds == 0 else (), inc="vE")
                    yev_last[(nn, ds)] = tok("vE")
        if tf == 0:
            for ds in range(8):
                P("sync", lambda e, ds=ds: e.dma_start(
                    out=dyd[ds * 128:(ds + 1) * 128, :],
                    in_=yacc[:, ds * 1024:(ds + 1) * 1024]),
                  waits=[yev_last[(0, ds)], yev_last[(1, ds)]], inc="dO", n=16)
            spill_tok = ("dO", pb.cnt["dO"])

    yfin = dict(yev_last)
    if debug:
        for ds in range(8):
            P("sync", lambda e, ds=ds: e.dma_start(
                out=dbg["y"][ds * 128:(ds + 1) * 128, TCF:2 * TCF],
                in_=yacc[:, ds * 1024:(ds + 1) * 1024]),
              waits=[yfin[(0, ds)], yfin[(1, ds)], spill_tok], inc="dS", n=16)
            P("sync", lambda e, ds=ds: e.dma_start(
                out=dbg["y"][ds * 128:(ds + 1) * 128, 0:TCF],
                in_=dyd[ds * 128:(ds + 1) * 128, :]), inc="dS", n=16)
        ydump = tok("dS")
        P("vector", lambda e: e.nop(), waits=[ydump])

    d_mm_last = None
    for hf in range(2):
        for ct in range(NCT):
            if hf == 0:
                fx = [spill_tok] if ct == 0 else []
                i, buf, off = load1024(dyd[ct * 128:(ct + 1) * 128, :],
                                       free_extra=fx, sem="dS")
                srcf = lambda s2, buf=buf, off=off: buf[:, off + s2 * 512:off + (s2 + 1) * 512]
                ld = [hfill[i]]
            else:
                srcf = lambda s2, ct=ct: yacc[:, ct * 1024 + s2 * 512:ct * 1024 + (s2 + 1) * 512]
                ld = [yfin[(0, ct)], yfin[(1, ct)]]
            for s2 in range(2):
                pw = [tok("vE")] if d_mm_last is None else [d_mm_last]
                P("gpsimd", lambda e, srcf=srcf, s2=s2: e.tensor_mul(
                    scrX[:], srcf(s2), srcf(s2)), waits=ld + pw, inc="pE")
                sqt = tok("pE")
                bidx = hf * 2 + s2
                P("tensor", lambda e, srcf=srcf, s2=s2, bidx=bidx, ct=ct: e.matmul(
                    PS[0:1, bidx * 512:(bidx + 1) * 512], ones_s[:], srcf(s2),
                    start=(ct == 0), stop=(ct == 7)),
                  waits=ld + ([tok("vE"), e2slot_free[0], e2slot_free[1]]
                              if hf == 0 and ct == 0 and s2 == 0 else []))
                P("tensor", lambda e, bidx=bidx, ct=ct: e.matmul(
                    PS[0:1, 2048 + bidx * 512:2048 + (bidx + 1) * 512], ones_s[:],
                    scrX[:], start=(ct == 0), stop=(ct == 7)),
                  waits=[sqt], inc="tE")
                d_mm_last = tok("tE")
            if hf == 0:
                free_half(i, tok("tE"))
    d_mm = tok("tE")
    for s in range(4):
        P("vector", lambda e, s=s: e.tensor_copy(
            rowsum[0:1, s * 512:(s + 1) * 512], PS[0:1, s * 512:(s + 1) * 512]),
          waits=[d_mm] if s == 0 else ())
        P("vector", lambda e, s=s: e.tensor_copy(
            rowsq[0:1, s * 512:(s + 1) * 512],
            PS[0:1, 2048 + s * 512:2048 + (s + 1) * 512]),
          inc="vE" if s == 3 else None)
    d_ev = tok("vE")
    ln3_tok, _ = ln_rows(rowsum[0:1, 0:T], rowsq[0:1, 0:T], rstdR[0:1, 0:T],
                         mrsR[0:1, 0:T], rowsum[0:1, 0:T], rowsq[0:1, 0:T],
                         [d_ev])
    P("sync", lambda e: e.dma_start(out=drB.ap()[0:1, :], in_=rstdR[0:1, 0:T]),
      waits=[ln3_tok, ("dV0", pb.cnt["dV0"]), ("dV1", pb.cnt["dV1"])], inc="dR", n=16)
    P("sync", lambda e: e.dma_start(out=drB.ap()[1:2, :], in_=mrsR[0:1, 0:T]),
      inc="dR", n=16)
    rows3_tok = tok("dR")


    for ct in range(NCT):
        for hf in range(2):
            i1, b1, o1 = load1024(dx1[ct * 128:(ct + 1) * 128,
                                      hf * 1024:(hf + 1) * 1024], sem="dS")
            ldx = hfill[i1]
            if hf == 0:
                i2, b2, o2 = load1024(dyd[ct * 128:(ct + 1) * 128, :], sem="dS")
                ysrc = lambda s2, b2=b2, o2=o2: b2[:, o2 + s2 * 512:o2 + (s2 + 1) * 512]
                ldy = [hfill[i2]]
            else:
                ysrc = lambda s2, ct=ct: yacc[:, ct * 1024 + s2 * 512:ct * 1024 + (s2 + 1) * 512]
                ldy = []
            last_ot = None
            for s2 in range(2):
                sidx = hf * 2 + s2
                p, bct = bc_load(sidx * 512, 512, rows3_tok)
                ob = outbuf[:, ct * 2048 + hf * 1024 + s2 * 512:
                            ct * 2048 + hf * 1024 + (s2 + 1) * 512]
                P("vector", lambda e, ysrc=ysrc, s2=s2, p=p: TT(
                    e, scrX[:], ysrc(s2), bcF[2 * p][:, 0:512], ALU.mult),
                  waits=[bct, ldx] + ldy)
                P("vector", lambda e, p=p: TT(e, scrX[:], scrX[:],
                                              bcF[2 * p + 1][:, 0:512],
                                              ALU.subtract))
                P("vector", lambda e, ob=ob, s2=s2, b1=b1, o1=o1: TT(
                    e, ob, scrX[:], b1[:, o1 + s2 * 512:o1 + (s2 + 1) * 512],
                    ALU.add), inc="vE")
                ot = tok("vE")
                bc_free2[p] = ot
                last_ot = ot
                P("sync", lambda e, ob=ob, s2=s2, ct=ct, hf=hf: e.dma_start(
                    out=dout[ct * 128:(ct + 1) * 128,
                             hf * 1024 + s2 * 512:hf * 1024 + (s2 + 1) * 512],
                    in_=ob), waits=[ot], inc="dO", n=16)
            free_half(i1, last_ot)
            if hf == 0:
                free_half(i2, last_ot)

    final_dO = tok("dO")
    P("sync", lambda e: e.nop(), waits=[final_dO])

    pb.emit_all()
    return nc


def kernel(**inputs):
    x = np.asarray(inputs["L0_attn_x"], np.float32)
    Wk = np.asarray(inputs["L0_attn_Wk"], np.float32)
    Wv = np.asarray(inputs["L0_attn_Wv"], np.float32)
    Wr = np.asarray(inputs["L0_attn_Wr"], np.float32)
    Wo = np.asarray(inputs["L0_attn_Wo"], np.float32)
    vec6 = np.concatenate([
        np.asarray(inputs["L0_attn_mix_k"], np.float32).reshape(8, 128),
        np.asarray(inputs["L0_attn_mix_v"], np.float32).reshape(8, 128),
        np.asarray(inputs["L0_attn_mix_r"], np.float32).reshape(8, 128),
        np.asarray(inputs["L0_ffn_rb1"], np.float32).reshape(8, 128),
        np.asarray(inputs["L0_attn_w"], np.float32).reshape(8, 128),
        np.asarray(inputs["L0_attn_u"], np.float32).reshape(8, 128),
    ], axis=0)
    common = dict(
        WkT=np.ascontiguousarray(Wk.T),
        WvT=np.ascontiguousarray(Wv.T),
        WrT=np.ascontiguousarray(Wr.T),
        WoT=np.ascontiguousarray(Wo.T),
        rw1=np.ascontiguousarray(np.asarray(inputs["L0_ffn_rw1"], np.float32)),
        rw2=np.ascontiguousarray(np.asarray(inputs["L0_ffn_rw2"], np.float32)),
        vec6=np.ascontiguousarray(vec6),
        eb1=np.ascontiguousarray(np.asarray(inputs["L0_ffn_eb1"], np.float32)),
        eb2=np.ascontiguousarray(np.asarray(inputs["L0_ffn_eb2"], np.float32)),
        rb2=np.ascontiguousarray(np.asarray(inputs["L0_ffn_rb2"], np.float32).reshape(8, 1)),
        ew1=np.ascontiguousarray(np.asarray(inputs["L0_ffn_ew1"], np.float32).reshape(NE * C, HD)),
        ew2=np.ascontiguousarray(np.asarray(inputs["L0_ffn_ew2"], np.float32).reshape(NE * HD, C)),
        eye=np.eye(128, dtype=np.float32),
    )
    nc = build_program(debug=False)
    in_maps = []
    for b in range(8):
        m = dict(common)
        m["x"] = np.ascontiguousarray(x[b].T)
        in_maps.append(m)
    res = run_bass_kernel_spmd(nc, in_maps, core_ids=list(range(8)))
    out = np.stack([np.asarray(r["out"], np.float32).T for r in res.results], axis=0)
    return np.ascontiguousarray(out)
